# version 16
# baseline (speedup 1.0000x reference)
import numpy as np
import ml_dtypes

import concourse.bass as bass
import concourse.mybir as mybir

F32 = mybir.dt.float32
BF16 = mybir.dt.bfloat16
ALU = mybir.AluOpType
AF = mybir.ActivationFunctionType
AX = mybir.AxisListType
bfnp = ml_dtypes.bfloat16

MAXREL = 4
NJ = 9
LPAD, RPAD = 64, 4
WWIN = LPAD + 128 + RPAD

ENGINES = ("sync", "gpsimd", "tensor", "vector", "scalar")


def SCAL_VALS(HID, FF=4096):
    import math
    ln127 = math.log(127.0)
    return [0.0, 0.5, 1.0, -1.0, -0.125, 1e-5, 1.0 / HID, 1e-6,
            ln127, -ln127, 0.125, -0.5, 1.0 / FF, 1.0 / (HID * FF), 8388608.0, 536870912.0,
            805306368.0, 1.5, 12582912.0, 0.25, 127.0, 1.0 / 127.0]


def make_coefs(a2x_mode=False):
    coef = np.zeros((NJ, 128, WWIN), np.float32)
    p = np.arange(128)[:, None]
    kw = np.arange(WWIN)[None, :] - LPAD
    jj = np.clip(kw - p, -MAXREL, MAXREL) + MAXREL
    main = (kw >= 0) & (kw < 128)
    left = kw < 0
    right = kw >= 128
    for j in range(NJ):
        coef[j][main & (jj == j)] += 1.0
        if 1 <= j <= 3:
            coef[j][left & (jj == j)] += 1.0
            coef[0][left & (jj == j)] -= 1.0
        if 5 <= j <= 7:
            coef[j][right & (jj == j)] += 1.0
            coef[8][right & (jj == j)] -= 1.0
    if a2x_mode:
        coef[8] -= np.broadcast_to(main, (128, WWIN)).astype(np.float32)
    return coef


class Plan:

    def __init__(self):
        self.steps = {e: [] for e in ENGINES}
        self.cnt = {}
        self.is_dma = {}
        self.last_write = {}
        self.readers = {}
        self.waited = {e: {} for e in ENGINES}
        self.eng_sem = {e: f"S_{e}" for e in ENGINES}
        self.dma_gen = {}
        self.sem_prev = {}
        for e in ENGINES:
            self.cnt[self.eng_sem[e]] = 0
            self.is_dma[self.eng_sem[e]] = False

    def dma_sem(self, name):
        if name not in self.cnt:
            self.cnt[name] = 0
            self.is_dma[name] = True
        return name

    def _wait(self, eng, sem, v):
        if v <= 0:
            return
        if self.waited[eng].get(sem, 0) >= v:
            return
        self.waited[eng][sem] = v
        self.steps[eng].append(("wait", sem, v))

    def emit(self, eng, fn, reads=(), writes=(), dma=None):
        for b in reads:
            if b in self.last_write:
                self._wait(eng, *self.last_write[b])
        for b in writes:
            if b in self.last_write:
                self._wait(eng, *self.last_write[b])
            for rs in self.readers.get(b, ()):
                self._wait(eng, *rs)
        if dma is not None:
            gen = self.dma_gen.setdefault(dma, 0)
            self.dma_gen[dma] = gen + 1
            sem = self.dma_sem(f"{dma}.{gen % 3}")
            if self.cnt[sem] > 0:
                self._wait(eng, sem, self.cnt[sem])
            self.cnt[sem] += 16
        else:
            sem = self.eng_sem[eng]
            self.cnt[sem] += 1
        v = self.cnt[sem]
        self.steps[eng].append(("op", fn, sem, v))
        for b in reads:
            self.readers.setdefault(b, []).append((sem, v))
        for b in writes:
            self.last_write[b] = (sem, v)
            self.readers[b] = []
        return (sem, v)

    def barrier_read(self, eng, bufs):
        for b in bufs:
            if b in self.last_write:
                self._wait(eng, *self.last_write[b])

    def drain_all_dma(self, eng):
        for sem, n in self.cnt.items():
            if self.is_dma[sem] and n > 0:
                self._wait(eng, sem, n)

    def finalize(self, keep_all=False):
        if keep_all:
            out = {}
            for e in ENGINES:
                lst = []
                for st in self.steps[e]:
                    if st[0] == "wait":
                        lst.append(st)
                    else:
                        lst.append(("op", st[1], st[2], 0))
                out[e] = lst
            return out
        needed = {}
        for e in ENGINES:
            for st in self.steps[e]:
                if st[0] == "wait":
                    needed.setdefault(st[1], set()).add(st[2])
        remap = {}
        for sem, vals in needed.items():
            sv = sorted(vals)
            remap[sem] = {v: i + 1 for i, v in enumerate(sv)}
        out = {}
        for e in ENGINES:
            lst = []
            for st in self.steps[e]:
                if st[0] == "wait":
                    sem, v = st[1], st[2]
                    if self.is_dma[sem]:
                        lst.append(("wait", sem, v))
                    else:
                        lst.append(("wait", sem, remap[sem][v]))
                else:
                    _, fn, sem, v = st
                    keep = self.is_dma[sem] or (sem in remap and v in remap[sem])
                    lst.append(("op", fn, sem if keep else None, 0))
            out[e] = lst
        return out


def build_consts(cfg):
    c = {}
    c["ident"] = np.eye(128, dtype=np.float32).astype(bfnp)
    coef = make_coefs()
    c["coef"] = np.ascontiguousarray(
        coef.transpose(1, 0, 2).reshape(128, NJ * WWIN)).astype(bfnp)
    c["ones_row"] = np.ones((1, 128), np.float32).astype(bfnp)
    c["ones_col"] = np.ones((128, 1), np.float32).astype(bfnp)
    c["ones_col_f32"] = np.ones((128, 1), np.float32)
    nt = cfg["S"] // 128
    sel = np.zeros((2, nt * nt), np.float32)
    for s in range(nt):
        for cc in range(nt):
            if cc < s:
                sel[0, s * nt + cc] = 1.0
            elif cc > s:
                sel[1, s * nt + cc] = 1.0
    c["sel"] = sel.astype(bfnp)
    return c


class K:

    def __init__(self, cfg):
        self.cfg = cfg
        self.nc = bass.Bass("TRN2", target_bir_lowering=False)
        self.plan = Plan()
        self.T = {}
        self.uid = 0

    def u(self, p="t"):
        self.uid += 1
        return f"{p}{self.uid}"

    def dram_in(self, name, shape, dt=F32):
        h = self.nc.dram_tensor(name, list(shape), dt, kind="ExternalInput")
        self.T[name] = h
        return h

    def dram_out(self, name, shape, dt=F32):
        h = self.nc.dram_tensor(name, list(shape), dt, kind="ExternalOutput")
        self.T[name] = h
        return h

    def dram_tmp(self, name, shape, dt=F32):
        h = self.nc.dram_tensor(name, list(shape), dt, kind="Internal")
        self.T[name] = h
        return h

    def sbuf(self, name, shape, dt=BF16):
        h = self.nc.alloc_sbuf_tensor(name, list(shape), dt)
        self.T[name] = h
        return h

    def psum(self, name, shape, dt=F32):
        h = self.nc.alloc_psum_tensor(name, list(shape), dt)
        self.T[name] = h
        return h




def build_program(cfg):
    S = cfg["S"]; NH = cfg["NH"]; HID = 64 * NH; FF = cfg["FF"]; BPC = cfg["BPC"]
    nt = S // 128; nhc = HID // 128; nfc = FF // 128
    NTOK = BPC * S
    nst = S // 512 if S >= 512 else 1
    SW = min(512, S)
    k = K(cfg)
    nc, plan, T = k.nc, k.plan, k.T
    ap = lambda n: T[n].ap()
    E = plan.emit

    k.dram_in("x", [NTOK, HID])
    for n in ("wq", "wk", "wv", "wo"):
        k.dram_in(n, [HID, HID])
    for n, w in (("bq2", nhc), ("bk2", nhc), ("bv2", nhc), ("bo2", nhc),
                 ("b12", nfc), ("b22", nhc)):
        k.dram_in(n, [128, w])
    k.dram_in("tkT2", [128, 65]); k.dram_in("tv", [NJ, 64]); k.dram_in("tv08", [2, 64])
    k.dram_in("w1T", [HID, FF]); k.dram_in("w2T", [FF, HID])
    k.dram_in("c_ident", [128, 128], BF16)
    k.dram_in("c_identd", [128, 64], BF16)
    k.dram_in("c_coef", [128, NJ * WWIN], BF16)
    k.dram_in("c_coefw", [128, NJ * WWIN], BF16)
    k.dram_in("c_ones_row", [1, 128], BF16)
    k.dram_in("c_ones_colb", [128, 1], BF16)
    k.dram_in("c_ones_colf", [128, 1], F32)
    k.dram_in("c_sel", [2, nt * nt], BF16)
    k.dram_in("c_vind", [128, nt * 33], BF16)
    k.dram_in("c_scal", [128, 22], F32)
    k.dram_in("c_p52", [128, 1], F32)
    k.dram_out("y", [NTOK, HID])
    k.dram_tmp("w1i_d", [HID, FF], BF16)
    k.dram_tmp("w2i_d", [FF, HID], BF16)
    k.dram_tmp("row_d", [8, max(S, 2048)], F32)
    k.dram_tmp("x2_d", [NTOK, HID], BF16)

    sb = k.sbuf
    sb("ident", [128, 128]); sb("identd", [128, 64])
    sb("coef", [128, NJ * WWIN]); sb("coefw", [128, NJ * WWIN]); sb("ones_row", [1, 128])
    sb("ones_colb", [128, 1]); sb("sel", [2, nt * nt]); sb("vind", [128, nt * 33])
    sb("tvsel", [16, nt * 64])
    sb("ones_colf", [128, 1], F32)
    sb("scal", [128, 22], F32)
    sb("p52", [128, 1], F32)
    sb("Ccol", [128, 1], F32)
    sb("tkT2_b", [128, 65])
    sb("KQ0a", [128, 128]); sb("KQ1a", [128, 128])
    sb("KQ0b", [128, 128]); sb("KQ1b", [128, 128])
    sb("s08row", [1, S]); sb("tv_b", [NJ, 64]); sb("tv08_b", [2, 64])
    for n, w in (("bq_s", nhc), ("bk_s", nhc), ("bv_s", nhc), ("bo_s", nhc),
                 ("b1_s", nfc), ("b2_s", nhc)):
        sb(n, [128, w], F32)
    sb("Wc0", [128, 128]); sb("Wc1", [128, 128])
    sb("XN1", [128, nhc * S])
    qvsz = max(nhc * S, nfc * min(512, S))
    sb("QF", [128, qvsz]); sb("KF", [128, nhc * S]); sb("VF", [128, qvsz])
    for r in range(3):
        sb(f"F{r}", [128, S])
    for r in range(4):
        sb(f"A2X{r}", [128, WWIN])
    VW = 64 + 32 + 1
    sb("vpm", [128, nt * VW])
    sb("SattnT", [NJ, S])
    sb("Sqp", [128, nt * NJ], F32)
    sb("FBqp", [128, nt * NJ], F32)
    sb("FBT", [NJ, S])
    sb("Rrows", [16, S])
    sb("zbt", [128, S])
    sb("negM", [128, 16], F32)
    sb("ttr_scr", [128, WWIN], F32)
    sb("vj", [128, 8], F32)
    sb("slotD0", [128, WWIN]); sb("slotD1", [128, WWIN])
    sb("fwv0", [128, WWIN]); sb("fwv1", [128, WWIN])
    sb("rowf", [1, 512], F32)
    sb("x2t", [128, HID])
    sb("xtile", [128, HID], F32)
    sb("toktmp", [128, max(HID, 1024)])
    sb("sqscr", [128, max(HID, 1024)])
    sb("statc", [128, 18], F32)
    sb("maxs", [128, 512], F32)
    sb("hcol", [128, 16], F32)
    sb("W1s", [128, 512])
    sb("W2s", [128, 512])
    sb("fq1", [128, max(HID, 1024)], F32)
    sb("fq2", [128, 512], F32); sb("x2t2", [128, 512], F32)
    k.psum("PS", [128, 4096], F32)
    PS = T["PS"].ap()

    def dma(q, dst, src, sem, reads=(), writes=()):
        E(q, (lambda d=dst, s=src: (lambda eng: eng.dma_start(out=d, in_=s)))(),
          reads=reads, writes=writes, dma=sem)

    for name, dst in (("c_ident", "ident"), ("c_identd", "identd"), ("c_coef", "coef"), ("c_coefw", "coefw"),
                      ("c_ones_row", "ones_row"), ("c_ones_colb", "ones_colb"),
                      ("c_sel", "sel"), ("c_vind", "vind")):
        dma("sync", ap(dst)[:, :], ap(name)[:, :], "d_c", writes=[dst])
    dma("sync", ap("ones_colf")[:, :], ap("c_ones_colf")[:, :], "d_c", writes=["ones_colf"])
    dma("sync", ap("scal")[:, :], ap("c_scal")[:, :], "d_c", writes=["scal"])
    for _i in range(2):
        k.dram_tmp(f"a2xd{_i}", [1, 128 * (WWIN + 2)], BF16)
        k.dram_tmp(f"fwd{_i}", [1, 128 * (WWIN + 1)], BF16)
    plan.emit("vector", lambda v: v.memset(T["slotD0"].ap()[:, :], 0.0),
              writes=["slotD0"])
    for _i in range(2):
        _v198 = ap(f"a2xd{_i}")[0:1, :].rearrange("o (p w) -> (o p) w", w=WWIN + 2)
        dma("sync", _v198[:, 0:WWIN], T["slotD0"].ap()[:, 0:WWIN],
            "d_az", reads=["slotD0"], writes=[f"a2xz{_i}"])
        dma("sync", _v198[:, WWIN:WWIN + 2], T["slotD0"].ap()[:, 0:2],
            "d_az", reads=["slotD0"], writes=[f"a2xz{_i}b"])
    for _nm in ("KQ0a", "KQ1a", "KQ0b", "KQ1b"):
        plan.emit("vector", (lambda _nm=_nm: (lambda v: v.memset(T[_nm].ap()[:, :], 0.0)))(),
                  writes=[f"{_nm}_z"])
    dma("sync", ap("p52")[:, :], ap("c_p52")[:, :], "d_c", writes=["p52"])
    for ci, cv in enumerate(SCAL_VALS(HID, FF)):
        nc.const_aps.register(F32, cv, T["scal"].ap()[:, ci:ci+1])
    E("vector", lambda v: v.tensor_scalar_add(T["statc"].ap()[:, 0:1],
        T["p52"].ap()[:, 0:1], 8388608.0), reads=["p52"], writes=["st0"])
    E("vector", lambda v: v.tensor_sub(T["statc"].ap()[:, 1:2],
        T["statc"].ap()[:, 0:1], T["p52"].ap()[:, 0:1]),
      reads=["st0", "p52"], writes=["st1"])
    E("vector", lambda v: v.tensor_scalar_mul(T["statc"].ap()[:, 2:3],
        T["statc"].ap()[:, 1:2], 805306368.0), reads=["st1"], writes=["st2"])
    E("vector", lambda v: v.tensor_scalar_mul(T["statc"].ap()[:, 1:2],
        T["statc"].ap()[:, 1:2], 1.5), reads=["st1"], writes=["st1"])
    E("vector", lambda v: v.tensor_sub(T["statc"].ap()[:, 2:3],
        T["statc"].ap()[:, 2:3], T["statc"].ap()[:, 1:2]),
      reads=["st2", "st1"], writes=["st2"])
    E("vector", lambda v: v.tensor_scalar_add(T["Ccol"].ap()[:, 0:1],
        T["statc"].ap()[:, 2:3], 12582912.0), reads=["st2"], writes=["Ccol"])
    for name, dst in (("bq2", "bq_s"), ("bk2", "bk_s"), ("bv2", "bv_s"),
                      ("bo2", "bo_s"), ("b12", "b1_s"), ("b22", "b2_s")):
        dma("sync", ap(dst)[:, :], ap(name)[:, :], "d_c", writes=[dst])
    for name, dst in (("tkT2", "tkT2_b"), ("tv", "tv_b"), ("tv08", "tv08_b")):
        dma("gpsimd", ap(dst)[:, :], ap(name)[:, :], "d_cg", writes=[dst])
    E("vector", lambda v: v.tensor_sub(T["tkT2_b"].ap()[:, 64:65],
        T["tkT2_b"].ap()[:, 0:1], T["tkT2_b"].ap()[:, 32:33]),
      reads=["tkT2_b"], writes=["tkT2_b"])
    for c in range(nt):
        dma("sync", ap("vpm")[:, VW*c+64:VW*(c+1)], ap("c_vind")[:, 33*c:33*(c+1)],
            "d_c", writes=["vpm"])
    for s in range(nt):
        E("tensor", (lambda s=s: (lambda te: te.matmul(
            T["PS"].ap()[0:nt, 3584+64*(s % 8):3584+64*(s % 8)+64],
            T["sel"].ap()[:, nt*s:nt*(s+1)], T["tv08_b"].ap()[:, :],
            start=True, stop=True)))(), reads=["sel", "tv08_b"], writes=["psB7"])
        E("vector", (lambda s=s: (lambda v: v.tensor_copy(
            T["tvsel"].ap()[0:nt, 64*s:64*(s+1)],
            T["PS"].ap()[0:nt, 3584+64*(s % 8):3584+64*(s % 8)+64])))(),
          reads=["psB7"], writes=["tvsel"])

    def wquant(srcname, dstname, rows, cols, lane):
        nr = rows // 128
        npc = (cols + 1023) // 1024
        first = True
        for r in range(nr):
          for pc in range(npc):
            c0, c1 = 1024 * pc, min(1024 * (pc + 1), cols)
            ld = "fq1" if (r * npc + pc) % 2 == 0 else "xtile"
            dma("sync", ap(ld)[:, 0:c1-c0], ap(srcname)[128*r:128*(r+1), c0:c1],
                "d_wq", writes=[ld])
            E("vector", (lambda c0=c0, c1=c1, ld=ld: (lambda v: v.tensor_reduce(
                T["statc"].ap()[:, 1:2], T[ld].ap()[:, 0:c1-c0], AX.X, ALU.add,
                apply_absolute_value=True)))(), reads=[ld], writes=["st1"])
            if first:
                first = False
                E("vector", lambda v: v.tensor_copy(T["statc"].ap()[:, 0:1],
                                                    T["statc"].ap()[:, 1:2]),
                  reads=["st1"], writes=["st0"])
            else:
                E("vector", lambda v: v.tensor_add(T["statc"].ap()[:, 0:1],
                                                   T["statc"].ap()[:, 0:1],
                                                   T["statc"].ap()[:, 1:2]),
                  reads=["st1", "st0"], writes=["st0"])
        E("tensor", lambda t: t.matmul(PS[0:1, 3584:3585], T["ones_colf"].ap()[:, 0:1],
                                       T["statc"].ap()[:, 0:1], start=True, stop=True),
          reads=["st0", "ones_colf"], writes=["psB7"])
        E("scalar", (lambda rows=rows, cols=cols: (lambda s: s.activation(
            T["statc"].ap()[0:1, 2:3], PS[0:1, 3584:3585], AF.Copy,
            scale=1.0 / (rows * cols))))(), reads=["psB7"], writes=["st2"])
        E("vector", lambda v: v.tensor_scalar_max(T["statc"].ap()[0:1, 2:3],
                                                  T["statc"].ap()[0:1, 2:3], 1e-5),
          reads=["st2"], writes=["st2"])
        E("scalar", lambda s: s.activation(T["statc"].ap()[0:1, 3:4],
                                           T["statc"].ap()[0:1, 2:3], AF.Ln),
          reads=["st2"], writes=["st3"])
        E("scalar", lambda s: s.activation(T["statc"].ap()[0:1, 3:4],
                                           T["statc"].ap()[0:1, 3:4], AF.Exp, scale=-1.0),
          reads=["st3"], writes=["st3"])
        dma("sync", ap("row_d")[lane:lane+1, 0:1], ap("statc")[0:1, 2:3],
            "d_wm", reads=["st2"], writes=[f"rdlane{lane}"])
        dma("sync", ap("row_d")[lane+2:lane+3, 0:1], ap("statc")[0:1, 3:4],
            "d_wm", reads=["st3"], writes=[f"rdlane{lane+2}"])
        dma("sync", ap("statc")[:, 3:4],
            ap("row_d")[lane+2:lane+3, 0:1].partition_broadcast(128),
            "d_wm2", reads=[f"rdlane{lane+2}"], writes=["st3c"])
        npc2 = (cols + 511) // 512
        for r in range(nr):
          for pc in range(npc2):
            c0, c1 = 512 * pc, min(512 * (pc + 1), cols)
            w_ = c1 - c0
            i_ = r * npc2 + pc
            ld = "fq1" if i_ % 2 == 0 else "xtile"
            lq = "fq2" if i_ % 2 == 0 else "x2t2"
            so_ = 512 * (i_ % 2)
            dma("sync", ap(ld)[:, 0:w_], ap(srcname)[128*r:128*(r+1), c0:c1],
                "d_wq", writes=[ld])
            E("vector", (lambda w_=w_, ld=ld, lq=lq: (lambda v: v.tensor_scalar(
                T[lq].ap()[:, 0:w_], T[ld].ap()[:, 0:w_],
                T["statc"].ap()[:, 3:4], T["Ccol"].ap()[:, 0:1],
                ALU.mult, ALU.add)))(),
              reads=[ld, "st3c", "Ccol"], writes=[lq])
            E("vector", (lambda w_=w_, lq=lq: (lambda v: v.tensor_scalar(
                T[lq].ap()[:, 0:w_], T[lq].ap()[:, 0:w_],
                T["Ccol"].ap()[:, 0:1], None, ALU.subtract)))(),
              reads=[lq, "Ccol"], writes=[lq])
            E("vector", (lambda w_=w_, lq=lq, so_=so_: (lambda v: v.tensor_scalar(
                T["sqscr"].ap()[:, so_:so_+w_], T[lq].ap()[:, 0:w_],
                1.0, -1.0, ALU.min, ALU.max)))(), reads=[lq],
              writes=[f"sqscr{i_ % 2}"])
            dma("gpsimd" if i_ % 2 else "sync",
                ap(dstname)[128*r:128*(r+1), c0:c1], ap("sqscr")[:, so_:so_+w_],
                "d_wqo" + str(i_ % 2), reads=[f"sqscr{i_ % 2}"], writes=[f"{dstname}{r}"])
    wquant("w1T", "w1i_d", HID, FF, 0)
    wquant("w2T", "w2i_d", FF, HID, 1)
    plan.emit("vector", lambda v: v.tensor_copy(T["sqscr"].ap()[0:1, 0:1],
                                                T["sqscr"].ap()[0:1, 0:1]),
              writes=["sqscr", "sqscr0", "sqscr1", "fq1", "fq2", "xtile", "x2t2"])

    for n in ("wq", "wk", "wv", "wo"):
        k.dram_tmp(f"{n}b_d", [HID, HID], BF16)
        dma("gpsimd", ap(f"{n}b_d")[:, :], ap(n)[:, :], "d_wc", writes=[f"{n}b"])

    for b in range(BPC):
        xoff = b * S

        for t in range(nt):
            dma("sync", ap("xtile")[:, :], ap("x")[xoff+128*t:xoff+128*(t+1), :],
                "d_x", writes=["xtile"])
            E("scalar", lambda s: s.activation(T["sqscr"].ap()[:, 0:HID],
                T["xtile"].ap()[:, :], AF.Square, accum_out=T["statc"].ap()[:, 1:2]),
              reads=["xtile"], writes=["sqscr", "st1"])
            E("vector", lambda v: v.reduce_sum(T["statc"].ap()[:, 0:1],
                T["xtile"].ap()[:, :], AX.X), reads=["xtile"], writes=["st0"])
            E("vector", (lambda H=HID: (lambda v: v.tensor_scalar_mul(
                T["statc"].ap()[:, 0:1], T["statc"].ap()[:, 0:1], 1.0 / H)))(),
              reads=["st0"], writes=["st0"])
            E("vector", lambda v: v.tensor_mul(T["statc"].ap()[:, 2:3],
                T["statc"].ap()[:, 0:1], T["statc"].ap()[:, 0:1]),
              reads=["st0"], writes=["st2"])
            E("vector", (lambda H=HID: (lambda v: v.scalar_tensor_tensor(
                T["statc"].ap()[:, 3:4], T["statc"].ap()[:, 1:2], 1.0 / H,
                T["statc"].ap()[:, 2:3], ALU.mult, ALU.subtract)))(),
              reads=["st1", "st2"], writes=["st3"])
            E("vector", lambda v: v.tensor_scalar_add(
                T["statc"].ap()[:, 3:4], T["statc"].ap()[:, 3:4],
                (1e-5 - float(np.float32(1e-5)))), reads=["st3"], writes=["st3"])
            E("scalar", lambda s: s.activation(T["statc"].ap()[:, 3:4],
                T["statc"].ap()[:, 3:4], AF.Ln, bias=1e-5), reads=["st3"], writes=["st3"])
            E("scalar", lambda s: s.activation(T["statc"].ap()[:, 3:4],
                T["statc"].ap()[:, 3:4], AF.Exp, scale=-0.5), reads=["st3"], writes=["st3"])
            E("vector", lambda v: v.tensor_scalar(T["toktmp"].ap()[:, 0:HID],
                T["xtile"].ap()[:, :], T["statc"].ap()[:, 0:1],
                T["statc"].ap()[:, 3:4], ALU.subtract, ALU.mult),
              reads=["xtile", "st0", "st3"], writes=["toktmp"])
            for c in range(nhc):
                E("tensor", (lambda c=c: (lambda te: te.matmul(
                    PS[0:128, 2048+128*c:2048+128*(c+1)],
                    T["toktmp"].ap()[:, 128*c:128*(c+1)], T["ident"].ap()[:, :],
                    start=True, stop=True)))(), reads=["toktmp", "ident"],
                  writes=psnames(2048+128*c, 2048+128*(c+1)))
            E("vector", (lambda t=t: (lambda v: v.tensor_copy(
                T["XN1"].ap().rearrange("p (c s) -> p c s", s=S)[:, :, 128*t:128*(t+1)],
                PS[0:128, 2048:2048+128*nhc].rearrange("p (c s) -> p c s", s=128))))(),
              reads=psnames(2048, 2048+128*nhc), writes=["XN1"])

        def proj(wname, bname, dstname):
            wslot = [0]
            for oc in range(nhc):
                    for i in range(nhc):
                        ws = f"Wc{wslot[0] % 2}"; wslot[0] += 1
                        dma("sync", T[ws].ap()[:, 0:128],
                            ap(f"{wname}b_d")[128*i:128*(i+1), 128*oc:128*(oc+1)], "d_w",
                            reads=[f"{wname}b"], writes=[ws])
                        for st4 in range(nst):
                            pso = 2048 + 512 * st4
                            pname = ["psB4", "psB5", "psB6w0", "psB7"][st4]
                            E("tensor", (lambda oc=oc, st4=st4, i=i, pso=pso, ws=ws: (lambda te: te.matmul(
                                PS[0:128, pso:pso+SW], T[ws].ap()[:, 0:128],
                                T["XN1"].ap()[:, S*i+SW*st4:S*i+SW*(st4+1)],
                                start=(i == 0), stop=(i == nhc-1))))(),
                              reads=[ws, "XN1"], writes=[pname])
                    for st4 in range(nst):
                        pso = 2048 + 512 * st4
                        pname = ["psB4", "psB5", "psB6w0", "psB7"][st4]
                        E("scalar", (lambda oc=oc, st4=st4, pso=pso, bname=bname, dstname=dstname:
                            (lambda s: s.activation(
                                T[dstname].ap()[:, S*oc+SW*st4:S*oc+SW*(st4+1)],
                                PS[0:128, pso:pso+SW], AF.Identity,
                                bias=T[bname].ap()[:, oc:oc+1])))(),
                          reads=[pname, bname], writes=[dstname])
        proj("wq", "bq_s", "QF")
        proj("wk", "bk_s", "KF")
        proj("wv", "bv_s", "VF")

        for side, FNAME, lane in (("q", "QF", 4), ("k", "KF", 5)):
            for hc in range(nhc):
                for half in range(2):
                    h = 2 * hc + half
                    hr = 64 * half
                    for st4 in range(nst):
                        E("scalar", (lambda hc=hc, hr=hr, st4=st4, FNAME=FNAME: (lambda s: s.activation(
                            T["sqscr"].ap()[hr:hr+64, 0:SW],
                            T[FNAME].ap()[hr:hr+64, S*hc+SW*st4:S*hc+SW*(st4+1)],
                            AF.Square)))(), reads=[FNAME], writes=["sqscr"])
                        E("tensor", (lambda st4=st4, hr=hr: (lambda te: te.matmul(
                            PS[0:1, 3584:3584+SW],
                            T["ones_colb"].ap()[hr:hr+64, 0:1],
                            T["sqscr"].ap()[hr:hr+64, 0:SW],
                            start=True, stop=True)))(),
                          reads=["sqscr", "ones_colb"], writes=["psB7"])
                        E("vector", (lambda: (lambda v: v.tensor_reduce(
                            T["statc"].ap()[0:1, 8:9], PS[0:1, 3584:3584+SW],
                            AX.X, ALU.max)))(), reads=["psB7"], writes=["st8"])
                        if st4 == 0:
                            E("vector", (lambda: (lambda v: v.tensor_copy(
                                T["statc"].ap()[0:1, 9:10], T["statc"].ap()[0:1, 8:9])))(),
                              reads=["st8"], writes=["st9"])
                        else:
                            E("vector", lambda v: v.tensor_max(
                                T["statc"].ap()[0:1, 9:10], T["statc"].ap()[0:1, 9:10],
                                T["statc"].ap()[0:1, 8:9]), reads=["st8", "st9"], writes=["st9"])
                    E("scalar", lambda s: s.activation(T["statc"].ap()[0:1, 9:10],
                        T["statc"].ap()[0:1, 9:10], AF.Ln), reads=["st9"], writes=["st9"])
                    E("scalar", lambda s: s.activation(T["statc"].ap()[0:1, 9:10],
                        T["statc"].ap()[0:1, 9:10], AF.Exp, scale=0.5), reads=["st9"], writes=["st9"])
                    dma("sync", ap("row_d")[lane:lane+1, h:h+1], ap("statc")[0:1, 9:10],
                        "d_m", reads=["st9"], writes=[f"rdl_{side}{h}"])
        if b == 0:
            E("scalar", lambda s: s.activation(T["sqscr"].ap()[0:64, 0:NJ],
                T["tkT2_b"].ap()[0:64, 0:NJ], AF.Square), reads=["tkT2_b"], writes=["sqscr"])
            E("tensor", lambda te: te.matmul(PS[0:1, 3584:3584+NJ],
                T["ones_colb"].ap()[0:64, 0:1], T["sqscr"].ap()[0:64, 0:NJ],
                start=True, stop=True), reads=["sqscr", "ones_colb"], writes=["psB7"])
            E("vector", lambda v: v.tensor_reduce(T["statc"].ap()[0:1, 10:11],
                PS[0:1, 3584:3584+NJ], AX.X, ALU.max), reads=["psB7"], writes=["st10"])
            E("scalar", lambda s: s.activation(T["statc"].ap()[0:1, 10:11],
                T["statc"].ap()[0:1, 10:11], AF.Ln), reads=["st10"], writes=["st10"])
            E("scalar", lambda s: s.activation(T["statc"].ap()[0:1, 10:11],
                T["statc"].ap()[0:1, 10:11], AF.Exp, scale=0.5), reads=["st10"], writes=["st10"])
            dma("sync", ap("row_d")[6:7, 0:1], ap("statc")[0:1, 10:11], "d_m",
                reads=["st10"], writes=["rdl_t"])
        build_attn(k, cfg, b)
        if not cfg.get("no_tail"):
            build_tail(k, cfg, b)
    return k

def psnames(c0, c1):
    out = []
    if c0 < 2048:
        out.append("psAV")
    for bkt, nm in ((2048, "psB4"), (2560, "psB5"), (3072, "psB6"), (3584, "psB7")):
        if c0 < bkt + 512 and c1 > bkt:
            out.append(nm)
    return out


def build_attn(k, cfg, b):
    S = cfg["S"]; NH = cfg["NH"]; HID = 64 * NH
    nt = S // 128; nhc = HID // 128
    nst = max(1, S // 512); SW = min(512, S)
    VW = 64 + 32 + 1
    T, plan = k.T, k.plan
    ap = lambda n: T[n].ap()
    PS = T["PS"].ap()
    E = plan.emit

    def dma(q, dst, src, sem, reads=(), writes=()):
        E(q, (lambda d=dst, s=src: (lambda eng: eng.dma_start(out=d, in_=s)))(),
          reads=reads, writes=writes, dma=sem)

    dma("sync", ap("maxs")[0:1, 0:NH], ap("row_d")[4:5, 0:NH], "d_m2",
        reads=[f"rdl_q{h}" for h in range(NH)], writes=["nmq"])
    dma("sync", ap("maxs")[0:1, 32:32+NH], ap("row_d")[5:6, 0:NH], "d_m2",
        reads=[f"rdl_k{h}" for h in range(NH)], writes=["nmk"])
    dma("sync", ap("maxs")[0:1, 64:65], ap("row_d")[6:7, 0:1], "d_m2",
        reads=["rdl_t"], writes=["nmt"])
    E("vector", lambda v: v.tensor_scalar(T["maxs"].ap()[0:1, 32:32+NH],
        T["maxs"].ap()[0:1, 32:32+NH], T["maxs"].ap()[0:1, 64:65], None, ALU.add),
      reads=["nmk", "nmt"], writes=["nmk"])
    E("vector", lambda v: v.tensor_mul(T["maxs"].ap()[0:1, 0:NH],
        T["maxs"].ap()[0:1, 0:NH], T["maxs"].ap()[0:1, 32:32+NH]),
      reads=["nmq", "nmk"], writes=["nmq"])
    E("vector", lambda v: v.tensor_scalar_mul(T["maxs"].ap()[0:1, 0:NH],
        T["maxs"].ap()[0:1, 0:NH], -0.125), reads=["nmq"], writes=["nmq"])
    dma("sync", ap("row_d")[7:8, 0:NH], ap("maxs")[0:1, 0:NH], "d_m3",
        reads=["nmq"], writes=["negm_r"])
    dma("sync", ap("negM")[:, 0:NH],
        ap("row_d")[7:8, 0:NH].partition_broadcast(128), "d_m4",
        reads=["negm_r"], writes=["negM"])

    for h in range(cfg.get("heads", NH)):
        hc, hr = h // 2, 64 * (h % 2)
        qf = lambda c0, c1, hr=hr, hc=hc: T["QF"].ap()[hr:hr+64, S*hc+c0:S*hc+c1]
        kf = lambda c0, c1, hr=hr, hc=hc: T["KF"].ap()[hr:hr+64, S*hc+c0:S*hc+c1]
        vf = lambda c0, c1, hr=hr, hc=hc: T["VF"].ap()[hr:hr+64, S*hc+c0:S*hc+c1]

        for t in range(nt):
            E("tensor", (lambda t=t, vf=vf, hr=hr: (lambda te: te.matmul(
                PS[0:128, 2048+64*t:2048+64*(t+1)], vf(128*t, 128*(t+1)),
                T["identd"].ap()[hr:hr+64, 0:64], start=True, stop=True)))(),
              reads=["VF", "identd"], writes=psnames(2048+64*t, 2048+64*(t+1)))
        E("vector", (lambda nt=nt: (lambda v: v.tensor_copy(
            T["vpm"].ap().rearrange("p (c w) -> p c w", w=VW)[:, :, 0:64],
            PS[0:128, 2048:2048+64*nt].rearrange("p (c w) -> p c w", w=64))))(),
          reads=psnames(2048, 2048+64*nt), writes=["vpm"])

        for st4 in range(nst):
            E("tensor", (lambda st4=st4, qf=qf, hr=hr: (lambda te: te.matmul(
                PS[0:65, 3584:3584+SW], T["tkT2_b"].ap()[hr:hr+64, :],
                qf(SW*st4, SW*(st4+1)), start=True, stop=True)))(),
              reads=["tkT2_b", "QF"], writes=["psB7"])
            E("scalar", (lambda st4=st4: (lambda s: s.activation(
                T["SattnT"].ap()[:, SW*st4:SW*(st4+1)], PS[0:NJ, 3584:3584+SW],
                AF.Copy)))(), reads=["psB7"], writes=["SattnT"])
            E("scalar", (lambda st4=st4: (lambda s: s.activation(
                T["s08row"].ap()[0:1, SW*st4:SW*(st4+1)], PS[64:65, 3584:3584+SW],
                AF.Copy)))(), reads=["psB7"], writes=["s08row"])

        for t in range(nt):
            E("tensor", (lambda t=t: (lambda te: te.matmul(
                PS[0:128, 3072+NJ*t:3072+NJ*(t+1)],
                T["SattnT"].ap()[0:NJ, 128*t:128*(t+1)], T["ident"].ap()[0:NJ, 0:NJ],
                start=True, stop=True)))(), reads=["SattnT", "ident"],
              writes=["psB6w0"])
        E("vector", (lambda nt=nt: (lambda v: v.tensor_copy(
            T["Sqp"].ap()[:, 0:NJ*nt], PS[0:128, 3072:3072+NJ*nt])))(),
          reads=["psB6w0"], writes=["Sqp"])


        def build_window(tw):
            slot = f"A2X{tw % 4}"
            blk = f"a2xd{tw % 2}"
            sD = f"slotD{tw % 2}"
            v198 = ap(blk)[0:1, :].rearrange("o (p w) -> (o p) w", w=WWIN + 2)
            v197 = ap(blk)[0:1, 0:128 * (WWIN + 1)].rearrange(
                "o (p w) -> (o p) w", w=WWIN + 1)
            E("vector", (lambda tw=tw: (lambda v: v.tensor_scalar(
                T["vj"].ap()[:, 0:8], T["Sqp"].ap()[:, NJ*tw:NJ*tw+8],
                T["Sqp"].ap()[:, NJ*tw+8:NJ*tw+9], None, ALU.subtract)))(),
              reads=["Sqp"], writes=["vj"])
            dma("gpsimd", v198[:, LPAD-3:LPAD+4], T["vj"].ap()[:, 1:8], "d_a2w",
                reads=["vj", f"a2xz{tw % 2}", f"a2xz{tw % 2}b"], writes=[f"a2dg{tw % 2}"])
            dma("sync", T[sD].ap()[:, 0:WWIN], v197[:, 0:WWIN], "d_a2r",
                reads=[f"a2dg{tw % 2}"], writes=[sD])
            E("vector", (lambda tw=tw, slot=slot: (lambda v: v.tensor_scalar_mul(
                T[slot].ap()[:, :], T["coef"].ap()[:, 0:WWIN],
                T["vj"].ap()[:, 0:1])))(),
              reads=["coef", "vj"], writes=[slot])
            E("vector", (lambda slot=slot, sD=sD: (lambda v: v.tensor_add(
                T[slot].ap()[:, :], T[slot].ap()[:, :], T[sD].ap()[:, :])))(),
              reads=[slot, sD], writes=[slot])
        build_window(0)
        if nt > 1:
            build_window(1)

        def fw_window(tw):
            c0 = 0 if tw > 0 else LPAD
            c1 = WWIN if tw < nt - 1 else LPAD + 128
            base = 3072 + 512 * (tw % 2)
            pw = "psB6w0" if tw % 2 == 0 else "psB7"
            fv = f"fwv{tw % 2}"
            blk = f"fwd{tw % 2}"
            if tw > 0:
                E("tensor", (lambda tw=tw, base=base: (lambda te: te.matmul(
                    PS[0:128, base:base+LPAD],
                    T[f"F{(tw-1) % 3}"].ap()[64:128, 128*tw:128*(tw+1)],
                    T["identd"].ap()[64:128, 0:64], start=True, stop=True)))(),
                  reads=[f"F{(tw-1) % 3}s{(128*tw) // SW}", "identd"], writes=[pw])
            E("tensor", (lambda tw=tw, base=base: (lambda te: te.matmul(
                PS[0:128, base+LPAD:base+LPAD+128],
                T[f"F{tw % 3}"].ap()[0:128, 128*tw:128*(tw+1)],
                T["ident"].ap()[:, :], start=True, stop=True)))(),
              reads=[f"F{tw % 3}s{(128*tw) // SW}", "ident"], writes=[pw])
            if tw < nt - 1:
                E("tensor", (lambda tw=tw, base=base: (lambda te: te.matmul(
                    PS[0:128, base+LPAD+128:base+WWIN],
                    T[f"F{(tw+1) % 3}"].ap()[0:RPAD, 128*tw:128*(tw+1)],
                    T["ident"].ap()[0:RPAD, 0:RPAD], start=True, stop=True)))(),
                  reads=[f"F{(tw+1) % 3}s{(128*tw) // SW}", "ident"], writes=[pw])
            if tw == 0:
                E("vector", (lambda fv=fv: (lambda v: v.memset(
                    T[fv].ap()[:, 0:LPAD], 0.0)))(), reads=[], writes=[fv])
            if tw == nt - 1:
                E("vector", (lambda fv=fv: (lambda v: v.memset(
                    T[fv].ap()[:, LPAD+128:WWIN], 0.0)))(), reads=[], writes=[fv])
            E("scalar", (lambda base=base, c0=c0, c1=c1, fv=fv: (lambda s: s.activation(
                T[fv].ap()[:, c0:c1], PS[0:128, base+c0:base+c1], AF.Copy)))(),
              reads=[pw], writes=[fv])
            dma("scalar", ap(blk)[0:1, 0:128 * WWIN].rearrange("o (p w) -> (o p) w", w=WWIN),
                T[fv].ap()[:, 0:WWIN], "d_fww", reads=[fv], writes=[f"fwb{tw % 2}"])
            dma("gpsimd",
                T["FBqp"].ap()[:, NJ*tw+1:NJ*tw+8],
                ap(blk)[0:1, 0:128 * (WWIN + 1)].rearrange(
                    "o (p w) -> (o p) w", w=WWIN + 1)[:, LPAD-3:LPAD+4],
                "d_fwr", reads=[f"fwb{tw % 2}"], writes=["FBqp"])
            for j in (0, 8):
                s0_, s1_ = (61, 188) if j == 0 else (68, 195)
                cc0, cc1 = max(c0, s0_), min(c1, s1_)
                E("vector", (lambda tw=tw, j=j, cc0=cc0, cc1=cc1, fv=fv: (lambda v: v.scalar_tensor_tensor(
                    T["ttr_scr"].ap()[:, cc0:cc1], T[fv].ap()[:, cc0:cc1], 1.0,
                    T["coefw"].ap()[:, WWIN*j+cc0:WWIN*j+cc1],
                    ALU.mult, ALU.mult,
                    accum_out=T["FBqp"].ap()[:, NJ*tw+j:NJ*tw+j+1])))(),
                  reads=[fv, "coefw"], writes=["ttr_scr", "FBqp"])

        par = "a" if hr == 0 else "b"
        for t in range(nt):
            stag = f"KQ{t % 2}{par}"
            E("scalar", (lambda t=t, kf=kf, stag=stag, hr=hr: (lambda s: s.activation(
                T[stag].ap()[hr:hr+64, 0:128], kf(128*t, 128*(t+1)), AF.Copy)))(),
              reads=["KF", f"{stag}_z"], writes=[stag])
            if t + 2 < nt:
                build_window(t + 2)
            for st4 in range(nst):
                g0, g1 = SW * st4, SW * (st4 + 1)
                pso = 2048 + 512 * (st4 % 2)
                pname = "psB4" if st4 % 2 == 0 else "psB5"
                lo, hi = max(g0, 128 * (t + 1)), g1
                have_sfx = lo < hi
                have_a = g0 <= 128 * t < g1
                have_b = t >= 1 and g0 <= 128 * (t - 1) < g1
                have_c = t + 1 < nt and g0 <= 128 * (t + 1) < g1
                npieces = int(have_a) + int(have_b) + int(have_c) + int(have_sfx)
                E("tensor", (lambda g0=g0, g1=g1, t=t, pso=pso, hc=hc, stag=stag,
                             stp=(npieces == 0): (lambda te: te.matmul(
                    PS[0:128, pso:pso+SW], T[stag].ap()[:, :],
                    T["QF"].ap()[0:128, S*hc+g0:S*hc+g1],
                    start=True, stop=stp)))(),
                  reads=[stag, "QF"], writes=[pname])
                if have_b:
                    E("tensor", (lambda t=t, pso=pso, g0=g0: (lambda te: te.matmul(
                        PS[0:RPAD, pso+128*(t-1)-g0:pso+128*t-g0],
                        T[f"A2X{(t-1) % 4}"].ap()[:, LPAD+128:WWIN],
                        T["ident"].ap()[:, :], start=False, stop=False)))(),
                      reads=[f"A2X{(t-1) % 4}", "ident"], writes=[pname])
                if have_c:
                    E("tensor", (lambda t=t, pso=pso, g0=g0: (lambda te: te.matmul(
                        PS[64:128, pso+128*(t+1)-g0:pso+128*(t+2)-g0],
                        T[f"A2X{(t+1) % 4}"].ap()[:, 0:LPAD],
                        T["ident"].ap()[:, :], start=False, stop=False)))(),
                      reads=[f"A2X{(t+1) % 4}", "ident"], writes=[pname])
                if have_a:
                    E("tensor", (lambda t=t, pso=pso, g0=g0, stp=not have_sfx: (lambda te: te.matmul(
                        PS[0:128, pso+128*t-g0:pso+128*(t+1)-g0],
                        T[f"A2X{t % 4}"].ap()[:, LPAD:LPAD+128],
                        T["ident"].ap()[:, :], start=False, stop=stp)))(),
                      reads=[f"A2X{t % 4}", "ident"], writes=[pname])
                if have_sfx:
                    E("tensor", (lambda lo=lo, hi=hi, pso=pso, g0=g0: (lambda te: te.matmul(
                        PS[0:128, pso+lo-g0:pso+hi-g0], T["ones_row"].ap()[:, :],
                        T["s08row"].ap()[0:1, lo:hi], start=False, stop=True)))(),
                      reads=["ones_row", "s08row"], writes=[pname])
                elif not have_a and npieces > 0:
                    E("tensor", (lambda pso=pso: (lambda te: te.matmul(
                        PS[0:128, pso:pso+1], T["ones_row"].ap()[:, :],
                        T["ident"].ap()[0:1, 1:2], start=False, stop=True)))(),
                      reads=["ones_row", "ident"], writes=[pname])
                E("scalar", (lambda g0=g0, g1=g1, t=t, pso=pso, h=h: (lambda s: s.activation(
                    T[f"F{t % 3}"].ap()[:, g0:g1], PS[0:128, pso:pso+SW], AF.Exp,
                    bias=T["negM"].ap()[:, h:h+1], scale=0.125)))(),
                  reads=[pname, "negM"], writes=[f"F{t % 3}s{st4}"])
                if t < nt - 1:
                    E("tensor", (lambda t=t, g0=g0, g1=g1: (lambda te: te.matmul(
                        PS[0:VW, g0:g1], T["vpm"].ap()[:, VW*t:VW*(t+1)],
                        T[f"F{t % 3}"].ap()[:, g0:g1],
                        start=(t == 0), stop=False)))(),
                      reads=["vpm", f"F{t % 3}s{st4}"], writes=["psAV"])
            if t >= 1:
                fw_window(t - 1)
            if t == nt - 1:
                fw_window(t)
                E("vector", (lambda nt=nt: (lambda v: v.tensor_copy(
                    T["sqscr"].ap()[:, 0:NJ*nt], T["FBqp"].ap()[:, 0:NJ*nt])))(),
                  reads=["FBqp"], writes=["sqscr"])
                for s4 in range((nt + 3) // 4):
                    ss = list(range(4 * s4, min(4 * s4 + 4, nt)))
                    for i_, s in enumerate(ss):
                        E("tensor", (lambda s=s, i_=i_: (lambda te: te.matmul(
                            PS[0:NJ, 2048+128*i_:2048+128*(i_+1)],
                            T["sqscr"].ap()[:, NJ*s:NJ*(s+1)], T["ident"].ap()[:, :],
                            start=True, stop=True)))(), reads=["sqscr", "ident"],
                          writes=psnames(2048+128*i_, 2048+128*(i_+1)))
                    E("vector", (lambda s4=s4, n=len(ss): (lambda v: v.tensor_copy(
                        T["FBT"].ap()[0:NJ, 512*s4:512*s4+128*n],
                        PS[0:NJ, 2048:2048+128*n])))(),
                      reads=psnames(2048, 2048+128*len(ss)), writes=["FBT"])
                for s in range(nt):
                    E("tensor", (lambda s=s: (lambda te: te.matmul(
                        PS[0:64, 128*s:128*(s+1)], T["tv_b"].ap()[:, :],
                        T["FBT"].ap()[0:NJ, 128*s:128*(s+1)], start=False, stop=False)))(),
                      reads=["tv_b", "FBT"], writes=["psAV"])
                for st4 in range(nst):
                    g0, g1 = SW * st4, SW * (st4 + 1)
                    E("tensor", (lambda t=t, g0=g0, g1=g1: (lambda te: te.matmul(
                        PS[0:VW, g0:g1], T["vpm"].ap()[:, VW*t:VW*(t+1)],
                        T[f"F{t % 3}"].ap()[:, g0:g1],
                        start=(t == 0), stop=True)))(),
                      reads=["vpm", f"F{t % 3}s{st4}"], writes=["psAV"])

        E("vector", (lambda nt=nt: (lambda v: v.tensor_copy(
            T["Rrows"].ap()[0:nt, 0:S], PS[64:64+nt, 0:S])))(),
          reads=["psAV"], writes=["Rrows"])
        for st4 in range(nst):
            z0, z1 = SW * st4, SW * (st4 + 1)
            E("scalar", (lambda nt=nt, z0=z0, z1=z1: (lambda s: s.activation(
                T["rowf"].ap()[0:1, 0:z1-z0], PS[96:97, z0:z1], AF.Copy)))(),
              reads=["psAV"], writes=["rowf"])
            E("scalar", (lambda z0=z0, z1=z1: (lambda s: s.activation(
                T["rowf"].ap()[0:1, 0:z1-z0], T["rowf"].ap()[0:1, 0:z1-z0],
                AF.Ln)))(), reads=["rowf"], writes=["rowf"])
            E("scalar", (lambda z0=z0, z1=z1: (lambda s: s.activation(
                T["rowf"].ap()[0:1, 0:z1-z0], T["rowf"].ap()[0:1, 0:z1-z0],
                AF.Exp, scale=-1.0)))(), reads=["rowf"], writes=["rowf"])
            dma("sync", ap("row_d")[2:3, z0:z1], ap("rowf")[0:1, 0:z1-z0], "d_z",
                reads=["rowf"], writes=["rdlane2"])
        dma("gpsimd", ap("zbt")[:, 0:S],
            ap("row_d")[2:3, 0:S].partition_broadcast(128), "d_zb",
            reads=["rdlane2"], writes=["zbt"])

        for st4 in range(nst):
            g0, g1 = SW*st4, SW*(st4+1)
            for si in range((g1 - g0) // 128):
                s = (g0 // 128) + si
                pso = 2048 + 128 * (s % 8)
                E("tensor", (lambda s=s, pso=pso: (lambda te: te.matmul(
                    PS[0:64, pso:pso+128], T["tvsel"].ap()[0:nt, 64*s:64*(s+1)],
                    T["Rrows"].ap()[0:nt, 128*s:128*(s+1)], start=True, stop=True)))(),
                  reads=["tvsel", "Rrows"], writes=psnames(pso, pso+128))
                E("vector", (lambda s=s, si=si, pso=pso: (lambda v: v.tensor_tensor(
                    T["toktmp"].ap()[0:64, SW+128*si:SW+128*(si+1)], PS[0:64, pso:pso+128],
                    T["zbt"].ap()[0:64, 128*s:128*(s+1)], ALU.mult)))(),
                  reads=psnames(pso, pso+128) + ["zbt"], writes=["toktmp2"])
            E("vector", (lambda g0=g0, g1=g1: (lambda v: v.tensor_tensor(
                T["toktmp"].ap()[0:64, 0:g1-g0], PS[0:64, g0:g1],
                T["zbt"].ap()[0:64, g0:g1], ALU.mult)))(),
              reads=["psAV", "zbt"], writes=["toktmp"])
            E("vector", (lambda g0=g0, g1=g1, hc=hc, hr=hr: (lambda v: v.tensor_add(
                T["XN1"].ap()[hr:hr+64, S*hc+g0:S*hc+g1],
                T["toktmp"].ap()[0:64, 0:g1-g0],
                T["toktmp"].ap()[0:64, SW:SW+g1-g0])))(),
              reads=["toktmp", "toktmp2"], writes=["XN1"])
        if cfg.get("dbg") and h == cfg["NH"] - 1 and b == 0:
            if "dbg_d" not in T:
                k.dram_tmp("dbg_d", [192, S], BF16)
            dma("sync", ap("dbg_d")[0:128, :], T["XN1"].ap()[0:128, 0:S], "d_dbg",
                reads=["XN1"], writes=["dbgd"])
    if cfg.get("dbg") and b == 0:
        if "dbg3_d" not in T:
            k.dram_tmp("dbg3_d", [128, S], BF16)


def build_tail(k, cfg, b):
    S = cfg["S"]; NH = cfg["NH"]; HID = 64 * NH; FF = cfg["FF"]
    nt = S // 128; nhc = HID // 128; nfc = FF // 128
    nst = max(1, S // 512); SW = min(512, S)
    T, plan = k.T, k.plan
    ap = lambda n: T[n].ap()
    PS = T["PS"].ap()
    E = plan.emit
    xoff = b * S
    LN127 = float(np.log(127.0))

    def dma(q, dst, src, sem, reads=(), writes=()):
        E(q, (lambda d=dst, s=src: (lambda eng: eng.dma_start(out=d, in_=s)))(),
          reads=reads, writes=writes, dma=sem)

    _BN = ["psB4", "psB5", "psB6w0", "psB7"]
    wslot = [0]
    for oc in range(nhc):
        for i in range(nhc):
            ws = f"Wc{wslot[0] % 2}"; wslot[0] += 1
            dma("sync", T[ws].ap()[:, 0:128],
                ap("wob_d")[128*i:128*(i+1), 128*oc:128*(oc+1)], "d_w",
                reads=["wob"], writes=[ws])
            for st4 in range(nst):
                pso = 2048 + 512 * st4
                E("tensor", (lambda st4=st4, i=i, pso=pso, ws=ws: (lambda te: te.matmul(
                    PS[0:128, pso:pso+SW], T[ws].ap()[:, 0:128],
                    T["XN1"].ap()[:, S*i+SW*st4:S*i+SW*(st4+1)],
                    start=(i == 0), stop=(i == nhc-1))))(),
                  reads=[ws, "XN1"], writes=[_BN[st4]])
        for st4 in range(nst):
            pso = 2048 + 512 * st4
            E("scalar", (lambda oc=oc, st4=st4, pso=pso: (lambda s: s.activation(
                T["QF"].ap()[:, S*oc+SW*st4:S*oc+SW*(st4+1)], PS[0:128, pso:pso+SW],
                AF.Identity, bias=T["bo_s"].ap()[:, oc:oc+1])))(),
              reads=[_BN[st4], "bo_s"], writes=["QF"])

    if cfg.get("dbg") and b == 0:
        if "dbg3_d" not in T:
            k.dram_tmp("dbg3_d", [128, S], BF16)
        dma("sync", ap("dbg3_d")[0:128, :], T["QF"].ap()[0:128, 0:S], "d_dbg",
            reads=["QF"], writes=["dbg3"])

    def ln_tok_chain(src_name, src_slice, eps, out_name, out_slice, width):
        E("scalar", (lambda src_slice=src_slice, width=width: (lambda s: s.activation(
            T["sqscr"].ap()[:, 0:width], src_slice(), AF.Square,
            accum_out=T["statc"].ap()[:, 1:2])))(),
          reads=[src_name], writes=["sqscr", "st1"])
        E("vector", (lambda src_slice=src_slice: (lambda v: v.reduce_sum(
            T["statc"].ap()[:, 0:1], src_slice(), AX.X)))(),
          reads=[src_name], writes=["st0"])
        E("vector", (lambda width=width: (lambda v: v.tensor_scalar_mul(
            T["statc"].ap()[:, 0:1], T["statc"].ap()[:, 0:1], 1.0 / width)))(),
          reads=["st0"], writes=["st0"])
        E("vector", lambda v: v.tensor_mul(T["statc"].ap()[:, 2:3],
            T["statc"].ap()[:, 0:1], T["statc"].ap()[:, 0:1]), reads=["st0"], writes=["st2"])
        E("vector", (lambda width=width: (lambda v: v.scalar_tensor_tensor(
            T["statc"].ap()[:, 3:4], T["statc"].ap()[:, 1:2], 1.0 / width,
            T["statc"].ap()[:, 2:3], ALU.mult, ALU.subtract)))(),
          reads=["st1", "st2"], writes=["st3"])
        E("vector", (lambda eps=eps: (lambda v: v.tensor_scalar_add(
            T["statc"].ap()[:, 3:4], T["statc"].ap()[:, 3:4],
            (eps - float(np.float32(eps))))))(), reads=["st3"], writes=["st3"])
        E("scalar", (lambda eps=eps: (lambda s: s.activation(T["statc"].ap()[:, 3:4],
            T["statc"].ap()[:, 3:4], AF.Ln, bias=eps)))(), reads=["st3"], writes=["st3"])
        E("scalar", lambda s: s.activation(T["statc"].ap()[:, 3:4],
            T["statc"].ap()[:, 3:4], AF.Exp, scale=-0.5), reads=["st3"], writes=["st3"])
        E("vector", (lambda src_slice=src_slice, out_slice=out_slice: (lambda v: v.tensor_scalar(
            out_slice(), src_slice(), T["statc"].ap()[:, 0:1], T["statc"].ap()[:, 3:4],
            ALU.subtract, ALU.mult)))(),
          reads=[src_name, "st0", "st3"], writes=[out_name])

    for t in range(nt):
        for c in range(nhc):
            E("tensor", (lambda t=t, c=c: (lambda te: te.matmul(
                PS[0:128, 2048+128*c:2048+128*(c+1)],
                T["QF"].ap()[:, S*c+128*t:S*c+128*(t+1)], T["ident"].ap()[:, :],
                start=True, stop=True)))(), reads=["QF", "ident"],
              writes=psnames(2048+128*c, 2048+128*(c+1)))
        E("vector", lambda v: v.tensor_copy(T["fq1"].ap()[:, 0:HID],
            PS[0:128, 2048:2048+HID]), reads=psnames(2048, 2048+HID), writes=["fq1"])
        ln_tok_chain("fq1", (lambda: T["fq1"].ap()[:, 0:HID]), 1e-5,
                     "toktmp", (lambda: T["toktmp"].ap()[:, 0:HID]), HID)
        dma("sync", ap("xtile")[:, :], ap("x")[xoff+128*t:xoff+128*(t+1), :],
            "d_x2", writes=["xtile"])
        E("vector", lambda v: v.tensor_add(T["x2t"].ap()[:, :],
            T["toktmp"].ap()[:, 0:HID], T["xtile"].ap()[:, :]),
          reads=["toktmp", "xtile"], writes=["x2t"])
        dma("sync", ap("x2_d")[xoff+128*t:xoff+128*(t+1), :], ap("x2t")[:, :],
            "d_x2o", reads=["x2t"], writes=[f"x2d_{b}_{t}"])

    if b == 0:
        for lane, col, nm in ((0, 7, "lnw1col"), (1, 15, "lnw2col")):
            dma("sync", ap("statc")[:, 14:15],
                ap("row_d")[lane:lane+1, 0:1].partition_broadcast(128), "d_m5",
                reads=[f"rdlane{lane}"], writes=["st14"])
            E("scalar", lambda s: s.activation(T["statc"].ap()[:, 14:15],
                T["statc"].ap()[:, 14:15], AF.Ln), reads=["st14"], writes=["st14"])
            E("vector", (lambda col=col, nm=nm: (lambda v: v.tensor_copy(
                T["statc"].ap()[:, col:col+1], T["statc"].ap()[:, 14:15])))(),
              reads=["st14"], writes=[nm])
        E("vector", lambda v: v.tensor_scalar(T["statc"].ap()[:, 16:17],
            T["scal"].ap()[:, 21:22], -127.0, 2.0, ALU.mult, ALU.add),
          reads=["scal"], writes=["r127t"])
        E("vector", lambda v: v.tensor_mul(T["statc"].ap()[:, 16:17],
            T["statc"].ap()[:, 16:17], T["scal"].ap()[:, 21:22]),
          reads=["r127t", "scal"], writes=["r127"])

    for t in range(nt):
        dma("sync", ap("x2t")[:, :], ap("x2_d")[xoff+128*t:xoff+128*(t+1), :],
            "d_x2i", reads=[f"x2d_{b}_{t}"], writes=["x2t"])
        E("vector", lambda v: v.tensor_copy(T["fq1"].ap()[:, 0:HID],
            T["x2t"].ap()[:, :]), reads=["x2t"], writes=["fq1"])
        ln_tok_chain("fq1", (lambda: T["fq1"].ap()[:, 0:HID]), 1e-5,
                     "toktmp", (lambda: T["toktmp"].ap()[:, 0:HID]), HID)
        E("scalar", lambda s: s.activation(T["sqscr"].ap()[:, 0:HID],
            T["toktmp"].ap()[:, 0:HID], AF.Square, accum_out=T["statc"].ap()[:, 4:5]),
          reads=["toktmp"], writes=["sqscr", "st4"])
        E("scalar", (lambda H=HID: (lambda s: s.activation(T["statc"].ap()[:, 4:5],
            T["statc"].ap()[:, 4:5], AF.Ln, bias=1e-6, scale=1.0 / H)))(),
          reads=["st4"], writes=["st4"])
        E("scalar", lambda s: s.activation(T["statc"].ap()[:, 4:5],
            T["statc"].ap()[:, 4:5], AF.Exp, scale=-0.5), reads=["st4"], writes=["st4"])
        E("vector", lambda v: v.reduce_max(T["statc"].ap()[:, 5:6],
            T["toktmp"].ap()[:, 0:HID], AX.X, apply_absolute_value=True),
          reads=["toktmp"], writes=["st5"])
        E("vector", lambda v: v.tensor_mul(T["statc"].ap()[:, 5:6],
            T["statc"].ap()[:, 4:5], T["statc"].ap()[:, 5:6]),
          reads=["st4", "st5"], writes=["st5"])
        E("vector", lambda v: v.tensor_scalar_max(T["statc"].ap()[:, 5:6],
            T["statc"].ap()[:, 5:6], 1e-5), reads=["st5"], writes=["st5"])
        E("scalar", lambda s: s.activation(T["statc"].ap()[:, 6:7],
            T["statc"].ap()[:, 5:6], AF.Ln), reads=["st5"], writes=["st6"])
        E("scalar", lambda s: s.activation(T["statc"].ap()[:, 5:6],
            T["statc"].ap()[:, 6:7], AF.Exp, scale=-1.0),
          reads=["st6"], writes=["st5"])
        E("vector", lambda v: v.tensor_mul(T["statc"].ap()[:, 8:9],
            T["statc"].ap()[:, 4:5], T["statc"].ap()[:, 5:6]),
          reads=["st4", "st5"], writes=["st8"])
        E("vector", lambda v: v.tensor_scalar(T["statc"].ap()[:, 8:9],
            T["statc"].ap()[:, 8:9], 127.0, None, ALU.mult),
          reads=["st8"], writes=["st8"])
        E("vector", lambda v: v.tensor_scalar(T["fq1"].ap()[:, 0:HID],
            T["toktmp"].ap()[:, 0:HID], T["statc"].ap()[:, 8:9],
            T["Ccol"].ap()[:, 0:1], ALU.mult, ALU.add),
          reads=["toktmp", "st8", "Ccol"], writes=["fq1"])
        E("vector", lambda v: v.tensor_add(T["statc"].ap()[:, 9:10],
            T["statc"].ap()[:, 7:8], T["statc"].ap()[:, 6:7]),
          reads=["lnw1col", "st6"], writes=["st9"])
        E("scalar", lambda s: s.activation(T["statc"].ap()[:, 9:10],
            T["statc"].ap()[:, 9:10], AF.Exp), reads=["st9"], writes=["st9"])
        E("vector", lambda v: v.tensor_mul(T["statc"].ap()[:, 9:10],
            T["statc"].ap()[:, 9:10], T["statc"].ap()[:, 16:17]),
          reads=["st9", "r127"], writes=["st9"])
        E("vector", lambda v: v.tensor_scalar(T["toktmp"].ap()[:, 0:HID],
            T["fq1"].ap()[:, 0:HID], T["Ccol"].ap()[:, 0:1],
            T["statc"].ap()[:, 9:10], ALU.subtract, ALU.mult),
          reads=["fq1", "Ccol", "st9"], writes=["toktmp"])
        for c in range(nhc):
            E("tensor", (lambda t=t, c=c: (lambda te: te.matmul(
                PS[0:128, 2048+128*c:2048+128*(c+1)],
                T["toktmp"].ap()[:, 128*c:128*(c+1)], T["ident"].ap()[:, :],
                start=True, stop=True)))(), reads=["toktmp", "ident"],
              writes=psnames(2048+128*c, 2048+128*(c+1)))
        E("vector", (lambda t=t: (lambda v: v.tensor_copy(
            T["KF"].ap().rearrange("p (c s) -> p c s", s=S)[:, :, 128*t:128*(t+1)],
            PS[0:128, 2048:2048+128*nhc].rearrange("p (c s) -> p c s", s=128))))(),
          reads=psnames(2048, 2048+128*nhc), writes=["KF"])

    for sp in range(nst):
        tg0 = SW * sp
        for ocg in range((nfc + 3) // 4):
            ocs = list(range(4 * ocg, min(4 * ocg + 4, nfc)))
            for i in range(nhc):
                wsn = "W1s" if i % 2 == 0 else "W2s"
                dma("sync", ap(wsn)[:, 0:128*len(ocs)],
                    ap("w1i_d")[128*i:128*(i+1), 128*ocs[0]:128*ocs[0]+128*len(ocs)],
                    "d_w1", writes=[wsn])
                for j, oc in enumerate(ocs):
                    E("tensor", (lambda i=i, j=j, tg0=tg0, wsn=wsn: (lambda te: te.matmul(
                        PS[0:128, 2048+512*j:2048+512*j+SW],
                        T[wsn].ap()[:, 128*j:128*(j+1)],
                        T["KF"].ap()[:, S*i+tg0:S*i+tg0+SW],
                        start=(i == 0), stop=(i == nhc-1))))(),
                      reads=[wsn, "KF"], writes=[f"psB{4+j}"])
            for j, oc in enumerate(ocs):
                E("scalar", (lambda j=j, oc=oc: (lambda s: s.activation(
                    T["QF"].ap()[:, SW*oc:SW*oc+SW], PS[0:128, 2048+512*j:2048+512*j+SW],
                    AF.Gelu, bias=T["b1_s"].ap()[:, oc:oc+1])))(),
                  reads=[f"psB{4+j}", "b1_s"], writes=["QF"])
        for oc in range(nfc):
            E("scalar", (lambda oc=oc: (lambda s: s.activation(
                T["sqscr"].ap()[:, 0:SW], T["QF"].ap()[:, SW*oc:SW*oc+SW],
                AF.Square)))(), reads=["QF"], writes=["sqscr"])
            E("tensor", (lambda oc=oc: (lambda te: te.matmul(
                PS[0:1, 3584:3584+SW], T["ones_colb"].ap()[:, 0:1],
                T["sqscr"].ap()[:, 0:SW], start=(oc == 0), stop=(oc == nfc-1))))(),
              reads=["sqscr", "ones_colb"], writes=["psB7"])
            if oc == 0:
                E("vector", lambda v: v.tensor_copy(
                    T["sqscr"].ap()[:, 512:512+SW], T["sqscr"].ap()[:, 0:SW]),
                  reads=["sqscr"], writes=["sqmax"])
            else:
                E("vector", lambda v: v.tensor_max(
                    T["sqscr"].ap()[:, 512:512+SW], T["sqscr"].ap()[:, 512:512+SW],
                    T["sqscr"].ap()[:, 0:SW]), reads=["sqscr", "sqmax"],
                  writes=["sqmax"])
        nb = SW // 128
        E("scalar", lambda s: s.activation(T["rowf"].ap()[0:1, 0:SW],
            PS[0:1, 3584:3584+SW], AF.Copy), reads=["psB7"], writes=["rowf"])
        for bq in range(nb):
            E("tensor", (lambda bq=bq: (lambda te: te.matmul(
                PS[0:128, 3648+bq:3648+bq+1],
                T["rowf"].ap()[0:1, 128*bq:128*(bq+1)],
                T["scal"].ap()[0:1, 2:3], start=True, stop=True)))(),
              reads=["rowf"], writes=["psB7"])
            E("tensor", (lambda bq=bq: (lambda te: te.matmul(
                PS[0:128, 2048+128*bq:2048+128*(bq+1)],
                T["sqscr"].ap()[:, 512+128*bq:512+128*(bq+1)], T["ident"].ap()[:, :],
                start=True, stop=True)))(), reads=["sqmax", "ident"],
              writes=["psB4"])
            E("vector", (lambda bq=bq: (lambda v: v.tensor_reduce(
                T["hcol"].ap()[:, bq:bq+1], PS[0:128, 2048+128*bq:2048+128*(bq+1)],
                AX.X, ALU.max)))(), reads=["psB4"], writes=["hcol_am"])
        E("vector", (lambda nb=nb: (lambda v: v.tensor_copy(
            T["hcol"].ap()[:, 4:4+nb], PS[0:128, 3648:3648+nb])))(),
          reads=["psB7"], writes=["hcol_sq"])
        E("scalar", (lambda FF=FF, nb=nb: (lambda s: s.activation(T["hcol"].ap()[:, 4:4+nb],
            T["hcol"].ap()[:, 4:4+nb], AF.Ln, bias=1e-6, scale=1.0 / FF)))(),
          reads=["hcol_sq"], writes=["hcol_sq"])
        E("scalar", (lambda nb=nb: (lambda s: s.activation(T["hcol"].ap()[:, 0:nb],
            T["hcol"].ap()[:, 0:nb], AF.Ln)))(), reads=["hcol_am"], writes=["hcol_am"])
        E("vector", (lambda nb=nb: (lambda v: v.tensor_tensor(T["hcol"].ap()[:, 0:nb],
            T["hcol"].ap()[:, 0:nb], T["hcol"].ap()[:, 4:4+nb], ALU.subtract)))(),
          reads=["hcol_am", "hcol_sq"], writes=["hcol_am"])
        E("vector", (lambda nb=nb: (lambda v: v.tensor_scalar_max(
            T["hcol"].ap()[:, 0:nb], T["hcol"].ap()[:, 0:nb], -23.025850929940457)))(),
          reads=["hcol_am"], writes=["hcol_am"])
        E("scalar", (lambda nb=nb: (lambda s: s.activation(T["hcol"].ap()[:, 4:4+nb],
            T["hcol"].ap()[:, 4:4+nb], AF.Exp, scale=-0.5)))(),
          reads=["hcol_sq"], writes=["hcol_sq"])
        E("scalar", (lambda nb=nb: (lambda s: s.activation(
            T["hcol"].ap()[:, 8:8+nb], T["hcol"].ap()[:, 0:nb],
            AF.Exp, scale=0.5, bias=T["statc"].ap()[:, 15:16])))(),
          reads=["hcol_am", "lnw2col"], writes=["hcol_cs"])
        E("vector", (lambda nb=nb: (lambda v: v.tensor_scalar(
            T["hcol"].ap()[:, 8:8+nb], T["hcol"].ap()[:, 8:8+nb],
            T["statc"].ap()[:, 16:17], None, ALU.mult)))(),
          reads=["hcol_cs", "r127"], writes=["hcol_cs"])
        E("scalar", (lambda nb=nb: (lambda s: s.activation(
            T["hcol"].ap()[:, 12:12+nb], T["hcol"].ap()[:, 0:nb],
            AF.Exp, scale=-0.5)))(),
          reads=["hcol_am"], writes=["hcol_hs"])
        E("vector", (lambda nb=nb: (lambda v: v.tensor_mul(
            T["hcol"].ap()[:, 12:12+nb], T["hcol"].ap()[:, 12:12+nb],
            T["hcol"].ap()[:, 4:4+nb])))(),
          reads=["hcol_hs", "hcol_sq"], writes=["hcol_hs"])
        E("vector", (lambda nb=nb: (lambda v: v.tensor_scalar(
            T["hcol"].ap()[:, 12:12+nb], T["hcol"].ap()[:, 12:12+nb],
            127.0, None, ALU.mult)))(), reads=["hcol_hs"], writes=["hcol_hs"])
        for c in range(nb):
            dma("sync", ap("row_d")[2:3, 128*c:128*(c+1)], ap("hcol")[:, 12+c:13+c],
                "d_h1", reads=["hcol_hs"], writes=[f"rdlane2_{c}"])
            dma("sync", ap("row_d")[3:4, 128*c:128*(c+1)], ap("hcol")[:, 8+c:9+c],
                "d_h3", reads=["hcol_cs"], writes=[f"rdlane3_{c}"])
        dma("sync", ap("fq1")[:, 0:SW],
            ap("row_d")[2:3, 0:SW].partition_broadcast(128), "d_h2",
            reads=[f"rdlane2_{c}" for c in range(nb)], writes=["fq1"])
        dma("gpsimd", ap("zbt")[:, 0:SW],
            ap("row_d")[3:4, 0:SW].partition_broadcast(128), "d_h4",
            reads=[f"rdlane3_{c}" for c in range(nb)], writes=["zbt"])
        for oc in range(nfc):
            E("vector", (lambda oc=oc: (lambda v: v.scalar_tensor_tensor(
                T["fq2"].ap()[:, 0:SW], T["QF"].ap()[:, SW*oc:SW*oc+SW],
                T["Ccol"].ap()[:, 0:1], T["fq1"].ap()[:, 0:SW],
                ALU.bypass, ALU.mult)))(),
              reads=["QF", "fq1"], writes=["fq2"])
            E("vector", (lambda oc=oc: (lambda v: v.tensor_scalar(
                T["fq2"].ap()[:, 0:SW], T["fq2"].ap()[:, 0:SW],
                T["Ccol"].ap()[:, 0:1], None, ALU.add)))(),
              reads=["fq2", "Ccol"], writes=["fq2"])
            E("vector", (lambda oc=oc: (lambda v: v.tensor_scalar(
                T["VF"].ap()[:, SW*oc:SW*oc+SW], T["fq2"].ap()[:, 0:SW],
                T["Ccol"].ap()[:, 0:1], None, ALU.subtract)))(),
              reads=["fq2", "Ccol"], writes=["VF"])
        for i2g in range((nhc + 3) // 4):
            i2s = list(range(4 * i2g, min(4 * i2g + 4, nhc)))
            for o in range(nfc):
                wsn = "W2s" if o % 2 == 0 else "W1s"
                dma("sync", ap(wsn)[:, 0:128*len(i2s)],
                    ap("w2i_d")[128*o:128*(o+1), 128*i2s[0]:128*i2s[0]+128*len(i2s)],
                    "d_w2", writes=[wsn])
                for j, i2 in enumerate(i2s):
                    E("tensor", (lambda o=o, j=j, tg0=tg0, wsn=wsn: (lambda te: te.matmul(
                        PS[0:128, 2048+512*j:2048+512*j+SW],
                        T[wsn].ap()[:, 128*j:128*(j+1)],
                        T["VF"].ap()[:, SW*o:SW*o+SW],
                        start=(o == 0), stop=(o == nfc-1))))(),
                      reads=[wsn, "VF"], writes=[f"psB{4+j}"])
            for j, i2 in enumerate(i2s):
                E("vector", (lambda j=j: (lambda v: v.tensor_tensor(
                    T["fq2"].ap()[:, 0:SW], PS[0:128, 2048+512*j:2048+512*j+SW],
                    T["zbt"].ap()[0:128, 0:SW], ALU.mult)))(),
                  reads=[f"psB{4+j}", "zbt"], writes=["fq2"])
                E("vector", (lambda i2=i2: (lambda v: v.tensor_scalar_add(
                    T["XN1"].ap()[:, SW*i2:SW*i2+SW], T["fq2"].ap()[:, 0:SW],
                    T["b2_s"].ap()[:, i2:i2+1])))(),
                  reads=["fq2", "b2_s"], writes=["XN1"])
        for tt in range(SW // 128):
            t = (tg0 // 128) + tt
            for c in range(nhc):
                E("tensor", (lambda tt=tt, c=c: (lambda te: te.matmul(
                    PS[0:128, 2048+128*c:2048+128*(c+1)],
                    T["XN1"].ap()[:, SW*c+128*tt:SW*c+128*(tt+1)], T["ident"].ap()[:, :],
                    start=True, stop=True)))(), reads=["XN1", "ident"],
                  writes=psnames(2048+128*c, 2048+128*(c+1)))
            E("vector", lambda v: v.tensor_copy(T["fq1"].ap()[:, 0:HID],
                PS[0:128, 2048:2048+HID]), reads=psnames(2048, 2048+HID), writes=["fq1"])
            ln_tok_chain("fq1", (lambda: T["fq1"].ap()[:, 0:HID]), 1e-5,
                         "toktmp", (lambda: T["toktmp"].ap()[:, 0:HID]), HID)
            dma("sync", ap("x2t")[:, :], ap("x2_d")[xoff+128*t:xoff+128*(t+1), :],
                "d_x3", reads=[f"x2d_{b}_{t}"], writes=["x2t"])
            E("vector", lambda v: v.tensor_add(T["xtile"].ap()[:, :],
                T["toktmp"].ap()[:, 0:HID], T["x2t"].ap()[:, :]),
              reads=["toktmp", "x2t"], writes=["xtile"])
            dma("sync", ap("y")[xoff+128*t:xoff+128*(t+1), :], ap("xtile")[:, :],
                "d_out", reads=["xtile"], writes=["yout"])


def emit_block(k):
    import contextlib
    plan = k.plan
    nc = k.nc
    import os
    steps = plan.finalize(keep_all=bool(os.environ.get("KEEP_ALL_SEMS")))
    with contextlib.ExitStack() as st:
        sems = {}
        for name in plan.cnt:
            sems[name] = st.enter_context(nc.semaphore(name))
        with nc.Block() as block:
            def mk(eng_name):
                def body(eng):
                    for s_ in steps[eng_name]:
                        if s_[0] == "wait":
                            eng.wait_ge(sems[s_[1]], s_[2])
                        else:
                            ins = s_[1](eng)
                            if s_[2] is not None:
                                ins.then_inc(sems[s_[2]], 16 if plan.is_dma[s_[2]] else 1)
                return body
            block.sync(mk("sync"))
            block.gpsimd(mk("gpsimd"))
            block.tensor(mk("tensor"))
            block.vector(mk("vector"))
            block.scalar(mk("scalar"))
    return k.nc


def host_inputs(cfg, ins, core):
    S = cfg["S"]; NH = cfg["NH"]; HID = 64 * NH; FF = cfg["FF"]; BPC = cfg["BPC"]
    nt = S // 128; nhc = HID // 128; nfc = FF // 128
    f32 = np.float32
    d = {}
    x = np.asarray(ins["L0_attn_x"], f32)
    d["x"] = np.ascontiguousarray(x[BPC*core:BPC*(core+1)].reshape(BPC*S, HID))
    for n, nm in (("Wq", "wq"), ("Wk", "wk"), ("Wv", "wv"), ("Wo", "wo")):
        d[nm] = np.ascontiguousarray(np.asarray(ins[f"L0_attn_{n}"], f32))
    for n, nm, w in (("bq", "bq2", nhc), ("bk", "bk2", nhc), ("bv", "bv2", nhc),
                     ("bo", "bo2", nhc)):
        d[nm] = np.ascontiguousarray(np.asarray(ins[f"L0_attn_{n}"], f32).reshape(w, 128).T)
    d["b12"] = np.ascontiguousarray(np.asarray(ins["L0_ffn_b1"], f32).reshape(nfc, 128).T)
    d["b22"] = np.ascontiguousarray(np.asarray(ins["L0_ffn_b2"], f32).reshape(nhc, 128).T)
    tk = np.asarray(ins["L0_attn_table_k"], f32)
    tv = np.asarray(ins["L0_attn_table_v"], f32)
    tkp = np.zeros((64, 65), f32)
    tkp[:, 0:NJ] = tk.T
    tkp[:, 32] = tk.T[:, 2 * MAXREL]
    d["tkT2"] = np.ascontiguousarray(np.concatenate([tkp, tkp], 0))
    d["tv"] = np.ascontiguousarray(tv)
    d["tv08"] = np.ascontiguousarray(tv[[0, 2 * MAXREL], :])
    d["w1T"] = np.ascontiguousarray(np.asarray(ins["L0_ffn_w1"], f32).T)
    d["w2T"] = np.ascontiguousarray(np.asarray(ins["L0_ffn_w2"], f32).T)
    d["c_ident"] = np.eye(128, dtype=f32).astype(bfnp)
    d["c_identd"] = np.tile(np.eye(64, dtype=f32), (2, 1)).astype(bfnp)
    coef = make_coefs(a2x_mode=True)
    d["c_coef"] = np.ascontiguousarray(coef.transpose(1, 0, 2).reshape(128, NJ * WWIN)).astype(bfnp)
    coefw = make_coefs(a2x_mode=False)
    d["c_coefw"] = np.ascontiguousarray(coefw.transpose(1, 0, 2).reshape(128, NJ * WWIN)).astype(bfnp)
    d["c_ones_row"] = np.ones((1, 128), f32).astype(bfnp)
    d["c_ones_colb"] = np.ones((128, 1), f32).astype(bfnp)
    d["c_ones_colf"] = np.ones((128, 1), f32)
    sel = np.zeros((2, nt * nt), f32)
    for s in range(nt):
        for c in range(nt):
            if c < s:
                sel[0, s * nt + c] = 1.0
            elif c > s:
                sel[1, s * nt + c] = 1.0
    d["c_sel"] = sel.astype(bfnp)
    vind = np.zeros((128, nt * 33), f32)
    for c in range(nt):
        vind[:, c * 33 + c] = 1.0
        vind[:, c * 33 + 32] = 1.0
    d["c_vind"] = vind.astype(bfnp)
    d["c_scal"] = np.ascontiguousarray(
        np.tile(np.array(SCAL_VALS(HID, FF), f32)[None, :], (128, 1)))
    d["c_p52"] = np.full((128, 1), 2.0 ** 52, f32)
    return d


_CFG_FULL = {"S": 2048, "NH": 16, "FF": 4096, "BPC": 2}


def run_full(ins, ncores=8, cfg=None):
    from concourse.bass_utils import run_bass_kernel_spmd
    cfg = cfg or _CFG_FULL
    k = build_program(cfg)
    nc = emit_block(k)
    in_maps = [host_inputs(cfg, ins, c) for c in range(ncores)]
    res = run_bass_kernel_spmd(nc, in_maps, core_ids=list(range(ncores)))
    BPC, S, HID = cfg["BPC"], cfg["S"], 64 * cfg["NH"]
    out = np.concatenate([r["y"].reshape(BPC, S, HID) for r in res.results], axis=0)
    return out


NCORES = 8


def kernel(**inputs):
    from concourse.bass_utils import run_bass_kernel_spmd
    cfg = _CFG_FULL
    k = build_program(cfg)
    nc = emit_block(k)
    in_maps = [host_inputs(cfg, inputs, c) for c in range(NCORES)]
    res = run_bass_kernel_spmd(nc, in_maps, core_ids=list(range(NCORES)))
    BPC, S, HID = cfg["BPC"], cfg["S"], 64 * cfg["NH"]
    out = np.concatenate([r["y"].reshape(BPC, S, HID) for r in res.results], axis=0)
    return out


# version 17
# speedup vs baseline: 1.1548x; 1.1548x over previous
import numpy as np
import ml_dtypes

import concourse.bass as bass
import concourse.mybir as mybir

F32 = mybir.dt.float32
BF16 = mybir.dt.bfloat16
ALU = mybir.AluOpType
AF = mybir.ActivationFunctionType
AX = mybir.AxisListType
bfnp = ml_dtypes.bfloat16

MAXREL = 4
NJ = 9
LPAD, RPAD = 64, 4
WWIN = LPAD + 128 + RPAD

ENGINES = ("sync", "gpsimd", "tensor", "vector", "scalar")


def SCAL_VALS(HID, FF=4096):
    import math
    ln127 = math.log(127.0)
    return [0.0, 0.5, 1.0, -1.0, -0.125, 1e-5, 1.0 / HID, 1e-6,
            ln127, -ln127, 0.125, -0.5, 1.0 / FF, 1.0 / (HID * FF), 8388608.0, 536870912.0,
            805306368.0, 1.5, 12582912.0, 0.25, 127.0, 1.0 / 127.0]


def make_coefs(a2x_mode=False):
    coef = np.zeros((NJ, 128, WWIN), np.float32)
    p = np.arange(128)[:, None]
    kw = np.arange(WWIN)[None, :] - LPAD
    jj = np.clip(kw - p, -MAXREL, MAXREL) + MAXREL
    main = (kw >= 0) & (kw < 128)
    left = kw < 0
    right = kw >= 128
    for j in range(NJ):
        coef[j][main & (jj == j)] += 1.0
        if 1 <= j <= 3:
            coef[j][left & (jj == j)] += 1.0
            coef[0][left & (jj == j)] -= 1.0
        if 5 <= j <= 7:
            coef[j][right & (jj == j)] += 1.0
            coef[8][right & (jj == j)] -= 1.0
    if a2x_mode:
        coef[8] -= np.broadcast_to(main, (128, WWIN)).astype(np.float32)
    return coef


class Plan:

    def __init__(self):
        self.steps = {e: [] for e in ENGINES}
        self.cnt = {}
        self.is_dma = {}
        self.last_write = {}
        self.readers = {}
        self.waited = {e: {} for e in ENGINES}
        self.eng_sem = {e: f"S_{e}" for e in ENGINES}
        self.dma_gen = {}
        self.sem_prev = {}
        for e in ENGINES:
            self.cnt[self.eng_sem[e]] = 0
            self.is_dma[self.eng_sem[e]] = False

    def dma_sem(self, name):
        if name not in self.cnt:
            self.cnt[name] = 0
            self.is_dma[name] = True
        return name

    def _wait(self, eng, sem, v):
        if v <= 0:
            return
        if self.waited[eng].get(sem, 0) >= v:
            return
        self.waited[eng][sem] = v
        self.steps[eng].append(("wait", sem, v))

    def emit(self, eng, fn, reads=(), writes=(), dma=None):
        for b in reads:
            if b in self.last_write:
                self._wait(eng, *self.last_write[b])
        for b in writes:
            if b in self.last_write:
                self._wait(eng, *self.last_write[b])
            for rs in self.readers.get(b, ()):
                self._wait(eng, *rs)
        if dma is not None:
            gen = self.dma_gen.setdefault(dma, 0)
            self.dma_gen[dma] = gen + 1
            sem = self.dma_sem(f"{dma}.{gen % 3}")
            if self.cnt[sem] > 0:
                self._wait(eng, sem, self.cnt[sem])
            self.cnt[sem] += 16
        else:
            sem = self.eng_sem[eng]
            self.cnt[sem] += 1
        v = self.cnt[sem]
        self.steps[eng].append(("op", fn, sem, v))
        for b in reads:
            self.readers.setdefault(b, []).append((sem, v))
        for b in writes:
            self.last_write[b] = (sem, v)
            self.readers[b] = []
        return (sem, v)

    def barrier_read(self, eng, bufs):
        for b in bufs:
            if b in self.last_write:
                self._wait(eng, *self.last_write[b])

    def drain_all_dma(self, eng):
        for sem, n in self.cnt.items():
            if self.is_dma[sem] and n > 0:
                self._wait(eng, sem, n)

    def finalize(self, keep_all=False):
        if keep_all:
            out = {}
            for e in ENGINES:
                lst = []
                for st in self.steps[e]:
                    if st[0] == "wait":
                        lst.append(st)
                    else:
                        lst.append(("op", st[1], st[2], 0))
                out[e] = lst
            return out
        needed = {}
        for e in ENGINES:
            for st in self.steps[e]:
                if st[0] == "wait":
                    needed.setdefault(st[1], set()).add(st[2])
        remap = {}
        for sem, vals in needed.items():
            sv = sorted(vals)
            remap[sem] = {v: i + 1 for i, v in enumerate(sv)}
        out = {}
        for e in ENGINES:
            lst = []
            for st in self.steps[e]:
                if st[0] == "wait":
                    sem, v = st[1], st[2]
                    if self.is_dma[sem]:
                        lst.append(("wait", sem, v))
                    else:
                        lst.append(("wait", sem, remap[sem][v]))
                else:
                    _, fn, sem, v = st
                    keep = self.is_dma[sem] or (sem in remap and v in remap[sem])
                    lst.append(("op", fn, sem if keep else None, 0))
            out[e] = lst
        return out


def build_consts(cfg):
    c = {}
    c["ident"] = np.eye(128, dtype=np.float32).astype(bfnp)
    coef = make_coefs()
    c["coef"] = np.ascontiguousarray(
        coef.transpose(1, 0, 2).reshape(128, NJ * WWIN)).astype(bfnp)
    c["ones_row"] = np.ones((1, 128), np.float32).astype(bfnp)
    c["ones_col"] = np.ones((128, 1), np.float32).astype(bfnp)
    c["ones_col_f32"] = np.ones((128, 1), np.float32)
    nt = cfg["S"] // 128
    sel = np.zeros((2, nt * nt), np.float32)
    for s in range(nt):
        for cc in range(nt):
            if cc < s:
                sel[0, s * nt + cc] = 1.0
            elif cc > s:
                sel[1, s * nt + cc] = 1.0
    c["sel"] = sel.astype(bfnp)
    return c


class K:

    def __init__(self, cfg):
        self.cfg = cfg
        self.nc = bass.Bass("TRN2", target_bir_lowering=False)
        self.plan = Plan()
        self.T = {}
        self.uid = 0

    def u(self, p="t"):
        self.uid += 1
        return f"{p}{self.uid}"

    def dram_in(self, name, shape, dt=F32):
        h = self.nc.dram_tensor(name, list(shape), dt, kind="ExternalInput")
        self.T[name] = h
        return h

    def dram_out(self, name, shape, dt=F32):
        h = self.nc.dram_tensor(name, list(shape), dt, kind="ExternalOutput")
        self.T[name] = h
        return h

    def dram_tmp(self, name, shape, dt=F32):
        h = self.nc.dram_tensor(name, list(shape), dt, kind="Internal")
        self.T[name] = h
        return h

    def sbuf(self, name, shape, dt=BF16):
        h = self.nc.alloc_sbuf_tensor(name, list(shape), dt)
        self.T[name] = h
        return h

    def psum(self, name, shape, dt=F32):
        h = self.nc.alloc_psum_tensor(name, list(shape), dt)
        self.T[name] = h
        return h




def build_program(cfg):
    S = cfg["S"]; NH = cfg["NH"]; HID = 64 * NH; FF = cfg["FF"]; BPC = cfg["BPC"]
    nt = S // 128; nhc = HID // 128; nfc = FF // 128
    NTOK = BPC * S
    nst = S // 512 if S >= 512 else 1
    SW = min(512, S)
    k = K(cfg)
    nc, plan, T = k.nc, k.plan, k.T
    ap = lambda n: T[n].ap()
    E = plan.emit

    k.dram_in("x", [NTOK, HID])
    for n in ("wq", "wk", "wv", "wo"):
        k.dram_in(n, [HID, HID])
    for n, w in (("bq2", nhc), ("bk2", nhc), ("bv2", nhc), ("bo2", nhc),
                 ("b12", nfc), ("b22", nhc)):
        k.dram_in(n, [128, w])
    k.dram_in("tkT2", [128, 65]); k.dram_in("tv", [NJ, 64]); k.dram_in("tv08", [2, 64])
    k.dram_in("w1T", [HID, FF]); k.dram_in("w2T", [FF, HID])
    k.dram_in("c_ident", [128, 128], BF16)
    k.dram_in("c_identd", [128, 64], BF16)
    k.dram_in("c_coef", [128, NJ * WWIN], BF16)
    k.dram_in("c_coefw", [128, NJ * WWIN], BF16)
    k.dram_in("c_ones_row", [1, 128], BF16)
    k.dram_in("c_ones_colb", [128, 1], BF16)
    k.dram_in("c_ones_colf", [128, 1], F32)
    k.dram_in("c_sel", [2, nt * nt], BF16)
    k.dram_in("c_vind", [128, nt * 33], BF16)
    k.dram_in("c_scal", [128, 22], F32)
    k.dram_in("c_p52", [128, 1], F32)
    k.dram_out("y", [NTOK, HID])
    k.dram_tmp("w1i_d", [HID, FF], BF16)
    k.dram_tmp("w2i_d", [FF, HID], BF16)
    k.dram_tmp("row_d", [8, max(S, 2048)], F32)
    k.dram_tmp("x2_d", [NTOK, HID], BF16)

    sb = k.sbuf
    sb("ident", [128, 128]); sb("identd", [128, 64])
    sb("coef", [128, NJ * WWIN]); sb("coefw", [128, NJ * WWIN]); sb("ones_row", [1, 128])
    sb("ones_colb", [128, 1]); sb("sel", [2, nt * nt]); sb("vind", [128, nt * 33])
    sb("tvsel", [16, nt * 64])
    sb("ones_colf", [128, 1], F32)
    sb("scal", [128, 22], F32)
    sb("p52", [128, 1], F32)
    sb("Ccol", [128, 1], F32)
    sb("tkT2_b", [128, 65])
    sb("KQ0a", [128, 128]); sb("KQ1a", [128, 128])
    sb("KQ0b", [128, 128]); sb("KQ1b", [128, 128])
    sb("s08row", [1, S]); sb("tv_b", [NJ, 64]); sb("tv08_b", [2, 64])
    for n, w in (("bq_s", nhc), ("bk_s", nhc), ("bv_s", nhc), ("bo_s", nhc),
                 ("b1_s", nfc), ("b2_s", nhc)):
        sb(n, [128, w], F32)
    sb("Wc0", [128, 128]); sb("Wc1", [128, 128])
    sb("XN1", [128, nhc * S])
    qvsz = max(nhc * S, nfc * min(512, S))
    sb("QF", [128, qvsz]); sb("KF", [128, nhc * S]); sb("VF", [128, qvsz])
    for r in range(3):
        sb(f"F{r}", [128, S])
    for r in range(4):
        sb(f"A2X{r}", [128, WWIN])
    VW = 64 + 32 + 1
    sb("vpm", [128, nt * VW])
    sb("SattnT", [NJ, S])
    sb("Sqp", [128, nt * NJ], F32)
    sb("FBqp", [128, nt * NJ], F32)
    sb("FBT", [NJ, S])
    sb("Rrows", [16, S])
    sb("zbt", [128, S])
    sb("negM", [128, 16], F32)
    sb("ttr_scr", [128, WWIN], F32)
    sb("vj", [128, 8], F32)
    sb("slotD0", [128, WWIN]); sb("slotD1", [128, WWIN])
    sb("fwv0", [128, WWIN]); sb("fwv1", [128, WWIN])
    sb("rowf", [1, 512], F32)
    sb("x2t", [128, HID])
    sb("xtile", [128, HID], F32)
    sb("toktmp", [128, max(HID, 1024)])
    sb("sqscr", [128, max(HID, 1024)])
    sb("statc", [128, 18], F32)
    sb("maxs", [128, 512], F32)
    sb("hcol", [128, 16], F32)
    sb("W1s", [128, 512])
    sb("W2s", [128, 512])
    sb("fq1", [128, max(HID, 1024)], F32)
    sb("fq2", [128, 512], F32); sb("x2t2", [128, 512], F32)
    k.psum("PS", [128, 4096], F32)
    PS = T["PS"].ap()

    def dma(q, dst, src, sem, reads=(), writes=()):
        E(q, (lambda d=dst, s=src: (lambda eng: eng.dma_start(out=d, in_=s)))(),
          reads=reads, writes=writes, dma=sem)

    for name, dst in (("c_ident", "ident"), ("c_identd", "identd"), ("c_coef", "coef"), ("c_coefw", "coefw"),
                      ("c_ones_row", "ones_row"), ("c_ones_colb", "ones_colb"),
                      ("c_sel", "sel"), ("c_vind", "vind")):
        dma("sync", ap(dst)[:, :], ap(name)[:, :], "d_c", writes=[dst])
    dma("sync", ap("ones_colf")[:, :], ap("c_ones_colf")[:, :], "d_c", writes=["ones_colf"])
    dma("sync", ap("scal")[:, :], ap("c_scal")[:, :], "d_c", writes=["scal"])
    for _i in range(2):
        k.dram_tmp(f"a2xd{_i}", [1, 128 * (WWIN + 2)], BF16)
        k.dram_tmp(f"fwd{_i}", [1, 128 * (WWIN + 1)], BF16)
    plan.emit("vector", lambda v: v.memset(T["slotD0"].ap()[:, :], 0.0),
              writes=["slotD0"])
    for _i in range(2):
        _v198 = ap(f"a2xd{_i}")[0:1, :].rearrange("o (p w) -> (o p) w", w=WWIN + 2)
        dma("sync", _v198[:, 0:WWIN], T["slotD0"].ap()[:, 0:WWIN],
            "d_az", reads=["slotD0"], writes=[f"a2xz{_i}"])
        dma("sync", _v198[:, WWIN:WWIN + 2], T["slotD0"].ap()[:, 0:2],
            "d_az", reads=["slotD0"], writes=[f"a2xz{_i}b"])
    for _nm in ("KQ0a", "KQ1a", "KQ0b", "KQ1b"):
        plan.emit("vector", (lambda _nm=_nm: (lambda v: v.memset(T[_nm].ap()[:, :], 0.0)))(),
                  writes=[f"{_nm}_z"])
    dma("sync", ap("p52")[:, :], ap("c_p52")[:, :], "d_c", writes=["p52"])
    for ci, cv in enumerate(SCAL_VALS(HID, FF)):
        nc.const_aps.register(F32, cv, T["scal"].ap()[:, ci:ci+1])
    E("vector", lambda v: v.tensor_scalar_add(T["statc"].ap()[:, 0:1],
        T["p52"].ap()[:, 0:1], 8388608.0), reads=["p52"], writes=["st0"])
    E("vector", lambda v: v.tensor_sub(T["statc"].ap()[:, 1:2],
        T["statc"].ap()[:, 0:1], T["p52"].ap()[:, 0:1]),
      reads=["st0", "p52"], writes=["st1"])
    E("vector", lambda v: v.tensor_scalar_mul(T["statc"].ap()[:, 2:3],
        T["statc"].ap()[:, 1:2], 805306368.0), reads=["st1"], writes=["st2"])
    E("vector", lambda v: v.tensor_scalar_mul(T["statc"].ap()[:, 1:2],
        T["statc"].ap()[:, 1:2], 1.5), reads=["st1"], writes=["st1"])
    E("vector", lambda v: v.tensor_sub(T["statc"].ap()[:, 2:3],
        T["statc"].ap()[:, 2:3], T["statc"].ap()[:, 1:2]),
      reads=["st2", "st1"], writes=["st2"])
    E("vector", lambda v: v.tensor_scalar_add(T["Ccol"].ap()[:, 0:1],
        T["statc"].ap()[:, 2:3], 12582912.0), reads=["st2"], writes=["Ccol"])
    for name, dst in (("bq2", "bq_s"), ("bk2", "bk_s"), ("bv2", "bv_s"),
                      ("bo2", "bo_s"), ("b12", "b1_s"), ("b22", "b2_s")):
        dma("sync", ap(dst)[:, :], ap(name)[:, :], "d_c", writes=[dst])
    for name, dst in (("tkT2", "tkT2_b"), ("tv", "tv_b"), ("tv08", "tv08_b")):
        dma("gpsimd", ap(dst)[:, :], ap(name)[:, :], "d_cg", writes=[dst])
    E("vector", lambda v: v.tensor_sub(T["tkT2_b"].ap()[:, 64:65],
        T["tkT2_b"].ap()[:, 0:1], T["tkT2_b"].ap()[:, 32:33]),
      reads=["tkT2_b"], writes=["tkT2_b"])
    for c in range(nt):
        dma("sync", ap("vpm")[:, VW*c+64:VW*(c+1)], ap("c_vind")[:, 33*c:33*(c+1)],
            "d_c", writes=["vpm"])
    for s in range(nt):
        E("tensor", (lambda s=s: (lambda te: te.matmul(
            T["PS"].ap()[0:nt, 3584+64*(s % 8):3584+64*(s % 8)+64],
            T["sel"].ap()[:, nt*s:nt*(s+1)], T["tv08_b"].ap()[:, :],
            start=True, stop=True)))(), reads=["sel", "tv08_b"], writes=["psB7"])
        E("vector", (lambda s=s: (lambda v: v.tensor_copy(
            T["tvsel"].ap()[0:nt, 64*s:64*(s+1)],
            T["PS"].ap()[0:nt, 3584+64*(s % 8):3584+64*(s % 8)+64])))(),
          reads=["psB7"], writes=["tvsel"])

    def wquant(srcname, dstname, rows, cols, lane):
        nr = rows // 128
        npc = (cols + 1023) // 1024
        first = True
        for r in range(nr):
          for pc in range(npc):
            c0, c1 = 1024 * pc, min(1024 * (pc + 1), cols)
            ld = "fq1" if (r * npc + pc) % 2 == 0 else "xtile"
            dma("sync", ap(ld)[:, 0:c1-c0], ap(srcname)[128*r:128*(r+1), c0:c1],
                "d_wq", writes=[ld])
            E("vector", (lambda c0=c0, c1=c1, ld=ld: (lambda v: v.tensor_reduce(
                T["statc"].ap()[:, 1:2], T[ld].ap()[:, 0:c1-c0], AX.X, ALU.add,
                apply_absolute_value=True)))(), reads=[ld], writes=["st1"])
            if first:
                first = False
                E("vector", lambda v: v.tensor_copy(T["statc"].ap()[:, 0:1],
                                                    T["statc"].ap()[:, 1:2]),
                  reads=["st1"], writes=["st0"])
            else:
                E("vector", lambda v: v.tensor_add(T["statc"].ap()[:, 0:1],
                                                   T["statc"].ap()[:, 0:1],
                                                   T["statc"].ap()[:, 1:2]),
                  reads=["st1", "st0"], writes=["st0"])
        E("tensor", lambda t: t.matmul(PS[0:1, 3584:3585], T["ones_colf"].ap()[:, 0:1],
                                       T["statc"].ap()[:, 0:1], start=True, stop=True),
          reads=["st0", "ones_colf"], writes=["psB7"])
        E("scalar", (lambda rows=rows, cols=cols: (lambda s: s.activation(
            T["statc"].ap()[0:1, 2:3], PS[0:1, 3584:3585], AF.Copy,
            scale=1.0 / (rows * cols))))(), reads=["psB7"], writes=["st2"])
        E("vector", lambda v: v.tensor_scalar_max(T["statc"].ap()[0:1, 2:3],
                                                  T["statc"].ap()[0:1, 2:3], 1e-5),
          reads=["st2"], writes=["st2"])
        E("scalar", lambda s: s.activation(T["statc"].ap()[0:1, 3:4],
                                           T["statc"].ap()[0:1, 2:3], AF.Ln),
          reads=["st2"], writes=["st3"])
        E("scalar", lambda s: s.activation(T["statc"].ap()[0:1, 3:4],
                                           T["statc"].ap()[0:1, 3:4], AF.Exp, scale=-1.0),
          reads=["st3"], writes=["st3"])
        dma("sync", ap("row_d")[lane:lane+1, 0:1], ap("statc")[0:1, 2:3],
            "d_wm", reads=["st2"], writes=[f"rdlane{lane}"])
        dma("sync", ap("row_d")[lane+2:lane+3, 0:1], ap("statc")[0:1, 3:4],
            "d_wm", reads=["st3"], writes=[f"rdlane{lane+2}"])
        dma("sync", ap("statc")[:, 3:4],
            ap("row_d")[lane+2:lane+3, 0:1].partition_broadcast(128),
            "d_wm2", reads=[f"rdlane{lane+2}"], writes=["st3c"])
        npc2 = (cols + 511) // 512
        for r in range(nr):
          for pc in range(npc2):
            c0, c1 = 512 * pc, min(512 * (pc + 1), cols)
            w_ = c1 - c0
            i_ = r * npc2 + pc
            ld = "fq1" if i_ % 2 == 0 else "xtile"
            lq = "fq2" if i_ % 2 == 0 else "x2t2"
            so_ = 512 * (i_ % 2)
            dma("sync", ap(ld)[:, 0:w_], ap(srcname)[128*r:128*(r+1), c0:c1],
                "d_wq", writes=[ld])
            E("vector", (lambda w_=w_, ld=ld, lq=lq: (lambda v: v.tensor_scalar(
                T[lq].ap()[:, 0:w_], T[ld].ap()[:, 0:w_],
                T["statc"].ap()[:, 3:4], T["Ccol"].ap()[:, 0:1],
                ALU.mult, ALU.add)))(),
              reads=[ld, "st3c", "Ccol"], writes=[lq])
            E("vector", (lambda w_=w_, lq=lq: (lambda v: v.tensor_scalar(
                T[lq].ap()[:, 0:w_], T[lq].ap()[:, 0:w_],
                T["Ccol"].ap()[:, 0:1], None, ALU.subtract)))(),
              reads=[lq, "Ccol"], writes=[lq])
            E("vector", (lambda w_=w_, lq=lq, so_=so_: (lambda v: v.tensor_scalar(
                T["sqscr"].ap()[:, so_:so_+w_], T[lq].ap()[:, 0:w_],
                1.0, -1.0, ALU.min, ALU.max)))(), reads=[lq],
              writes=[f"sqscr{i_ % 2}"])
            dma("gpsimd" if i_ % 2 else "sync",
                ap(dstname)[128*r:128*(r+1), c0:c1], ap("sqscr")[:, so_:so_+w_],
                "d_wqo" + str(i_ % 2), reads=[f"sqscr{i_ % 2}"], writes=[f"{dstname}{r}"])
    wquant("w1T", "w1i_d", HID, FF, 0)
    wquant("w2T", "w2i_d", FF, HID, 1)
    plan.emit("vector", lambda v: v.tensor_copy(T["sqscr"].ap()[0:1, 0:1],
                                                T["sqscr"].ap()[0:1, 0:1]),
              writes=["sqscr", "sqscr0", "sqscr1", "fq1", "fq2", "xtile", "x2t2"])

    for n in ("wq", "wk", "wv", "wo"):
        k.dram_tmp(f"{n}b_d", [HID, HID], BF16)
        dma("gpsimd", ap(f"{n}b_d")[:, :], ap(n)[:, :], "d_wc", writes=[f"{n}b"])

    for b in range(BPC):
        xoff = b * S

        for t in range(nt):
            dma("sync", ap("xtile")[:, :], ap("x")[xoff+128*t:xoff+128*(t+1), :],
                "d_x", writes=["xtile"])
            E("scalar", lambda s: s.activation(T["sqscr"].ap()[:, 0:HID],
                T["xtile"].ap()[:, :], AF.Square, accum_out=T["statc"].ap()[:, 1:2]),
              reads=["xtile"], writes=["sqscr", "st1"])
            E("vector", lambda v: v.reduce_sum(T["statc"].ap()[:, 0:1],
                T["xtile"].ap()[:, :], AX.X), reads=["xtile"], writes=["st0"])
            E("vector", (lambda H=HID: (lambda v: v.tensor_scalar_mul(
                T["statc"].ap()[:, 0:1], T["statc"].ap()[:, 0:1], 1.0 / H)))(),
              reads=["st0"], writes=["st0"])
            E("vector", lambda v: v.tensor_mul(T["statc"].ap()[:, 2:3],
                T["statc"].ap()[:, 0:1], T["statc"].ap()[:, 0:1]),
              reads=["st0"], writes=["st2"])
            E("vector", (lambda H=HID: (lambda v: v.scalar_tensor_tensor(
                T["statc"].ap()[:, 3:4], T["statc"].ap()[:, 1:2], 1.0 / H,
                T["statc"].ap()[:, 2:3], ALU.mult, ALU.subtract)))(),
              reads=["st1", "st2"], writes=["st3"])
            E("vector", lambda v: v.tensor_scalar_add(
                T["statc"].ap()[:, 3:4], T["statc"].ap()[:, 3:4],
                (1e-5 - float(np.float32(1e-5)))), reads=["st3"], writes=["st3"])
            E("scalar", lambda s: s.activation(T["statc"].ap()[:, 3:4],
                T["statc"].ap()[:, 3:4], AF.Ln, bias=1e-5), reads=["st3"], writes=["st3"])
            E("scalar", lambda s: s.activation(T["statc"].ap()[:, 3:4],
                T["statc"].ap()[:, 3:4], AF.Exp, scale=-0.5), reads=["st3"], writes=["st3"])
            E("vector", lambda v: v.tensor_scalar(T["toktmp"].ap()[:, 0:HID],
                T["xtile"].ap()[:, :], T["statc"].ap()[:, 0:1],
                T["statc"].ap()[:, 3:4], ALU.subtract, ALU.mult),
              reads=["xtile", "st0", "st3"], writes=["toktmp"])
            for c in range(nhc):
                E("tensor", (lambda c=c: (lambda te: te.matmul(
                    PS[0:128, 2048+128*c:2048+128*(c+1)],
                    T["toktmp"].ap()[:, 128*c:128*(c+1)], T["ident"].ap()[:, :],
                    start=True, stop=True)))(), reads=["toktmp", "ident"],
                  writes=psnames(2048+128*c, 2048+128*(c+1)))
            E("vector", (lambda t=t: (lambda v: v.tensor_copy(
                T["XN1"].ap().rearrange("p (c s) -> p c s", s=S)[:, :, 128*t:128*(t+1)],
                PS[0:128, 2048:2048+128*nhc].rearrange("p (c s) -> p c s", s=128))))(),
              reads=psnames(2048, 2048+128*nhc), writes=["XN1"])

        def proj(wname, bname, dstname):
            wslot = [0]
            for oc in range(nhc):
                    for i in range(nhc):
                        ws = f"Wc{wslot[0] % 2}"; wslot[0] += 1
                        dma("sync", T[ws].ap()[:, 0:128],
                            ap(f"{wname}b_d")[128*i:128*(i+1), 128*oc:128*(oc+1)], "d_w",
                            reads=[f"{wname}b"], writes=[ws])
                        for st4 in range(nst):
                            pso = 2048 + 512 * st4
                            pname = ["psB4", "psB5", "psB6w0", "psB7"][st4]
                            E("tensor", (lambda oc=oc, st4=st4, i=i, pso=pso, ws=ws: (lambda te: te.matmul(
                                PS[0:128, pso:pso+SW], T[ws].ap()[:, 0:128],
                                T["XN1"].ap()[:, S*i+SW*st4:S*i+SW*(st4+1)],
                                start=(i == 0), stop=(i == nhc-1))))(),
                              reads=[ws, "XN1"], writes=[pname])
                    for st4 in range(nst):
                        pso = 2048 + 512 * st4
                        pname = ["psB4", "psB5", "psB6w0", "psB7"][st4]
                        E("scalar", (lambda oc=oc, st4=st4, pso=pso, bname=bname, dstname=dstname:
                            (lambda s: s.activation(
                                T[dstname].ap()[:, S*oc+SW*st4:S*oc+SW*(st4+1)],
                                PS[0:128, pso:pso+SW], AF.Identity,
                                bias=T[bname].ap()[:, oc:oc+1])))(),
                          reads=[pname, bname], writes=[dstname])
        proj("wq", "bq_s", "QF")
        proj("wk", "bk_s", "KF")
        proj("wv", "bv_s", "VF")

        for side, FNAME, lane in (("q", "QF", 4), ("k", "KF", 5)):
            for hc in range(nhc):
                for half in range(2):
                    h = 2 * hc + half
                    hr = 64 * half
                    for st4 in range(nst):
                        E("scalar", (lambda hc=hc, hr=hr, st4=st4, FNAME=FNAME: (lambda s: s.activation(
                            T["sqscr"].ap()[hr:hr+64, 0:SW],
                            T[FNAME].ap()[hr:hr+64, S*hc+SW*st4:S*hc+SW*(st4+1)],
                            AF.Square)))(), reads=[FNAME], writes=["sqscr"])
                        E("tensor", (lambda st4=st4, hr=hr: (lambda te: te.matmul(
                            PS[0:1, 3584:3584+SW],
                            T["ones_colb"].ap()[hr:hr+64, 0:1],
                            T["sqscr"].ap()[hr:hr+64, 0:SW],
                            start=True, stop=True)))(),
                          reads=["sqscr", "ones_colb"], writes=["psB7"])
                        E("vector", (lambda: (lambda v: v.tensor_reduce(
                            T["statc"].ap()[0:1, 8:9], PS[0:1, 3584:3584+SW],
                            AX.X, ALU.max)))(), reads=["psB7"], writes=["st8"])
                        if st4 == 0:
                            E("vector", (lambda: (lambda v: v.tensor_copy(
                                T["statc"].ap()[0:1, 9:10], T["statc"].ap()[0:1, 8:9])))(),
                              reads=["st8"], writes=["st9"])
                        else:
                            E("vector", lambda v: v.tensor_max(
                                T["statc"].ap()[0:1, 9:10], T["statc"].ap()[0:1, 9:10],
                                T["statc"].ap()[0:1, 8:9]), reads=["st8", "st9"], writes=["st9"])
                    E("scalar", lambda s: s.activation(T["statc"].ap()[0:1, 9:10],
                        T["statc"].ap()[0:1, 9:10], AF.Ln), reads=["st9"], writes=["st9"])
                    E("scalar", lambda s: s.activation(T["statc"].ap()[0:1, 9:10],
                        T["statc"].ap()[0:1, 9:10], AF.Exp, scale=0.5), reads=["st9"], writes=["st9"])
                    dma("sync", ap("row_d")[lane:lane+1, h:h+1], ap("statc")[0:1, 9:10],
                        "d_m", reads=["st9"], writes=[f"rdl_{side}{h}"])
        if b == 0:
            E("scalar", lambda s: s.activation(T["sqscr"].ap()[0:64, 0:NJ],
                T["tkT2_b"].ap()[0:64, 0:NJ], AF.Square), reads=["tkT2_b"], writes=["sqscr"])
            E("tensor", lambda te: te.matmul(PS[0:1, 3584:3584+NJ],
                T["ones_colb"].ap()[0:64, 0:1], T["sqscr"].ap()[0:64, 0:NJ],
                start=True, stop=True), reads=["sqscr", "ones_colb"], writes=["psB7"])
            E("vector", lambda v: v.tensor_reduce(T["statc"].ap()[0:1, 10:11],
                PS[0:1, 3584:3584+NJ], AX.X, ALU.max), reads=["psB7"], writes=["st10"])
            E("scalar", lambda s: s.activation(T["statc"].ap()[0:1, 10:11],
                T["statc"].ap()[0:1, 10:11], AF.Ln), reads=["st10"], writes=["st10"])
            E("scalar", lambda s: s.activation(T["statc"].ap()[0:1, 10:11],
                T["statc"].ap()[0:1, 10:11], AF.Exp, scale=0.5), reads=["st10"], writes=["st10"])
            dma("sync", ap("row_d")[6:7, 0:1], ap("statc")[0:1, 10:11], "d_m",
                reads=["st10"], writes=["rdl_t"])
        build_attn(k, cfg, b)
        if not cfg.get("no_tail"):
            build_tail(k, cfg, b)
    return k

def psnames(c0, c1):
    out = []
    if c0 < 2048:
        out.append("psAV")
    for bkt, nm in ((2048, "psB4"), (2560, "psB5"), (3072, "psB6"), (3584, "psB7")):
        if c0 < bkt + 512 and c1 > bkt:
            out.append(nm)
    return out


def build_attn(k, cfg, b):
    S = cfg["S"]; NH = cfg["NH"]; HID = 64 * NH
    nt = S // 128; nhc = HID // 128
    nst = max(1, S // 512); SW = min(512, S)
    VW = 64 + 32 + 1
    T, plan = k.T, k.plan
    ap = lambda n: T[n].ap()
    PS = T["PS"].ap()
    E = plan.emit

    def dma(q, dst, src, sem, reads=(), writes=()):
        E(q, (lambda d=dst, s=src: (lambda eng: eng.dma_start(out=d, in_=s)))(),
          reads=reads, writes=writes, dma=sem)

    dma("sync", ap("maxs")[0:1, 0:NH], ap("row_d")[4:5, 0:NH], "d_m2",
        reads=[f"rdl_q{h}" for h in range(NH)], writes=["nmq"])
    dma("sync", ap("maxs")[0:1, 32:32+NH], ap("row_d")[5:6, 0:NH], "d_m2",
        reads=[f"rdl_k{h}" for h in range(NH)], writes=["nmk"])
    dma("sync", ap("maxs")[0:1, 64:65], ap("row_d")[6:7, 0:1], "d_m2",
        reads=["rdl_t"], writes=["nmt"])
    E("vector", lambda v: v.tensor_scalar(T["maxs"].ap()[0:1, 32:32+NH],
        T["maxs"].ap()[0:1, 32:32+NH], T["maxs"].ap()[0:1, 64:65], None, ALU.add),
      reads=["nmk", "nmt"], writes=["nmk"])
    E("vector", lambda v: v.tensor_mul(T["maxs"].ap()[0:1, 0:NH],
        T["maxs"].ap()[0:1, 0:NH], T["maxs"].ap()[0:1, 32:32+NH]),
      reads=["nmq", "nmk"], writes=["nmq"])
    E("vector", lambda v: v.tensor_scalar_mul(T["maxs"].ap()[0:1, 0:NH],
        T["maxs"].ap()[0:1, 0:NH], -0.125), reads=["nmq"], writes=["nmq"])
    dma("sync", ap("row_d")[7:8, 0:NH], ap("maxs")[0:1, 0:NH], "d_m3",
        reads=["nmq"], writes=["negm_r"])
    dma("sync", ap("negM")[:, 0:NH],
        ap("row_d")[7:8, 0:NH].partition_broadcast(128), "d_m4",
        reads=["negm_r"], writes=["negM"])

    for h in range(cfg.get("heads", NH)):
        hc, hr = h // 2, 64 * (h % 2)
        qf = lambda c0, c1, hr=hr, hc=hc: T["QF"].ap()[hr:hr+64, S*hc+c0:S*hc+c1]
        kf = lambda c0, c1, hr=hr, hc=hc: T["KF"].ap()[hr:hr+64, S*hc+c0:S*hc+c1]
        vf = lambda c0, c1, hr=hr, hc=hc: T["VF"].ap()[hr:hr+64, S*hc+c0:S*hc+c1]

        for t in range(nt):
            E("tensor", (lambda t=t, vf=vf, hr=hr: (lambda te: te.matmul(
                PS[0:128, 2048+64*t:2048+64*(t+1)], vf(128*t, 128*(t+1)),
                T["identd"].ap()[hr:hr+64, 0:64], start=True, stop=True)))(),
              reads=["VF", "identd"], writes=psnames(2048+64*t, 2048+64*(t+1)))
        E("vector", (lambda nt=nt: (lambda v: v.tensor_copy(
            T["vpm"].ap().rearrange("p (c w) -> p c w", w=VW)[:, :, 0:64],
            PS[0:128, 2048:2048+64*nt].rearrange("p (c w) -> p c w", w=64))))(),
          reads=psnames(2048, 2048+64*nt), writes=["vpm"])

        for st4 in range(nst):
            E("tensor", (lambda st4=st4, qf=qf, hr=hr: (lambda te: te.matmul(
                PS[0:65, 3584:3584+SW], T["tkT2_b"].ap()[hr:hr+64, :],
                qf(SW*st4, SW*(st4+1)), start=True, stop=True)))(),
              reads=["tkT2_b", "QF"], writes=["psB7"])
            E("scalar", (lambda st4=st4: (lambda s: s.activation(
                T["SattnT"].ap()[:, SW*st4:SW*(st4+1)], PS[0:NJ, 3584:3584+SW],
                AF.Copy)))(), reads=["psB7"], writes=["SattnT"])
            E("scalar", (lambda st4=st4: (lambda s: s.activation(
                T["s08row"].ap()[0:1, SW*st4:SW*(st4+1)], PS[64:65, 3584:3584+SW],
                AF.Copy)))(), reads=["psB7"], writes=["s08row"])

        for t in range(nt):
            E("tensor", (lambda t=t: (lambda te: te.matmul(
                PS[0:128, 3072+NJ*t:3072+NJ*(t+1)],
                T["SattnT"].ap()[0:NJ, 128*t:128*(t+1)], T["ident"].ap()[0:NJ, 0:NJ],
                start=True, stop=True)))(), reads=["SattnT", "ident"],
              writes=["psB6w0"])
        E("vector", (lambda nt=nt: (lambda v: v.tensor_copy(
            T["Sqp"].ap()[:, 0:NJ*nt], PS[0:128, 3072:3072+NJ*nt])))(),
          reads=["psB6w0"], writes=["Sqp"])


        def build_window(tw):
            slot = f"A2X{tw % 4}"
            blk = f"a2xd{tw % 2}"
            sD = f"slotD{tw % 2}"
            v198 = ap(blk)[0:1, :].rearrange("o (p w) -> (o p) w", w=WWIN + 2)
            v197 = ap(blk)[0:1, 0:128 * (WWIN + 1)].rearrange(
                "o (p w) -> (o p) w", w=WWIN + 1)
            E("vector", (lambda tw=tw: (lambda v: v.tensor_scalar(
                T["vj"].ap()[:, 0:8], T["Sqp"].ap()[:, NJ*tw:NJ*tw+8],
                T["Sqp"].ap()[:, NJ*tw+8:NJ*tw+9], None, ALU.subtract)))(),
              reads=["Sqp"], writes=["vj"])
            dma("gpsimd", v198[:, LPAD-3:LPAD+4], T["vj"].ap()[:, 1:8], "d_a2w",
                reads=["vj", f"a2xz{tw % 2}", f"a2xz{tw % 2}b"], writes=[f"a2dg{tw % 2}"])
            dma("sync", T[sD].ap()[:, 0:WWIN], v197[:, 0:WWIN], "d_a2r",
                reads=[f"a2dg{tw % 2}"], writes=[sD])
            E("vector", (lambda tw=tw, slot=slot: (lambda v: v.tensor_scalar_mul(
                T[slot].ap()[:, :], T["coef"].ap()[:, 0:WWIN],
                T["vj"].ap()[:, 0:1])))(),
              reads=["coef", "vj"], writes=[slot])
            E("vector", (lambda slot=slot, sD=sD: (lambda v: v.tensor_add(
                T[slot].ap()[:, :], T[slot].ap()[:, :], T[sD].ap()[:, :])))(),
              reads=[slot, sD], writes=[slot])
        build_window(0)
        if nt > 1:
            build_window(1)

        def fw_window(tw):
            c0 = 0 if tw > 0 else LPAD
            c1 = WWIN if tw < nt - 1 else LPAD + 128
            base = 3072 + 512 * (tw % 2)
            pw = "psB6w0" if tw % 2 == 0 else "psB7"
            fv = f"fwv{tw % 2}"
            blk = f"fwd{tw % 2}"
            if tw > 0:
                E("tensor", (lambda tw=tw, base=base: (lambda te: te.matmul(
                    PS[0:128, base:base+LPAD],
                    T[f"F{(tw-1) % 3}"].ap()[64:128, 128*tw:128*(tw+1)],
                    T["identd"].ap()[64:128, 0:64], start=True, stop=True)))(),
                  reads=[f"F{(tw-1) % 3}s{(128*tw) // SW}", "identd"], writes=[pw])
            E("tensor", (lambda tw=tw, base=base: (lambda te: te.matmul(
                PS[0:128, base+LPAD:base+LPAD+128],
                T[f"F{tw % 3}"].ap()[0:128, 128*tw:128*(tw+1)],
                T["ident"].ap()[:, :], start=True, stop=True)))(),
              reads=[f"F{tw % 3}s{(128*tw) // SW}", "ident"], writes=[pw])
            if tw < nt - 1:
                E("tensor", (lambda tw=tw, base=base: (lambda te: te.matmul(
                    PS[0:128, base+LPAD+128:base+WWIN],
                    T[f"F{(tw+1) % 3}"].ap()[0:RPAD, 128*tw:128*(tw+1)],
                    T["ident"].ap()[0:RPAD, 0:RPAD], start=True, stop=True)))(),
                  reads=[f"F{(tw+1) % 3}s{(128*tw) // SW}", "ident"], writes=[pw])
            if tw == 0:
                E("vector", (lambda fv=fv: (lambda v: v.memset(
                    T[fv].ap()[:, 0:LPAD], 0.0)))(), reads=[], writes=[fv])
            if tw == nt - 1:
                E("vector", (lambda fv=fv: (lambda v: v.memset(
                    T[fv].ap()[:, LPAD+128:WWIN], 0.0)))(), reads=[], writes=[fv])
            E("scalar", (lambda base=base, c0=c0, c1=c1, fv=fv: (lambda s: s.activation(
                T[fv].ap()[:, c0:c1], PS[0:128, base+c0:base+c1], AF.Copy)))(),
              reads=[pw], writes=[fv])
            dma("scalar", ap(blk)[0:1, 0:128 * WWIN].rearrange("o (p w) -> (o p) w", w=WWIN),
                T[fv].ap()[:, 0:WWIN], "d_fww", reads=[fv], writes=[f"fwb{tw % 2}"])
            dma("gpsimd",
                T["FBqp"].ap()[:, NJ*tw+1:NJ*tw+8],
                ap(blk)[0:1, 0:128 * (WWIN + 1)].rearrange(
                    "o (p w) -> (o p) w", w=WWIN + 1)[:, LPAD-3:LPAD+4],
                "d_fwr", reads=[f"fwb{tw % 2}"], writes=["FBqp"])
            for j in (0, 8):
                s0_, s1_ = (61, 188) if j == 0 else (68, 195)
                cc0, cc1 = max(c0, s0_), min(c1, s1_)
                E("vector", (lambda tw=tw, j=j, cc0=cc0, cc1=cc1, fv=fv: (lambda v: v.scalar_tensor_tensor(
                    T["ttr_scr"].ap()[:, cc0:cc1], T[fv].ap()[:, cc0:cc1], 1.0,
                    T["coefw"].ap()[:, WWIN*j+cc0:WWIN*j+cc1],
                    ALU.mult, ALU.mult,
                    accum_out=T["FBqp"].ap()[:, NJ*tw+j:NJ*tw+j+1])))(),
                  reads=[fv, "coefw"], writes=["ttr_scr", "FBqp"])

        par = "a" if hr == 0 else "b"
        for t in range(nt):
            stag = f"KQ{t % 2}{par}"
            E("scalar", (lambda t=t, kf=kf, stag=stag, hr=hr: (lambda s: s.activation(
                T[stag].ap()[hr:hr+64, 0:128], kf(128*t, 128*(t+1)), AF.Copy)))(),
              reads=["KF", f"{stag}_z"], writes=[stag])
            if t + 2 < nt:
                build_window(t + 2)
            for st4 in range(nst):
                g0, g1 = SW * st4, SW * (st4 + 1)
                pso = 2048 + 512 * (st4 % 2)
                pname = "psB4" if st4 % 2 == 0 else "psB5"
                lo, hi = max(g0, 128 * (t + 1)), g1
                have_sfx = lo < hi
                have_a = g0 <= 128 * t < g1
                have_b = t >= 1 and g0 <= 128 * (t - 1) < g1
                have_c = t + 1 < nt and g0 <= 128 * (t + 1) < g1
                npieces = int(have_a) + int(have_b) + int(have_c) + int(have_sfx)
                E("tensor", (lambda g0=g0, g1=g1, t=t, pso=pso, hc=hc, stag=stag,
                             stp=(npieces == 0): (lambda te: te.matmul(
                    PS[0:128, pso:pso+SW], T[stag].ap()[:, :],
                    T["QF"].ap()[0:128, S*hc+g0:S*hc+g1],
                    start=True, stop=stp)))(),
                  reads=[stag, "QF"], writes=[pname])
                if have_b:
                    E("tensor", (lambda t=t, pso=pso, g0=g0: (lambda te: te.matmul(
                        PS[0:RPAD, pso+128*(t-1)-g0:pso+128*t-g0],
                        T[f"A2X{(t-1) % 4}"].ap()[:, LPAD+128:WWIN],
                        T["ident"].ap()[:, :], start=False, stop=False)))(),
                      reads=[f"A2X{(t-1) % 4}", "ident"], writes=[pname])
                if have_c:
                    E("tensor", (lambda t=t, pso=pso, g0=g0: (lambda te: te.matmul(
                        PS[64:128, pso+128*(t+1)-g0:pso+128*(t+2)-g0],
                        T[f"A2X{(t+1) % 4}"].ap()[:, 0:LPAD],
                        T["ident"].ap()[:, :], start=False, stop=False)))(),
                      reads=[f"A2X{(t+1) % 4}", "ident"], writes=[pname])
                if have_a:
                    E("tensor", (lambda t=t, pso=pso, g0=g0, stp=not have_sfx: (lambda te: te.matmul(
                        PS[0:128, pso+128*t-g0:pso+128*(t+1)-g0],
                        T[f"A2X{t % 4}"].ap()[:, LPAD:LPAD+128],
                        T["ident"].ap()[:, :], start=False, stop=stp)))(),
                      reads=[f"A2X{t % 4}", "ident"], writes=[pname])
                if have_sfx:
                    E("tensor", (lambda lo=lo, hi=hi, pso=pso, g0=g0: (lambda te: te.matmul(
                        PS[0:128, pso+lo-g0:pso+hi-g0], T["ones_row"].ap()[:, :],
                        T["s08row"].ap()[0:1, lo:hi], start=False, stop=True)))(),
                      reads=["ones_row", "s08row"], writes=[pname])
                elif not have_a and npieces > 0:
                    E("tensor", (lambda pso=pso: (lambda te: te.matmul(
                        PS[0:128, pso:pso+1], T["ones_row"].ap()[:, :],
                        T["ident"].ap()[0:1, 1:2], start=False, stop=True)))(),
                      reads=["ones_row", "ident"], writes=[pname])
                E("scalar", (lambda g0=g0, g1=g1, t=t, pso=pso, h=h: (lambda s: s.activation(
                    T[f"F{t % 3}"].ap()[:, g0:g1], PS[0:128, pso:pso+SW], AF.Exp,
                    bias=T["negM"].ap()[:, h:h+1], scale=0.125)))(),
                  reads=[pname, "negM"], writes=[f"F{t % 3}s{st4}"])

            if t >= 1:
                fw_window(t - 1)
            if t == nt - 1:
                fw_window(t)
                E("vector", (lambda nt=nt: (lambda v: v.tensor_copy(
                    T["sqscr"].ap()[:, 0:NJ*nt], T["FBqp"].ap()[:, 0:NJ*nt])))(),
                  reads=["FBqp"], writes=["sqscr"])
                for s4 in range((nt + 3) // 4):
                    ss = list(range(4 * s4, min(4 * s4 + 4, nt)))
                    for i_, s in enumerate(ss):
                        E("tensor", (lambda s=s, i_=i_: (lambda te: te.matmul(
                            PS[0:NJ, 2048+128*i_:2048+128*(i_+1)],
                            T["sqscr"].ap()[:, NJ*s:NJ*(s+1)], T["ident"].ap()[:, :],
                            start=True, stop=True)))(), reads=["sqscr", "ident"],
                          writes=psnames(2048+128*i_, 2048+128*(i_+1)))
                    E("vector", (lambda s4=s4, n=len(ss): (lambda v: v.tensor_copy(
                        T["FBT"].ap()[0:NJ, 512*s4:512*s4+128*n],
                        PS[0:NJ, 2048:2048+128*n])))(),
                      reads=psnames(2048, 2048+128*len(ss)), writes=["FBT"])
                for s in range(nt):
                    E("tensor", (lambda s=s: (lambda te: te.matmul(
                        PS[0:64, 128*s:128*(s+1)], T["tv_b"].ap()[:, :],
                        T["FBT"].ap()[0:NJ, 128*s:128*(s+1)], start=False, stop=False)))(),
                      reads=["tv_b", "FBT"], writes=["psAV"])
            for st4 in range(nst):
                g0, g1 = SW * st4, SW * (st4 + 1)
                E("tensor", (lambda t=t, g0=g0, g1=g1: (lambda te: te.matmul(
                    PS[0:VW, g0:g1], T["vpm"].ap()[:, VW*t:VW*(t+1)],
                    T[f"F{t % 3}"].ap()[:, g0:g1],
                    start=(t == 0), stop=(t == nt - 1))))(),
                  reads=["vpm", f"F{t % 3}s{st4}"], writes=["psAV"])

        E("vector", (lambda nt=nt: (lambda v: v.tensor_copy(
            T["Rrows"].ap()[0:nt, 0:S], PS[64:64+nt, 0:S])))(),
          reads=["psAV"], writes=["Rrows"])
        for st4 in range(nst):
            z0, z1 = SW * st4, SW * (st4 + 1)
            E("scalar", (lambda nt=nt, z0=z0, z1=z1: (lambda s: s.activation(
                T["rowf"].ap()[0:1, 0:z1-z0], PS[96:97, z0:z1], AF.Copy)))(),
              reads=["psAV"], writes=["rowf"])
            E("scalar", (lambda z0=z0, z1=z1: (lambda s: s.activation(
                T["rowf"].ap()[0:1, 0:z1-z0], T["rowf"].ap()[0:1, 0:z1-z0],
                AF.Ln)))(), reads=["rowf"], writes=["rowf"])
            E("scalar", (lambda z0=z0, z1=z1: (lambda s: s.activation(
                T["rowf"].ap()[0:1, 0:z1-z0], T["rowf"].ap()[0:1, 0:z1-z0],
                AF.Exp, scale=-1.0)))(), reads=["rowf"], writes=["rowf"])
            dma("sync", ap("row_d")[2:3, z0:z1], ap("rowf")[0:1, 0:z1-z0], "d_z",
                reads=["rowf"], writes=["rdlane2"])
        dma("gpsimd", ap("zbt")[:, 0:S],
            ap("row_d")[2:3, 0:S].partition_broadcast(128), "d_zb",
            reads=["rdlane2"], writes=["zbt"])

        for st4 in range(nst):
            g0, g1 = SW*st4, SW*(st4+1)
            for si in range((g1 - g0) // 128):
                s = (g0 // 128) + si
                pso = 2048 + 128 * (s % 8)
                E("tensor", (lambda s=s, pso=pso: (lambda te: te.matmul(
                    PS[0:64, pso:pso+128], T["tvsel"].ap()[0:nt, 64*s:64*(s+1)],
                    T["Rrows"].ap()[0:nt, 128*s:128*(s+1)], start=True, stop=True)))(),
                  reads=["tvsel", "Rrows"], writes=psnames(pso, pso+128))
                E("vector", (lambda s=s, si=si, pso=pso: (lambda v: v.tensor_tensor(
                    T["toktmp"].ap()[0:64, SW+128*si:SW+128*(si+1)], PS[0:64, pso:pso+128],
                    T["zbt"].ap()[0:64, 128*s:128*(s+1)], ALU.mult)))(),
                  reads=psnames(pso, pso+128) + ["zbt"], writes=["toktmp2"])
            E("vector", (lambda g0=g0, g1=g1: (lambda v: v.tensor_tensor(
                T["toktmp"].ap()[0:64, 0:g1-g0], PS[0:64, g0:g1],
                T["zbt"].ap()[0:64, g0:g1], ALU.mult)))(),
              reads=["psAV", "zbt"], writes=["toktmp"])
            E("vector", (lambda g0=g0, g1=g1, hc=hc, hr=hr: (lambda v: v.tensor_add(
                T["XN1"].ap()[hr:hr+64, S*hc+g0:S*hc+g1],
                T["toktmp"].ap()[0:64, 0:g1-g0],
                T["toktmp"].ap()[0:64, SW:SW+g1-g0])))(),
              reads=["toktmp", "toktmp2"], writes=["XN1"])
        if cfg.get("dbg") and h == cfg["NH"] - 1 and b == 0:
            if "dbg_d" not in T:
                k.dram_tmp("dbg_d", [192, S], BF16)
            dma("sync", ap("dbg_d")[0:128, :], T["XN1"].ap()[0:128, 0:S], "d_dbg",
                reads=["XN1"], writes=["dbgd"])
    if cfg.get("dbg") and b == 0:
        if "dbg3_d" not in T:
            k.dram_tmp("dbg3_d", [128, S], BF16)


def build_tail(k, cfg, b):
    S = cfg["S"]; NH = cfg["NH"]; HID = 64 * NH; FF = cfg["FF"]
    nt = S // 128; nhc = HID // 128; nfc = FF // 128
    nst = max(1, S // 512); SW = min(512, S)
    T, plan = k.T, k.plan
    ap = lambda n: T[n].ap()
    PS = T["PS"].ap()
    E = plan.emit
    xoff = b * S
    LN127 = float(np.log(127.0))

    def dma(q, dst, src, sem, reads=(), writes=()):
        E(q, (lambda d=dst, s=src: (lambda eng: eng.dma_start(out=d, in_=s)))(),
          reads=reads, writes=writes, dma=sem)

    _BN = ["psB4", "psB5", "psB6w0", "psB7"]
    wslot = [0]
    for oc in range(nhc):
        for i in range(nhc):
            ws = f"Wc{wslot[0] % 2}"; wslot[0] += 1
            dma("sync", T[ws].ap()[:, 0:128],
                ap("wob_d")[128*i:128*(i+1), 128*oc:128*(oc+1)], "d_w",
                reads=["wob"], writes=[ws])
            for st4 in range(nst):
                pso = 2048 + 512 * st4
                E("tensor", (lambda st4=st4, i=i, pso=pso, ws=ws: (lambda te: te.matmul(
                    PS[0:128, pso:pso+SW], T[ws].ap()[:, 0:128],
                    T["XN1"].ap()[:, S*i+SW*st4:S*i+SW*(st4+1)],
                    start=(i == 0), stop=(i == nhc-1))))(),
                  reads=[ws, "XN1"], writes=[_BN[st4]])
        for st4 in range(nst):
            pso = 2048 + 512 * st4
            E("scalar", (lambda oc=oc, st4=st4, pso=pso: (lambda s: s.activation(
                T["QF"].ap()[:, S*oc+SW*st4:S*oc+SW*(st4+1)], PS[0:128, pso:pso+SW],
                AF.Identity, bias=T["bo_s"].ap()[:, oc:oc+1])))(),
              reads=[_BN[st4], "bo_s"], writes=["QF"])

    if cfg.get("dbg") and b == 0:
        if "dbg3_d" not in T:
            k.dram_tmp("dbg3_d", [128, S], BF16)
        dma("sync", ap("dbg3_d")[0:128, :], T["QF"].ap()[0:128, 0:S], "d_dbg",
            reads=["QF"], writes=["dbg3"])

    def ln_tok_chain(src_name, src_slice, eps, out_name, out_slice, width):
        E("scalar", (lambda src_slice=src_slice, width=width: (lambda s: s.activation(
            T["sqscr"].ap()[:, 0:width], src_slice(), AF.Square,
            accum_out=T["statc"].ap()[:, 1:2])))(),
          reads=[src_name], writes=["sqscr", "st1"])
        E("vector", (lambda src_slice=src_slice: (lambda v: v.reduce_sum(
            T["statc"].ap()[:, 0:1], src_slice(), AX.X)))(),
          reads=[src_name], writes=["st0"])
        E("vector", (lambda width=width: (lambda v: v.tensor_scalar_mul(
            T["statc"].ap()[:, 0:1], T["statc"].ap()[:, 0:1], 1.0 / width)))(),
          reads=["st0"], writes=["st0"])
        E("vector", lambda v: v.tensor_mul(T["statc"].ap()[:, 2:3],
            T["statc"].ap()[:, 0:1], T["statc"].ap()[:, 0:1]), reads=["st0"], writes=["st2"])
        E("vector", (lambda width=width: (lambda v: v.scalar_tensor_tensor(
            T["statc"].ap()[:, 3:4], T["statc"].ap()[:, 1:2], 1.0 / width,
            T["statc"].ap()[:, 2:3], ALU.mult, ALU.subtract)))(),
          reads=["st1", "st2"], writes=["st3"])
        E("vector", (lambda eps=eps: (lambda v: v.tensor_scalar_add(
            T["statc"].ap()[:, 3:4], T["statc"].ap()[:, 3:4],
            (eps - float(np.float32(eps))))))(), reads=["st3"], writes=["st3"])
        E("scalar", (lambda eps=eps: (lambda s: s.activation(T["statc"].ap()[:, 3:4],
            T["statc"].ap()[:, 3:4], AF.Ln, bias=eps)))(), reads=["st3"], writes=["st3"])
        E("scalar", lambda s: s.activation(T["statc"].ap()[:, 3:4],
            T["statc"].ap()[:, 3:4], AF.Exp, scale=-0.5), reads=["st3"], writes=["st3"])
        E("vector", (lambda src_slice=src_slice, out_slice=out_slice: (lambda v: v.tensor_scalar(
            out_slice(), src_slice(), T["statc"].ap()[:, 0:1], T["statc"].ap()[:, 3:4],
            ALU.subtract, ALU.mult)))(),
          reads=[src_name, "st0", "st3"], writes=[out_name])

    for t in range(nt):
        for c in range(nhc):
            E("tensor", (lambda t=t, c=c: (lambda te: te.matmul(
                PS[0:128, 2048+128*c:2048+128*(c+1)],
                T["QF"].ap()[:, S*c+128*t:S*c+128*(t+1)], T["ident"].ap()[:, :],
                start=True, stop=True)))(), reads=["QF", "ident"],
              writes=psnames(2048+128*c, 2048+128*(c+1)))
        E("vector", lambda v: v.tensor_copy(T["fq1"].ap()[:, 0:HID],
            PS[0:128, 2048:2048+HID]), reads=psnames(2048, 2048+HID), writes=["fq1"])
        ln_tok_chain("fq1", (lambda: T["fq1"].ap()[:, 0:HID]), 1e-5,
                     "toktmp", (lambda: T["toktmp"].ap()[:, 0:HID]), HID)
        dma("sync", ap("xtile")[:, :], ap("x")[xoff+128*t:xoff+128*(t+1), :],
            "d_x2", writes=["xtile"])
        E("vector", lambda v: v.tensor_add(T["x2t"].ap()[:, :],
            T["toktmp"].ap()[:, 0:HID], T["xtile"].ap()[:, :]),
          reads=["toktmp", "xtile"], writes=["x2t"])
        dma("sync", ap("x2_d")[xoff+128*t:xoff+128*(t+1), :], ap("x2t")[:, :],
            "d_x2o", reads=["x2t"], writes=[f"x2d_{b}_{t}"])

    if b == 0:
        for lane, col, nm in ((0, 7, "lnw1col"), (1, 15, "lnw2col")):
            dma("sync", ap("statc")[:, 14:15],
                ap("row_d")[lane:lane+1, 0:1].partition_broadcast(128), "d_m5",
                reads=[f"rdlane{lane}"], writes=["st14"])
            E("scalar", lambda s: s.activation(T["statc"].ap()[:, 14:15],
                T["statc"].ap()[:, 14:15], AF.Ln), reads=["st14"], writes=["st14"])
            E("vector", (lambda col=col, nm=nm: (lambda v: v.tensor_copy(
                T["statc"].ap()[:, col:col+1], T["statc"].ap()[:, 14:15])))(),
              reads=["st14"], writes=[nm])
        E("vector", lambda v: v.tensor_scalar(T["statc"].ap()[:, 16:17],
            T["scal"].ap()[:, 21:22], -127.0, 2.0, ALU.mult, ALU.add),
          reads=["scal"], writes=["r127t"])
        E("vector", lambda v: v.tensor_mul(T["statc"].ap()[:, 16:17],
            T["statc"].ap()[:, 16:17], T["scal"].ap()[:, 21:22]),
          reads=["r127t", "scal"], writes=["r127"])

    for t in range(nt):
        dma("sync", ap("x2t")[:, :], ap("x2_d")[xoff+128*t:xoff+128*(t+1), :],
            "d_x2i", reads=[f"x2d_{b}_{t}"], writes=["x2t"])
        E("vector", lambda v: v.tensor_copy(T["fq1"].ap()[:, 0:HID],
            T["x2t"].ap()[:, :]), reads=["x2t"], writes=["fq1"])
        ln_tok_chain("fq1", (lambda: T["fq1"].ap()[:, 0:HID]), 1e-5,
                     "toktmp", (lambda: T["toktmp"].ap()[:, 0:HID]), HID)
        E("scalar", lambda s: s.activation(T["sqscr"].ap()[:, 0:HID],
            T["toktmp"].ap()[:, 0:HID], AF.Square, accum_out=T["statc"].ap()[:, 4:5]),
          reads=["toktmp"], writes=["sqscr", "st4"])
        E("scalar", (lambda H=HID: (lambda s: s.activation(T["statc"].ap()[:, 4:5],
            T["statc"].ap()[:, 4:5], AF.Ln, bias=1e-6, scale=1.0 / H)))(),
          reads=["st4"], writes=["st4"])
        E("scalar", lambda s: s.activation(T["statc"].ap()[:, 4:5],
            T["statc"].ap()[:, 4:5], AF.Exp, scale=-0.5), reads=["st4"], writes=["st4"])
        E("vector", lambda v: v.reduce_max(T["statc"].ap()[:, 5:6],
            T["toktmp"].ap()[:, 0:HID], AX.X, apply_absolute_value=True),
          reads=["toktmp"], writes=["st5"])
        E("vector", lambda v: v.tensor_mul(T["statc"].ap()[:, 5:6],
            T["statc"].ap()[:, 4:5], T["statc"].ap()[:, 5:6]),
          reads=["st4", "st5"], writes=["st5"])
        E("vector", lambda v: v.tensor_scalar_max(T["statc"].ap()[:, 5:6],
            T["statc"].ap()[:, 5:6], 1e-5), reads=["st5"], writes=["st5"])
        E("scalar", lambda s: s.activation(T["statc"].ap()[:, 6:7],
            T["statc"].ap()[:, 5:6], AF.Ln), reads=["st5"], writes=["st6"])
        E("scalar", lambda s: s.activation(T["statc"].ap()[:, 5:6],
            T["statc"].ap()[:, 6:7], AF.Exp, scale=-1.0),
          reads=["st6"], writes=["st5"])
        E("vector", lambda v: v.tensor_mul(T["statc"].ap()[:, 8:9],
            T["statc"].ap()[:, 4:5], T["statc"].ap()[:, 5:6]),
          reads=["st4", "st5"], writes=["st8"])
        E("vector", lambda v: v.tensor_scalar(T["statc"].ap()[:, 8:9],
            T["statc"].ap()[:, 8:9], 127.0, None, ALU.mult),
          reads=["st8"], writes=["st8"])
        E("vector", lambda v: v.tensor_scalar(T["fq1"].ap()[:, 0:HID],
            T["toktmp"].ap()[:, 0:HID], T["statc"].ap()[:, 8:9],
            T["Ccol"].ap()[:, 0:1], ALU.mult, ALU.add),
          reads=["toktmp", "st8", "Ccol"], writes=["fq1"])
        E("vector", lambda v: v.tensor_add(T["statc"].ap()[:, 9:10],
            T["statc"].ap()[:, 7:8], T["statc"].ap()[:, 6:7]),
          reads=["lnw1col", "st6"], writes=["st9"])
        E("scalar", lambda s: s.activation(T["statc"].ap()[:, 9:10],
            T["statc"].ap()[:, 9:10], AF.Exp), reads=["st9"], writes=["st9"])
        E("vector", lambda v: v.tensor_mul(T["statc"].ap()[:, 9:10],
            T["statc"].ap()[:, 9:10], T["statc"].ap()[:, 16:17]),
          reads=["st9", "r127"], writes=["st9"])
        E("vector", lambda v: v.tensor_scalar(T["toktmp"].ap()[:, 0:HID],
            T["fq1"].ap()[:, 0:HID], T["Ccol"].ap()[:, 0:1],
            T["statc"].ap()[:, 9:10], ALU.subtract, ALU.mult),
          reads=["fq1", "Ccol", "st9"], writes=["toktmp"])
        for c in range(nhc):
            E("tensor", (lambda t=t, c=c: (lambda te: te.matmul(
                PS[0:128, 2048+128*c:2048+128*(c+1)],
                T["toktmp"].ap()[:, 128*c:128*(c+1)], T["ident"].ap()[:, :],
                start=True, stop=True)))(), reads=["toktmp", "ident"],
              writes=psnames(2048+128*c, 2048+128*(c+1)))
        E("vector", (lambda t=t: (lambda v: v.tensor_copy(
            T["KF"].ap().rearrange("p (c s) -> p c s", s=S)[:, :, 128*t:128*(t+1)],
            PS[0:128, 2048:2048+128*nhc].rearrange("p (c s) -> p c s", s=128))))(),
          reads=psnames(2048, 2048+128*nhc), writes=["KF"])

    for sp in range(nst):
        tg0 = SW * sp
        for ocg in range((nfc + 3) // 4):
            ocs = list(range(4 * ocg, min(4 * ocg + 4, nfc)))
            for i in range(nhc):
                wsn = "W1s" if i % 2 == 0 else "W2s"
                dma("sync", ap(wsn)[:, 0:128*len(ocs)],
                    ap("w1i_d")[128*i:128*(i+1), 128*ocs[0]:128*ocs[0]+128*len(ocs)],
                    "d_w1", writes=[wsn])
                for j, oc in enumerate(ocs):
                    E("tensor", (lambda i=i, j=j, tg0=tg0, wsn=wsn: (lambda te: te.matmul(
                        PS[0:128, 2048+512*j:2048+512*j+SW],
                        T[wsn].ap()[:, 128*j:128*(j+1)],
                        T["KF"].ap()[:, S*i+tg0:S*i+tg0+SW],
                        start=(i == 0), stop=(i == nhc-1))))(),
                      reads=[wsn, "KF"], writes=[f"psB{4+j}"])
            for j, oc in enumerate(ocs):
                E("scalar", (lambda j=j, oc=oc: (lambda s: s.activation(
                    T["QF"].ap()[:, SW*oc:SW*oc+SW], PS[0:128, 2048+512*j:2048+512*j+SW],
                    AF.Gelu, bias=T["b1_s"].ap()[:, oc:oc+1])))(),
                  reads=[f"psB{4+j}", "b1_s"], writes=["QF"])
        for oc in range(nfc):
            E("scalar", (lambda oc=oc: (lambda s: s.activation(
                T["sqscr"].ap()[:, 0:SW], T["QF"].ap()[:, SW*oc:SW*oc+SW],
                AF.Square)))(), reads=["QF"], writes=["sqscr"])
            E("tensor", (lambda oc=oc: (lambda te: te.matmul(
                PS[0:1, 3584:3584+SW], T["ones_colb"].ap()[:, 0:1],
                T["sqscr"].ap()[:, 0:SW], start=(oc == 0), stop=(oc == nfc-1))))(),
              reads=["sqscr", "ones_colb"], writes=["psB7"])
            if oc == 0:
                E("vector", lambda v: v.tensor_copy(
                    T["sqscr"].ap()[:, 512:512+SW], T["sqscr"].ap()[:, 0:SW]),
                  reads=["sqscr"], writes=["sqmax"])
            else:
                E("vector", lambda v: v.tensor_max(
                    T["sqscr"].ap()[:, 512:512+SW], T["sqscr"].ap()[:, 512:512+SW],
                    T["sqscr"].ap()[:, 0:SW]), reads=["sqscr", "sqmax"],
                  writes=["sqmax"])
        nb = SW // 128
        E("scalar", lambda s: s.activation(T["rowf"].ap()[0:1, 0:SW],
            PS[0:1, 3584:3584+SW], AF.Copy), reads=["psB7"], writes=["rowf"])
        for bq in range(nb):
            E("tensor", (lambda bq=bq: (lambda te: te.matmul(
                PS[0:128, 3648+bq:3648+bq+1],
                T["rowf"].ap()[0:1, 128*bq:128*(bq+1)],
                T["scal"].ap()[0:1, 2:3], start=True, stop=True)))(),
              reads=["rowf"], writes=["psB7"])
            E("tensor", (lambda bq=bq: (lambda te: te.matmul(
                PS[0:128, 2048+128*bq:2048+128*(bq+1)],
                T["sqscr"].ap()[:, 512+128*bq:512+128*(bq+1)], T["ident"].ap()[:, :],
                start=True, stop=True)))(), reads=["sqmax", "ident"],
              writes=["psB4"])
            E("vector", (lambda bq=bq: (lambda v: v.tensor_reduce(
                T["hcol"].ap()[:, bq:bq+1], PS[0:128, 2048+128*bq:2048+128*(bq+1)],
                AX.X, ALU.max)))(), reads=["psB4"], writes=["hcol_am"])
        E("vector", (lambda nb=nb: (lambda v: v.tensor_copy(
            T["hcol"].ap()[:, 4:4+nb], PS[0:128, 3648:3648+nb])))(),
          reads=["psB7"], writes=["hcol_sq"])
        E("scalar", (lambda FF=FF, nb=nb: (lambda s: s.activation(T["hcol"].ap()[:, 4:4+nb],
            T["hcol"].ap()[:, 4:4+nb], AF.Ln, bias=1e-6, scale=1.0 / FF)))(),
          reads=["hcol_sq"], writes=["hcol_sq"])
        E("scalar", (lambda nb=nb: (lambda s: s.activation(T["hcol"].ap()[:, 0:nb],
            T["hcol"].ap()[:, 0:nb], AF.Ln)))(), reads=["hcol_am"], writes=["hcol_am"])
        E("vector", (lambda nb=nb: (lambda v: v.tensor_tensor(T["hcol"].ap()[:, 0:nb],
            T["hcol"].ap()[:, 0:nb], T["hcol"].ap()[:, 4:4+nb], ALU.subtract)))(),
          reads=["hcol_am", "hcol_sq"], writes=["hcol_am"])
        E("vector", (lambda nb=nb: (lambda v: v.tensor_scalar_max(
            T["hcol"].ap()[:, 0:nb], T["hcol"].ap()[:, 0:nb], -23.025850929940457)))(),
          reads=["hcol_am"], writes=["hcol_am"])
        E("scalar", (lambda nb=nb: (lambda s: s.activation(T["hcol"].ap()[:, 4:4+nb],
            T["hcol"].ap()[:, 4:4+nb], AF.Exp, scale=-0.5)))(),
          reads=["hcol_sq"], writes=["hcol_sq"])
        E("scalar", (lambda nb=nb: (lambda s: s.activation(
            T["hcol"].ap()[:, 8:8+nb], T["hcol"].ap()[:, 0:nb],
            AF.Exp, scale=0.5, bias=T["statc"].ap()[:, 15:16])))(),
          reads=["hcol_am", "lnw2col"], writes=["hcol_cs"])
        E("vector", (lambda nb=nb: (lambda v: v.tensor_scalar(
            T["hcol"].ap()[:, 8:8+nb], T["hcol"].ap()[:, 8:8+nb],
            T["statc"].ap()[:, 16:17], None, ALU.mult)))(),
          reads=["hcol_cs", "r127"], writes=["hcol_cs"])
        E("scalar", (lambda nb=nb: (lambda s: s.activation(
            T["hcol"].ap()[:, 12:12+nb], T["hcol"].ap()[:, 0:nb],
            AF.Exp, scale=-0.5)))(),
          reads=["hcol_am"], writes=["hcol_hs"])
        E("vector", (lambda nb=nb: (lambda v: v.tensor_mul(
            T["hcol"].ap()[:, 12:12+nb], T["hcol"].ap()[:, 12:12+nb],
            T["hcol"].ap()[:, 4:4+nb])))(),
          reads=["hcol_hs", "hcol_sq"], writes=["hcol_hs"])
        E("vector", (lambda nb=nb: (lambda v: v.tensor_scalar(
            T["hcol"].ap()[:, 12:12+nb], T["hcol"].ap()[:, 12:12+nb],
            127.0, None, ALU.mult)))(), reads=["hcol_hs"], writes=["hcol_hs"])
        for c in range(nb):
            dma("sync", ap("row_d")[2:3, 128*c:128*(c+1)], ap("hcol")[:, 12+c:13+c],
                "d_h1", reads=["hcol_hs"], writes=[f"rdlane2_{c}"])
            dma("sync", ap("row_d")[3:4, 128*c:128*(c+1)], ap("hcol")[:, 8+c:9+c],
                "d_h3", reads=["hcol_cs"], writes=[f"rdlane3_{c}"])
        dma("sync", ap("fq1")[:, 0:SW],
            ap("row_d")[2:3, 0:SW].partition_broadcast(128), "d_h2",
            reads=[f"rdlane2_{c}" for c in range(nb)], writes=["fq1"])
        dma("gpsimd", ap("zbt")[:, 0:SW],
            ap("row_d")[3:4, 0:SW].partition_broadcast(128), "d_h4",
            reads=[f"rdlane3_{c}" for c in range(nb)], writes=["zbt"])
        for oc in range(nfc):
            E("vector", (lambda oc=oc: (lambda v: v.scalar_tensor_tensor(
                T["fq2"].ap()[:, 0:SW], T["QF"].ap()[:, SW*oc:SW*oc+SW],
                T["Ccol"].ap()[:, 0:1], T["fq1"].ap()[:, 0:SW],
                ALU.bypass, ALU.mult)))(),
              reads=["QF", "fq1"], writes=["fq2"])
            E("vector", (lambda oc=oc: (lambda v: v.tensor_scalar(
                T["fq2"].ap()[:, 0:SW], T["fq2"].ap()[:, 0:SW],
                T["Ccol"].ap()[:, 0:1], None, ALU.add)))(),
              reads=["fq2", "Ccol"], writes=["fq2"])
            E("vector", (lambda oc=oc: (lambda v: v.tensor_scalar(
                T["VF"].ap()[:, SW*oc:SW*oc+SW], T["fq2"].ap()[:, 0:SW],
                T["Ccol"].ap()[:, 0:1], None, ALU.subtract)))(),
              reads=["fq2", "Ccol"], writes=["VF"])
        for i2g in range((nhc + 3) // 4):
            i2s = list(range(4 * i2g, min(4 * i2g + 4, nhc)))
            for o in range(nfc):
                wsn = "W2s" if o % 2 == 0 else "W1s"
                dma("sync", ap(wsn)[:, 0:128*len(i2s)],
                    ap("w2i_d")[128*o:128*(o+1), 128*i2s[0]:128*i2s[0]+128*len(i2s)],
                    "d_w2", writes=[wsn])
                for j, i2 in enumerate(i2s):
                    E("tensor", (lambda o=o, j=j, tg0=tg0, wsn=wsn: (lambda te: te.matmul(
                        PS[0:128, 2048+512*j:2048+512*j+SW],
                        T[wsn].ap()[:, 128*j:128*(j+1)],
                        T["VF"].ap()[:, SW*o:SW*o+SW],
                        start=(o == 0), stop=(o == nfc-1))))(),
                      reads=[wsn, "VF"], writes=[f"psB{4+j}"])
            for j, i2 in enumerate(i2s):
                E("vector", (lambda j=j: (lambda v: v.tensor_tensor(
                    T["fq2"].ap()[:, 0:SW], PS[0:128, 2048+512*j:2048+512*j+SW],
                    T["zbt"].ap()[0:128, 0:SW], ALU.mult)))(),
                  reads=[f"psB{4+j}", "zbt"], writes=["fq2"])
                E("vector", (lambda i2=i2: (lambda v: v.tensor_scalar_add(
                    T["XN1"].ap()[:, SW*i2:SW*i2+SW], T["fq2"].ap()[:, 0:SW],
                    T["b2_s"].ap()[:, i2:i2+1])))(),
                  reads=["fq2", "b2_s"], writes=["XN1"])
        for tt in range(SW // 128):
            t = (tg0 // 128) + tt
            for c in range(nhc):
                E("tensor", (lambda tt=tt, c=c: (lambda te: te.matmul(
                    PS[0:128, 2048+128*c:2048+128*(c+1)],
                    T["XN1"].ap()[:, SW*c+128*tt:SW*c+128*(tt+1)], T["ident"].ap()[:, :],
                    start=True, stop=True)))(), reads=["XN1", "ident"],
                  writes=psnames(2048+128*c, 2048+128*(c+1)))
            E("vector", lambda v: v.tensor_copy(T["fq1"].ap()[:, 0:HID],
                PS[0:128, 2048:2048+HID]), reads=psnames(2048, 2048+HID), writes=["fq1"])
            ln_tok_chain("fq1", (lambda: T["fq1"].ap()[:, 0:HID]), 1e-5,
                         "toktmp", (lambda: T["toktmp"].ap()[:, 0:HID]), HID)
            dma("sync", ap("x2t")[:, :], ap("x2_d")[xoff+128*t:xoff+128*(t+1), :],
                "d_x3", reads=[f"x2d_{b}_{t}"], writes=["x2t"])
            E("vector", lambda v: v.tensor_add(T["xtile"].ap()[:, :],
                T["toktmp"].ap()[:, 0:HID], T["x2t"].ap()[:, :]),
              reads=["toktmp", "x2t"], writes=["xtile"])
            dma("sync", ap("y")[xoff+128*t:xoff+128*(t+1), :], ap("xtile")[:, :],
                "d_out", reads=["xtile"], writes=["yout"])


def emit_block(k):
    import contextlib
    plan = k.plan
    nc = k.nc
    import os
    steps = plan.finalize(keep_all=bool(os.environ.get("KEEP_ALL_SEMS")))
    with contextlib.ExitStack() as st:
        sems = {}
        for name in plan.cnt:
            sems[name] = st.enter_context(nc.semaphore(name))
        with nc.Block() as block:
            def mk(eng_name):
                def body(eng):
                    for s_ in steps[eng_name]:
                        if s_[0] == "wait":
                            eng.wait_ge(sems[s_[1]], s_[2])
                        else:
                            ins = s_[1](eng)
                            if s_[2] is not None:
                                ins.then_inc(sems[s_[2]], 16 if plan.is_dma[s_[2]] else 1)
                return body
            block.sync(mk("sync"))
            block.gpsimd(mk("gpsimd"))
            block.tensor(mk("tensor"))
            block.vector(mk("vector"))
            block.scalar(mk("scalar"))
    return k.nc


def host_inputs(cfg, ins, core):
    S = cfg["S"]; NH = cfg["NH"]; HID = 64 * NH; FF = cfg["FF"]; BPC = cfg["BPC"]
    nt = S // 128; nhc = HID // 128; nfc = FF // 128
    f32 = np.float32
    d = {}
    x = np.asarray(ins["L0_attn_x"], f32)
    d["x"] = np.ascontiguousarray(x[BPC*core:BPC*(core+1)].reshape(BPC*S, HID))
    for n, nm in (("Wq", "wq"), ("Wk", "wk"), ("Wv", "wv"), ("Wo", "wo")):
        d[nm] = np.ascontiguousarray(np.asarray(ins[f"L0_attn_{n}"], f32))
    for n, nm, w in (("bq", "bq2", nhc), ("bk", "bk2", nhc), ("bv", "bv2", nhc),
                     ("bo", "bo2", nhc)):
        d[nm] = np.ascontiguousarray(np.asarray(ins[f"L0_attn_{n}"], f32).reshape(w, 128).T)
    d["b12"] = np.ascontiguousarray(np.asarray(ins["L0_ffn_b1"], f32).reshape(nfc, 128).T)
    d["b22"] = np.ascontiguousarray(np.asarray(ins["L0_ffn_b2"], f32).reshape(nhc, 128).T)
    tk = np.asarray(ins["L0_attn_table_k"], f32)
    tv = np.asarray(ins["L0_attn_table_v"], f32)
    tkp = np.zeros((64, 65), f32)
    tkp[:, 0:NJ] = tk.T
    tkp[:, 32] = tk.T[:, 2 * MAXREL]
    d["tkT2"] = np.ascontiguousarray(np.concatenate([tkp, tkp], 0))
    d["tv"] = np.ascontiguousarray(tv)
    d["tv08"] = np.ascontiguousarray(tv[[0, 2 * MAXREL], :])
    d["w1T"] = np.ascontiguousarray(np.asarray(ins["L0_ffn_w1"], f32).T)
    d["w2T"] = np.ascontiguousarray(np.asarray(ins["L0_ffn_w2"], f32).T)
    d["c_ident"] = np.eye(128, dtype=f32).astype(bfnp)
    d["c_identd"] = np.tile(np.eye(64, dtype=f32), (2, 1)).astype(bfnp)
    coef = make_coefs(a2x_mode=True)
    d["c_coef"] = np.ascontiguousarray(coef.transpose(1, 0, 2).reshape(128, NJ * WWIN)).astype(bfnp)
    coefw = make_coefs(a2x_mode=False)
    d["c_coefw"] = np.ascontiguousarray(coefw.transpose(1, 0, 2).reshape(128, NJ * WWIN)).astype(bfnp)
    d["c_ones_row"] = np.ones((1, 128), f32).astype(bfnp)
    d["c_ones_colb"] = np.ones((128, 1), f32).astype(bfnp)
    d["c_ones_colf"] = np.ones((128, 1), f32)
    sel = np.zeros((2, nt * nt), f32)
    for s in range(nt):
        for c in range(nt):
            if c < s:
                sel[0, s * nt + c] = 1.0
            elif c > s:
                sel[1, s * nt + c] = 1.0
    d["c_sel"] = sel.astype(bfnp)
    vind = np.zeros((128, nt * 33), f32)
    for c in range(nt):
        vind[:, c * 33 + c] = 1.0
        vind[:, c * 33 + 32] = 1.0
    d["c_vind"] = vind.astype(bfnp)
    d["c_scal"] = np.ascontiguousarray(
        np.tile(np.array(SCAL_VALS(HID, FF), f32)[None, :], (128, 1)))
    d["c_p52"] = np.full((128, 1), 2.0 ** 52, f32)
    return d


_CFG_FULL = {"S": 2048, "NH": 16, "FF": 4096, "BPC": 2}


def run_full(ins, ncores=8, cfg=None):
    from concourse.bass_utils import run_bass_kernel_spmd
    cfg = cfg or _CFG_FULL
    k = build_program(cfg)
    nc = emit_block(k)
    in_maps = [host_inputs(cfg, ins, c) for c in range(ncores)]
    res = run_bass_kernel_spmd(nc, in_maps, core_ids=list(range(ncores)))
    BPC, S, HID = cfg["BPC"], cfg["S"], 64 * cfg["NH"]
    out = np.concatenate([r["y"].reshape(BPC, S, HID) for r in res.results], axis=0)
    return out


NCORES = 8


def kernel(**inputs):
    from concourse.bass_utils import run_bass_kernel_spmd
    cfg = _CFG_FULL
    k = build_program(cfg)
    nc = emit_block(k)
    in_maps = [host_inputs(cfg, inputs, c) for c in range(NCORES)]
    res = run_bass_kernel_spmd(nc, in_maps, core_ids=list(range(NCORES)))
    BPC, S, HID = cfg["BPC"], cfg["S"], 64 * cfg["NH"]
    out = np.concatenate([r["y"].reshape(BPC, S, HID) for r in res.results], axis=0)
    return out


# version 18
# speedup vs baseline: 1.1931x; 1.0332x over previous
import numpy as np
import ml_dtypes

import concourse.bass as bass
import concourse.mybir as mybir

F32 = mybir.dt.float32
BF16 = mybir.dt.bfloat16
ALU = mybir.AluOpType
AF = mybir.ActivationFunctionType
AX = mybir.AxisListType
bfnp = ml_dtypes.bfloat16

MAXREL = 4
NJ = 9
LPAD, RPAD = 64, 4
WWIN = LPAD + 128 + RPAD

ENGINES = ("sync", "gpsimd", "tensor", "vector", "scalar")


def SCAL_VALS(HID, FF=4096):
    import math
    ln127 = math.log(127.0)
    return [0.0, 0.5, 1.0, -1.0, -0.125, 1e-5, 1.0 / HID, 1e-6,
            ln127, -ln127, 0.125, -0.5, 1.0 / FF, 1.0 / (HID * FF), 8388608.0, 536870912.0,
            805306368.0, 1.5, 12582912.0, 0.25, 127.0, 1.0 / 127.0]


def make_coefs(a2x_mode=False):
    coef = np.zeros((NJ, 128, WWIN), np.float32)
    p = np.arange(128)[:, None]
    kw = np.arange(WWIN)[None, :] - LPAD
    jj = np.clip(kw - p, -MAXREL, MAXREL) + MAXREL
    main = (kw >= 0) & (kw < 128)
    left = kw < 0
    right = kw >= 128
    for j in range(NJ):
        coef[j][main & (jj == j)] += 1.0
        if 1 <= j <= 3:
            coef[j][left & (jj == j)] += 1.0
            coef[0][left & (jj == j)] -= 1.0
        if 5 <= j <= 7:
            coef[j][right & (jj == j)] += 1.0
            coef[8][right & (jj == j)] -= 1.0
    if a2x_mode:
        coef[8] -= np.broadcast_to(main, (128, WWIN)).astype(np.float32)
    return coef


class Plan:

    def __init__(self):
        self.steps = {e: [] for e in ENGINES}
        self.cnt = {}
        self.is_dma = {}
        self.last_write = {}
        self.readers = {}
        self.waited = {e: {} for e in ENGINES}
        self.eng_sem = {e: f"S_{e}" for e in ENGINES}
        self.dma_gen = {}
        self.sem_prev = {}
        for e in ENGINES:
            self.cnt[self.eng_sem[e]] = 0
            self.is_dma[self.eng_sem[e]] = False

    def dma_sem(self, name):
        if name not in self.cnt:
            self.cnt[name] = 0
            self.is_dma[name] = True
        return name

    def _wait(self, eng, sem, v):
        if v <= 0:
            return
        if self.waited[eng].get(sem, 0) >= v:
            return
        self.waited[eng][sem] = v
        self.steps[eng].append(("wait", sem, v))

    def emit(self, eng, fn, reads=(), writes=(), dma=None):
        for b in reads:
            if b in self.last_write:
                self._wait(eng, *self.last_write[b])
        for b in writes:
            if b in self.last_write:
                self._wait(eng, *self.last_write[b])
            for rs in self.readers.get(b, ()):
                self._wait(eng, *rs)
        if dma is not None:
            gen = self.dma_gen.setdefault(dma, 0)
            self.dma_gen[dma] = gen + 1
            sem = self.dma_sem(f"{dma}.{gen % 3}")
            if self.cnt[sem] > 0:
                self._wait(eng, sem, self.cnt[sem])
            self.cnt[sem] += 16
        else:
            sem = self.eng_sem[eng]
            self.cnt[sem] += 1
        v = self.cnt[sem]
        self.steps[eng].append(("op", fn, sem, v))
        for b in reads:
            self.readers.setdefault(b, []).append((sem, v))
        for b in writes:
            self.last_write[b] = (sem, v)
            self.readers[b] = []
        return (sem, v)

    def barrier_read(self, eng, bufs):
        for b in bufs:
            if b in self.last_write:
                self._wait(eng, *self.last_write[b])

    def drain_all_dma(self, eng):
        for sem, n in self.cnt.items():
            if self.is_dma[sem] and n > 0:
                self._wait(eng, sem, n)

    def finalize(self, keep_all=False):
        if keep_all:
            out = {}
            for e in ENGINES:
                lst = []
                for st in self.steps[e]:
                    if st[0] == "wait":
                        lst.append(st)
                    else:
                        lst.append(("op", st[1], st[2], 0))
                out[e] = lst
            return out
        needed = {}
        for e in ENGINES:
            for st in self.steps[e]:
                if st[0] == "wait":
                    needed.setdefault(st[1], set()).add(st[2])
        remap = {}
        for sem, vals in needed.items():
            sv = sorted(vals)
            remap[sem] = {v: i + 1 for i, v in enumerate(sv)}
        out = {}
        for e in ENGINES:
            lst = []
            for st in self.steps[e]:
                if st[0] == "wait":
                    sem, v = st[1], st[2]
                    if self.is_dma[sem]:
                        lst.append(("wait", sem, v))
                    else:
                        lst.append(("wait", sem, remap[sem][v]))
                else:
                    _, fn, sem, v = st
                    keep = self.is_dma[sem] or (sem in remap and v in remap[sem])
                    lst.append(("op", fn, sem if keep else None, 0))
            out[e] = lst
        return out


def build_consts(cfg):
    c = {}
    c["ident"] = np.eye(128, dtype=np.float32).astype(bfnp)
    coef = make_coefs()
    c["coef"] = np.ascontiguousarray(
        coef.transpose(1, 0, 2).reshape(128, NJ * WWIN)).astype(bfnp)
    c["ones_row"] = np.ones((1, 128), np.float32).astype(bfnp)
    c["ones_col"] = np.ones((128, 1), np.float32).astype(bfnp)
    c["ones_col_f32"] = np.ones((128, 1), np.float32)
    nt = cfg["S"] // 128
    sel = np.zeros((2, nt * nt), np.float32)
    for s in range(nt):
        for cc in range(nt):
            if cc < s:
                sel[0, s * nt + cc] = 1.0
            elif cc > s:
                sel[1, s * nt + cc] = 1.0
    c["sel"] = sel.astype(bfnp)
    return c


class K:

    def __init__(self, cfg):
        self.cfg = cfg
        self.nc = bass.Bass("TRN2", target_bir_lowering=False)
        self.plan = Plan()
        self.T = {}
        self.uid = 0

    def u(self, p="t"):
        self.uid += 1
        return f"{p}{self.uid}"

    def dram_in(self, name, shape, dt=F32):
        h = self.nc.dram_tensor(name, list(shape), dt, kind="ExternalInput")
        self.T[name] = h
        return h

    def dram_out(self, name, shape, dt=F32):
        h = self.nc.dram_tensor(name, list(shape), dt, kind="ExternalOutput")
        self.T[name] = h
        return h

    def dram_tmp(self, name, shape, dt=F32):
        h = self.nc.dram_tensor(name, list(shape), dt, kind="Internal")
        self.T[name] = h
        return h

    def sbuf(self, name, shape, dt=BF16):
        h = self.nc.alloc_sbuf_tensor(name, list(shape), dt)
        self.T[name] = h
        return h

    def psum(self, name, shape, dt=F32):
        h = self.nc.alloc_psum_tensor(name, list(shape), dt)
        self.T[name] = h
        return h




def build_program(cfg):
    S = cfg["S"]; NH = cfg["NH"]; HID = 64 * NH; FF = cfg["FF"]; BPC = cfg["BPC"]
    nt = S // 128; nhc = HID // 128; nfc = FF // 128
    NTOK = BPC * S
    nst = S // 512 if S >= 512 else 1
    SW = min(512, S)
    k = K(cfg)
    nc, plan, T = k.nc, k.plan, k.T
    ap = lambda n: T[n].ap()
    E = plan.emit

    k.dram_in("x", [NTOK, HID])
    for n in ("wq", "wk", "wv", "wo"):
        k.dram_in(n, [HID, HID])
    for n, w in (("bq2", nhc), ("bk2", nhc), ("bv2", nhc), ("bo2", nhc),
                 ("b12", nfc), ("b22", nhc)):
        k.dram_in(n, [128, w])
    k.dram_in("tkT2", [128, 65]); k.dram_in("tv", [NJ, 64]); k.dram_in("tv08", [2, 64])
    k.dram_in("w1T", [HID, FF]); k.dram_in("w2T", [FF, HID])
    k.dram_in("c_ident", [128, 128], BF16)
    k.dram_in("c_identd", [128, 64], BF16)
    k.dram_in("c_coef", [128, NJ * WWIN], BF16)
    k.dram_in("c_coefw", [128, NJ * WWIN], BF16)
    k.dram_in("c_ones_row", [1, 128], BF16)
    k.dram_in("c_ones_colb", [128, 1], BF16)
    k.dram_in("c_ones_colf", [128, 1], F32)
    k.dram_in("c_sel", [2, nt * nt], BF16)
    k.dram_in("c_vind", [128, nt * 33], BF16)
    k.dram_in("c_scal", [128, 22], F32)
    k.dram_in("c_p52", [128, 1], F32)
    k.dram_out("y", [NTOK, HID])
    k.dram_tmp("w1i_d", [HID, FF], BF16)
    k.dram_tmp("w2i_d", [FF, HID], BF16)
    k.dram_tmp("row_d", [8, max(S, 2048)], F32)
    k.dram_tmp("x2_d", [NTOK, HID], BF16)

    sb = k.sbuf
    sb("ident", [128, 128]); sb("identd", [128, 64])
    sb("coef", [128, NJ * WWIN]); sb("coefw", [128, NJ * WWIN]); sb("ones_row", [1, 128])
    sb("ones_colb", [128, 1]); sb("sel", [2, nt * nt]); sb("vind", [128, nt * 33])
    sb("tvsel", [16, nt * 64])
    sb("ones_colf", [128, 1], F32)
    sb("scal", [128, 22], F32)
    sb("p52", [128, 1], F32)
    sb("Ccol", [128, 1], F32)
    sb("tkT2_b", [128, 65])
    sb("KQ0a", [128, 128]); sb("KQ1a", [128, 128])
    sb("KQ0b", [128, 128]); sb("KQ1b", [128, 128])
    sb("s08row", [1, S]); sb("tv_b", [NJ, 64]); sb("tv08_b", [2, 64])
    for n, w in (("bq_s", nhc), ("bk_s", nhc), ("bv_s", nhc), ("bo_s", nhc),
                 ("b1_s", nfc), ("b2_s", nhc)):
        sb(n, [128, w], F32)
    sb("Wc0", [128, 128]); sb("Wc1", [128, 128])
    sb("XN1", [128, nhc * S])
    qvsz = max(nhc * S, nfc * min(512, S))
    sb("QF", [128, qvsz]); sb("KF", [128, nhc * S]); sb("VF", [128, qvsz])
    for r in range(3):
        sb(f"F{r}", [128, S])
    for r in range(4):
        sb(f"A2X{r}", [128, WWIN])
    VW = 64 + 32 + 1
    sb("vpm", [128, nt * VW])
    sb("SattnT", [NJ, S])
    sb("Sqp", [128, nt * NJ], F32)
    sb("FBqp", [128, nt * NJ], F32)
    sb("FBT", [NJ, S])
    sb("Rrows", [16, S])
    sb("zbt", [128, S])
    sb("negM", [128, 16], F32)
    sb("ttr_scr", [128, WWIN], F32)
    sb("vj", [128, 8], F32)
    sb("slotD0", [128, WWIN]); sb("slotD1", [128, WWIN])
    sb("fwv0", [128, WWIN]); sb("fwv1", [128, WWIN])
    sb("rowf", [1, 512], F32)
    sb("x2t", [128, HID])
    sb("xtile", [128, HID], F32)
    sb("toktmp", [128, max(HID, 1024)])
    sb("sqscr", [128, max(HID, 1024)])
    sb("statc", [128, 18], F32)
    sb("maxs", [128, 512], F32)
    sb("hcol", [128, 16], F32)
    sb("W1s", [128, 512])
    sb("W2s", [128, 512])
    sb("fq1", [128, max(HID, 1024)], F32)
    sb("fq2", [128, 512], F32); sb("x2t2", [128, 512], F32)
    k.psum("PS", [128, 4096], F32)
    PS = T["PS"].ap()

    def dma(q, dst, src, sem, reads=(), writes=()):
        E(q, (lambda d=dst, s=src: (lambda eng: eng.dma_start(out=d, in_=s)))(),
          reads=reads, writes=writes, dma=sem)

    for name, dst in (("c_ident", "ident"), ("c_identd", "identd"), ("c_coef", "coef"), ("c_coefw", "coefw"),
                      ("c_ones_row", "ones_row"), ("c_ones_colb", "ones_colb"),
                      ("c_sel", "sel"), ("c_vind", "vind")):
        dma("sync", ap(dst)[:, :], ap(name)[:, :], "d_c", writes=[dst])
    dma("sync", ap("ones_colf")[:, :], ap("c_ones_colf")[:, :], "d_c", writes=["ones_colf"])
    dma("sync", ap("scal")[:, :], ap("c_scal")[:, :], "d_c", writes=["scal"])
    for _i in range(2):
        k.dram_tmp(f"a2xd{_i}", [1, 128 * (WWIN + 2)], BF16)
        k.dram_tmp(f"fwd{_i}", [1, 128 * (WWIN + 1)], BF16)
    plan.emit("vector", lambda v: v.memset(T["slotD0"].ap()[:, :], 0.0),
              writes=["slotD0"])
    for _i in range(2):
        _v198 = ap(f"a2xd{_i}")[0:1, :].rearrange("o (p w) -> (o p) w", w=WWIN + 2)
        dma("sync", _v198[:, 0:WWIN], T["slotD0"].ap()[:, 0:WWIN],
            "d_az", reads=["slotD0"], writes=[f"a2xz{_i}"])
        dma("sync", _v198[:, WWIN:WWIN + 2], T["slotD0"].ap()[:, 0:2],
            "d_az", reads=["slotD0"], writes=[f"a2xz{_i}b"])
    for _nm in ("KQ0a", "KQ1a", "KQ0b", "KQ1b"):
        plan.emit("vector", (lambda _nm=_nm: (lambda v: v.memset(T[_nm].ap()[:, :], 0.0)))(),
                  writes=[f"{_nm}_z"])
    dma("sync", ap("p52")[:, :], ap("c_p52")[:, :], "d_c", writes=["p52"])
    for ci, cv in enumerate(SCAL_VALS(HID, FF)):
        nc.const_aps.register(F32, cv, T["scal"].ap()[:, ci:ci+1])
    E("vector", lambda v: v.tensor_scalar_add(T["statc"].ap()[:, 0:1],
        T["p52"].ap()[:, 0:1], 8388608.0), reads=["p52"], writes=["st0"])
    E("vector", lambda v: v.tensor_sub(T["statc"].ap()[:, 1:2],
        T["statc"].ap()[:, 0:1], T["p52"].ap()[:, 0:1]),
      reads=["st0", "p52"], writes=["st1"])
    E("vector", lambda v: v.tensor_scalar_mul(T["statc"].ap()[:, 2:3],
        T["statc"].ap()[:, 1:2], 805306368.0), reads=["st1"], writes=["st2"])
    E("vector", lambda v: v.tensor_scalar_mul(T["statc"].ap()[:, 1:2],
        T["statc"].ap()[:, 1:2], 1.5), reads=["st1"], writes=["st1"])
    E("vector", lambda v: v.tensor_sub(T["statc"].ap()[:, 2:3],
        T["statc"].ap()[:, 2:3], T["statc"].ap()[:, 1:2]),
      reads=["st2", "st1"], writes=["st2"])
    E("vector", lambda v: v.tensor_scalar_add(T["Ccol"].ap()[:, 0:1],
        T["statc"].ap()[:, 2:3], 12582912.0), reads=["st2"], writes=["Ccol"])
    for name, dst in (("bq2", "bq_s"), ("bk2", "bk_s"), ("bv2", "bv_s"),
                      ("bo2", "bo_s"), ("b12", "b1_s"), ("b22", "b2_s")):
        dma("sync", ap(dst)[:, :], ap(name)[:, :], "d_c", writes=[dst])
    for name, dst in (("tkT2", "tkT2_b"), ("tv", "tv_b"), ("tv08", "tv08_b")):
        dma("gpsimd", ap(dst)[:, :], ap(name)[:, :], "d_cg", writes=[dst])
    E("vector", lambda v: v.tensor_sub(T["tkT2_b"].ap()[:, 64:65],
        T["tkT2_b"].ap()[:, 0:1], T["tkT2_b"].ap()[:, 32:33]),
      reads=["tkT2_b"], writes=["tkT2_b"])
    for c in range(nt):
        dma("sync", ap("vpm")[:, VW*c+64:VW*(c+1)], ap("c_vind")[:, 33*c:33*(c+1)],
            "d_c", writes=["vpm"])
    for s in range(nt):
        E("tensor", (lambda s=s: (lambda te: te.matmul(
            T["PS"].ap()[0:nt, 3584+64*(s % 8):3584+64*(s % 8)+64],
            T["sel"].ap()[:, nt*s:nt*(s+1)], T["tv08_b"].ap()[:, :],
            start=True, stop=True)))(), reads=["sel", "tv08_b"], writes=["psB7"])
        E("vector", (lambda s=s: (lambda v: v.tensor_copy(
            T["tvsel"].ap()[0:nt, 64*s:64*(s+1)],
            T["PS"].ap()[0:nt, 3584+64*(s % 8):3584+64*(s % 8)+64])))(),
          reads=["psB7"], writes=["tvsel"])

    def wquant(srcname, dstname, rows, cols, lane):
        nr = rows // 128
        npc = (cols + 1023) // 1024
        first = True
        for r in range(nr):
          for pc in range(npc):
            c0, c1 = 1024 * pc, min(1024 * (pc + 1), cols)
            ld = "fq1" if (r * npc + pc) % 2 == 0 else "xtile"
            dma("sync", ap(ld)[:, 0:c1-c0], ap(srcname)[128*r:128*(r+1), c0:c1],
                "d_wq", writes=[ld])
            E("vector", (lambda c0=c0, c1=c1, ld=ld: (lambda v: v.tensor_reduce(
                T["statc"].ap()[:, 1:2], T[ld].ap()[:, 0:c1-c0], AX.X, ALU.add,
                apply_absolute_value=True)))(), reads=[ld], writes=["st1"])
            if first:
                first = False
                E("vector", lambda v: v.tensor_copy(T["statc"].ap()[:, 0:1],
                                                    T["statc"].ap()[:, 1:2]),
                  reads=["st1"], writes=["st0"])
            else:
                E("vector", lambda v: v.tensor_add(T["statc"].ap()[:, 0:1],
                                                   T["statc"].ap()[:, 0:1],
                                                   T["statc"].ap()[:, 1:2]),
                  reads=["st1", "st0"], writes=["st0"])
        E("tensor", lambda t: t.matmul(PS[0:1, 3584:3585], T["ones_colf"].ap()[:, 0:1],
                                       T["statc"].ap()[:, 0:1], start=True, stop=True),
          reads=["st0", "ones_colf"], writes=["psB7"])
        E("scalar", (lambda rows=rows, cols=cols: (lambda s: s.activation(
            T["statc"].ap()[0:1, 2:3], PS[0:1, 3584:3585], AF.Copy,
            scale=1.0 / (rows * cols))))(), reads=["psB7"], writes=["st2"])
        E("vector", lambda v: v.tensor_scalar_max(T["statc"].ap()[0:1, 2:3],
                                                  T["statc"].ap()[0:1, 2:3], 1e-5),
          reads=["st2"], writes=["st2"])
        E("scalar", lambda s: s.activation(T["statc"].ap()[0:1, 3:4],
                                           T["statc"].ap()[0:1, 2:3], AF.Ln),
          reads=["st2"], writes=["st3"])
        E("scalar", lambda s: s.activation(T["statc"].ap()[0:1, 3:4],
                                           T["statc"].ap()[0:1, 3:4], AF.Exp, scale=-1.0),
          reads=["st3"], writes=["st3"])
        dma("sync", ap("row_d")[lane:lane+1, 0:1], ap("statc")[0:1, 2:3],
            "d_wm", reads=["st2"], writes=[f"rdlane{lane}"])
        dma("sync", ap("row_d")[lane+2:lane+3, 0:1], ap("statc")[0:1, 3:4],
            "d_wm", reads=["st3"], writes=[f"rdlane{lane+2}"])
        dma("sync", ap("statc")[:, 3:4],
            ap("row_d")[lane+2:lane+3, 0:1].partition_broadcast(128),
            "d_wm2", reads=[f"rdlane{lane+2}"], writes=["st3c"])
        npc2 = (cols + 511) // 512
        for r in range(nr):
          for pc in range(npc2):
            c0, c1 = 512 * pc, min(512 * (pc + 1), cols)
            w_ = c1 - c0
            i_ = r * npc2 + pc
            ld = "fq1" if i_ % 2 == 0 else "xtile"
            lq = "fq2" if i_ % 2 == 0 else "x2t2"
            so_ = 512 * (i_ % 2)
            dma("sync", ap(ld)[:, 0:w_], ap(srcname)[128*r:128*(r+1), c0:c1],
                "d_wq", writes=[ld])
            E("vector", (lambda w_=w_, ld=ld, lq=lq: (lambda v: v.tensor_scalar(
                T[lq].ap()[:, 0:w_], T[ld].ap()[:, 0:w_],
                T["statc"].ap()[:, 3:4], T["Ccol"].ap()[:, 0:1],
                ALU.mult, ALU.add)))(),
              reads=[ld, "st3c", "Ccol"], writes=[lq])
            E("vector", (lambda w_=w_, lq=lq: (lambda v: v.tensor_scalar(
                T[lq].ap()[:, 0:w_], T[lq].ap()[:, 0:w_],
                T["Ccol"].ap()[:, 0:1], None, ALU.subtract)))(),
              reads=[lq, "Ccol"], writes=[lq])
            E("vector", (lambda w_=w_, lq=lq, so_=so_: (lambda v: v.tensor_scalar(
                T["sqscr"].ap()[:, so_:so_+w_], T[lq].ap()[:, 0:w_],
                1.0, -1.0, ALU.min, ALU.max)))(), reads=[lq],
              writes=[f"sqscr{i_ % 2}"])
            dma("gpsimd" if i_ % 2 else "sync",
                ap(dstname)[128*r:128*(r+1), c0:c1], ap("sqscr")[:, so_:so_+w_],
                "d_wqo" + str(i_ % 2), reads=[f"sqscr{i_ % 2}"], writes=[f"{dstname}{r}"])
    wquant("w1T", "w1i_d", HID, FF, 0)
    wquant("w2T", "w2i_d", FF, HID, 1)
    plan.emit("vector", lambda v: v.tensor_copy(T["sqscr"].ap()[0:1, 0:1],
                                                T["sqscr"].ap()[0:1, 0:1]),
              writes=["sqscr", "sqscr0", "sqscr1", "fq1", "fq2", "xtile", "x2t2"])

    for n in ("wq", "wk", "wv", "wo"):
        k.dram_tmp(f"{n}b_d", [HID, HID], BF16)
        dma("gpsimd", ap(f"{n}b_d")[:, :], ap(n)[:, :], "d_wc", writes=[f"{n}b"])

    for b in range(BPC):
        xoff = b * S

        for t in range(nt):
            dma("sync", ap("xtile")[:, :], ap("x")[xoff+128*t:xoff+128*(t+1), :],
                "d_x", writes=["xtile"])
            E("scalar", lambda s: s.activation(T["sqscr"].ap()[:, 0:HID],
                T["xtile"].ap()[:, :], AF.Square, accum_out=T["statc"].ap()[:, 1:2]),
              reads=["xtile"], writes=["sqscr", "st1"])
            E("vector", lambda v: v.reduce_sum(T["statc"].ap()[:, 0:1],
                T["xtile"].ap()[:, :], AX.X), reads=["xtile"], writes=["st0"])
            E("vector", (lambda H=HID: (lambda v: v.tensor_scalar_mul(
                T["statc"].ap()[:, 0:1], T["statc"].ap()[:, 0:1], 1.0 / H)))(),
              reads=["st0"], writes=["st0"])
            E("vector", lambda v: v.tensor_mul(T["statc"].ap()[:, 2:3],
                T["statc"].ap()[:, 0:1], T["statc"].ap()[:, 0:1]),
              reads=["st0"], writes=["st2"])
            E("vector", (lambda H=HID: (lambda v: v.scalar_tensor_tensor(
                T["statc"].ap()[:, 3:4], T["statc"].ap()[:, 1:2], 1.0 / H,
                T["statc"].ap()[:, 2:3], ALU.mult, ALU.subtract)))(),
              reads=["st1", "st2"], writes=["st3"])
            E("vector", lambda v: v.tensor_scalar_add(
                T["statc"].ap()[:, 3:4], T["statc"].ap()[:, 3:4],
                (1e-5 - float(np.float32(1e-5)))), reads=["st3"], writes=["st3"])
            E("scalar", lambda s: s.activation(T["statc"].ap()[:, 3:4],
                T["statc"].ap()[:, 3:4], AF.Ln, bias=1e-5), reads=["st3"], writes=["st3"])
            E("scalar", lambda s: s.activation(T["statc"].ap()[:, 3:4],
                T["statc"].ap()[:, 3:4], AF.Exp, scale=-0.5), reads=["st3"], writes=["st3"])
            E("vector", lambda v: v.tensor_scalar(T["toktmp"].ap()[:, 0:HID],
                T["xtile"].ap()[:, :], T["statc"].ap()[:, 0:1],
                T["statc"].ap()[:, 3:4], ALU.subtract, ALU.mult),
              reads=["xtile", "st0", "st3"], writes=["toktmp"])
            for c in range(nhc):
                E("tensor", (lambda c=c: (lambda te: te.matmul(
                    PS[0:128, 2048+128*c:2048+128*(c+1)],
                    T["toktmp"].ap()[:, 128*c:128*(c+1)], T["ident"].ap()[:, :],
                    start=True, stop=True)))(), reads=["toktmp", "ident"],
                  writes=psnames(2048+128*c, 2048+128*(c+1)))
            E("vector", (lambda t=t: (lambda v: v.tensor_copy(
                T["XN1"].ap().rearrange("p (c s) -> p c s", s=S)[:, :, 128*t:128*(t+1)],
                PS[0:128, 2048:2048+128*nhc].rearrange("p (c s) -> p c s", s=128))))(),
              reads=psnames(2048, 2048+128*nhc), writes=["XN1"])

        def proj(wname, bname, dstname):
            wslot = [0]
            for oc in range(nhc):
                    for i in range(nhc):
                        ws = f"Wc{wslot[0] % 2}"; wslot[0] += 1
                        dma("sync", T[ws].ap()[:, 0:128],
                            ap(f"{wname}b_d")[128*i:128*(i+1), 128*oc:128*(oc+1)], "d_w",
                            reads=[f"{wname}b"], writes=[ws])
                        for st4 in range(nst):
                            pso = 2048 + 512 * st4
                            pname = ["psB4", "psB5", "psB6w0", "psB7"][st4]
                            E("tensor", (lambda oc=oc, st4=st4, i=i, pso=pso, ws=ws: (lambda te: te.matmul(
                                PS[0:128, pso:pso+SW], T[ws].ap()[:, 0:128],
                                T["XN1"].ap()[:, S*i+SW*st4:S*i+SW*(st4+1)],
                                start=(i == 0), stop=(i == nhc-1))))(),
                              reads=[ws, "XN1"], writes=[pname])
                    for st4 in range(nst):
                        pso = 2048 + 512 * st4
                        pname = ["psB4", "psB5", "psB6w0", "psB7"][st4]
                        E("scalar", (lambda oc=oc, st4=st4, pso=pso, bname=bname, dstname=dstname:
                            (lambda s: s.activation(
                                T[dstname].ap()[:, S*oc+SW*st4:S*oc+SW*(st4+1)],
                                PS[0:128, pso:pso+SW], AF.Identity,
                                bias=T[bname].ap()[:, oc:oc+1])))(),
                          reads=[pname, bname], writes=[dstname])
        proj("wq", "bq_s", "QF")
        proj("wk", "bk_s", "KF")
        proj("wv", "bv_s", "VF")

        for side, FNAME, lane in (("q", "QF", 4), ("k", "KF", 5)):
            for hc in range(nhc):
                for half in range(2):
                    h = 2 * hc + half
                    hr = 64 * half
                    for st4 in range(nst):
                        E("scalar", (lambda hc=hc, hr=hr, st4=st4, FNAME=FNAME: (lambda s: s.activation(
                            T["sqscr"].ap()[hr:hr+64, 0:SW],
                            T[FNAME].ap()[hr:hr+64, S*hc+SW*st4:S*hc+SW*(st4+1)],
                            AF.Square)))(), reads=[FNAME], writes=["sqscr"])
                        E("tensor", (lambda st4=st4, hr=hr: (lambda te: te.matmul(
                            PS[0:1, 3584:3584+SW],
                            T["ones_colb"].ap()[hr:hr+64, 0:1],
                            T["sqscr"].ap()[hr:hr+64, 0:SW],
                            start=True, stop=True)))(),
                          reads=["sqscr", "ones_colb"], writes=["psB7"])
                        E("vector", (lambda: (lambda v: v.tensor_reduce(
                            T["statc"].ap()[0:1, 8:9], PS[0:1, 3584:3584+SW],
                            AX.X, ALU.max)))(), reads=["psB7"], writes=["st8"])
                        if st4 == 0:
                            E("vector", (lambda: (lambda v: v.tensor_copy(
                                T["statc"].ap()[0:1, 9:10], T["statc"].ap()[0:1, 8:9])))(),
                              reads=["st8"], writes=["st9"])
                        else:
                            E("vector", lambda v: v.tensor_max(
                                T["statc"].ap()[0:1, 9:10], T["statc"].ap()[0:1, 9:10],
                                T["statc"].ap()[0:1, 8:9]), reads=["st8", "st9"], writes=["st9"])
                    E("scalar", lambda s: s.activation(T["statc"].ap()[0:1, 9:10],
                        T["statc"].ap()[0:1, 9:10], AF.Ln), reads=["st9"], writes=["st9"])
                    E("scalar", lambda s: s.activation(T["statc"].ap()[0:1, 9:10],
                        T["statc"].ap()[0:1, 9:10], AF.Exp, scale=0.5), reads=["st9"], writes=["st9"])
                    dma("sync", ap("row_d")[lane:lane+1, h:h+1], ap("statc")[0:1, 9:10],
                        "d_m", reads=["st9"], writes=[f"rdl_{side}{h}"])
        if b == 0:
            E("scalar", lambda s: s.activation(T["sqscr"].ap()[0:64, 0:NJ],
                T["tkT2_b"].ap()[0:64, 0:NJ], AF.Square), reads=["tkT2_b"], writes=["sqscr"])
            E("tensor", lambda te: te.matmul(PS[0:1, 3584:3584+NJ],
                T["ones_colb"].ap()[0:64, 0:1], T["sqscr"].ap()[0:64, 0:NJ],
                start=True, stop=True), reads=["sqscr", "ones_colb"], writes=["psB7"])
            E("vector", lambda v: v.tensor_reduce(T["statc"].ap()[0:1, 10:11],
                PS[0:1, 3584:3584+NJ], AX.X, ALU.max), reads=["psB7"], writes=["st10"])
            E("scalar", lambda s: s.activation(T["statc"].ap()[0:1, 10:11],
                T["statc"].ap()[0:1, 10:11], AF.Ln), reads=["st10"], writes=["st10"])
            E("scalar", lambda s: s.activation(T["statc"].ap()[0:1, 10:11],
                T["statc"].ap()[0:1, 10:11], AF.Exp, scale=0.5), reads=["st10"], writes=["st10"])
            dma("sync", ap("row_d")[6:7, 0:1], ap("statc")[0:1, 10:11], "d_m",
                reads=["st10"], writes=["rdl_t"])
        build_attn(k, cfg, b)
        if not cfg.get("no_tail"):
            build_tail(k, cfg, b)
    return k

def psnames(c0, c1):
    out = []
    if c0 < 2048:
        out.append("psAV")
    for bkt, nm in ((2048, "psB4"), (2560, "psB5"), (3072, "psB6"), (3584, "psB7")):
        if c0 < bkt + 512 and c1 > bkt:
            out.append(nm)
    return out


def build_attn(k, cfg, b):
    S = cfg["S"]; NH = cfg["NH"]; HID = 64 * NH
    nt = S // 128; nhc = HID // 128
    nst = max(1, S // 512); SW = min(512, S)
    VW = 64 + 32 + 1
    T, plan = k.T, k.plan
    ap = lambda n: T[n].ap()
    PS = T["PS"].ap()
    E = plan.emit

    def dma(q, dst, src, sem, reads=(), writes=()):
        E(q, (lambda d=dst, s=src: (lambda eng: eng.dma_start(out=d, in_=s)))(),
          reads=reads, writes=writes, dma=sem)

    dma("sync", ap("maxs")[0:1, 0:NH], ap("row_d")[4:5, 0:NH], "d_m2",
        reads=[f"rdl_q{h}" for h in range(NH)], writes=["nmq"])
    dma("sync", ap("maxs")[0:1, 32:32+NH], ap("row_d")[5:6, 0:NH], "d_m2",
        reads=[f"rdl_k{h}" for h in range(NH)], writes=["nmk"])
    dma("sync", ap("maxs")[0:1, 64:65], ap("row_d")[6:7, 0:1], "d_m2",
        reads=["rdl_t"], writes=["nmt"])
    E("vector", lambda v: v.tensor_scalar(T["maxs"].ap()[0:1, 32:32+NH],
        T["maxs"].ap()[0:1, 32:32+NH], T["maxs"].ap()[0:1, 64:65], None, ALU.add),
      reads=["nmk", "nmt"], writes=["nmk"])
    E("vector", lambda v: v.tensor_mul(T["maxs"].ap()[0:1, 0:NH],
        T["maxs"].ap()[0:1, 0:NH], T["maxs"].ap()[0:1, 32:32+NH]),
      reads=["nmq", "nmk"], writes=["nmq"])
    E("vector", lambda v: v.tensor_scalar_mul(T["maxs"].ap()[0:1, 0:NH],
        T["maxs"].ap()[0:1, 0:NH], -0.125), reads=["nmq"], writes=["nmq"])
    dma("sync", ap("row_d")[7:8, 0:NH], ap("maxs")[0:1, 0:NH], "d_m3",
        reads=["nmq"], writes=["negm_r"])
    dma("sync", ap("negM")[:, 0:NH],
        ap("row_d")[7:8, 0:NH].partition_broadcast(128), "d_m4",
        reads=["negm_r"], writes=["negM"])

    for h in range(cfg.get("heads", NH)):
        hc, hr = h // 2, 64 * (h % 2)
        qf = lambda c0, c1, hr=hr, hc=hc: T["QF"].ap()[hr:hr+64, S*hc+c0:S*hc+c1]
        kf = lambda c0, c1, hr=hr, hc=hc: T["KF"].ap()[hr:hr+64, S*hc+c0:S*hc+c1]
        vf = lambda c0, c1, hr=hr, hc=hc: T["VF"].ap()[hr:hr+64, S*hc+c0:S*hc+c1]

        for t in range(nt):
            E("tensor", (lambda t=t, vf=vf, hr=hr: (lambda te: te.matmul(
                PS[0:128, 2048+64*t:2048+64*(t+1)], vf(128*t, 128*(t+1)),
                T["identd"].ap()[hr:hr+64, 0:64], start=True, stop=True)))(),
              reads=["VF", "identd"], writes=psnames(2048+64*t, 2048+64*(t+1)))
        E("vector", (lambda nt=nt: (lambda v: v.tensor_copy(
            T["vpm"].ap().rearrange("p (c w) -> p c w", w=VW)[:, :, 0:64],
            PS[0:128, 2048:2048+64*nt].rearrange("p (c w) -> p c w", w=64))))(),
          reads=psnames(2048, 2048+64*nt), writes=["vpm"])

        for st4 in range(nst):
            E("tensor", (lambda st4=st4, qf=qf, hr=hr: (lambda te: te.matmul(
                PS[0:65, 3584:3584+SW], T["tkT2_b"].ap()[hr:hr+64, :],
                qf(SW*st4, SW*(st4+1)), start=True, stop=True)))(),
              reads=["tkT2_b", "QF"], writes=["psB7"])
            E("scalar", (lambda st4=st4: (lambda s: s.activation(
                T["SattnT"].ap()[:, SW*st4:SW*(st4+1)], PS[0:NJ, 3584:3584+SW],
                AF.Copy)))(), reads=["psB7"], writes=["SattnT"])
            E("scalar", (lambda st4=st4: (lambda s: s.activation(
                T["s08row"].ap()[0:1, SW*st4:SW*(st4+1)], PS[64:65, 3584:3584+SW],
                AF.Copy)))(), reads=["psB7"], writes=["s08row"])

        for t in range(nt):
            E("tensor", (lambda t=t: (lambda te: te.matmul(
                PS[0:128, 3072+NJ*t:3072+NJ*(t+1)],
                T["SattnT"].ap()[0:NJ, 128*t:128*(t+1)], T["ident"].ap()[0:NJ, 0:NJ],
                start=True, stop=True)))(), reads=["SattnT", "ident"],
              writes=["psB6w0"])
        E("vector", (lambda nt=nt: (lambda v: v.tensor_copy(
            T["Sqp"].ap()[:, 0:NJ*nt], PS[0:128, 3072:3072+NJ*nt])))(),
          reads=["psB6w0"], writes=["Sqp"])


        def build_window(tw):
            slot = f"A2X{tw % 4}"
            blk = f"a2xd{tw % 2}"
            sD = f"slotD{tw % 2}"
            v198 = ap(blk)[0:1, :].rearrange("o (p w) -> (o p) w", w=WWIN + 2)
            v197 = ap(blk)[0:1, 0:128 * (WWIN + 1)].rearrange(
                "o (p w) -> (o p) w", w=WWIN + 1)
            E("vector", (lambda tw=tw: (lambda v: v.tensor_scalar(
                T["vj"].ap()[:, 0:8], T["Sqp"].ap()[:, NJ*tw:NJ*tw+8],
                T["Sqp"].ap()[:, NJ*tw+8:NJ*tw+9], None, ALU.subtract)))(),
              reads=["Sqp"], writes=["vj"])
            dma("gpsimd", v198[:, LPAD-3:LPAD+4], T["vj"].ap()[:, 1:8], "d_a2w",
                reads=["vj", f"a2xz{tw % 2}", f"a2xz{tw % 2}b"], writes=[f"a2dg{tw % 2}"])
            dma("sync", T[sD].ap()[:, 0:WWIN], v197[:, 0:WWIN], "d_a2r",
                reads=[f"a2dg{tw % 2}"], writes=[sD])
            E("vector", (lambda tw=tw, slot=slot: (lambda v: v.tensor_scalar_mul(
                T[slot].ap()[:, :], T["coef"].ap()[:, 0:WWIN],
                T["vj"].ap()[:, 0:1])))(),
              reads=["coef", "vj"], writes=[slot])
            E("vector", (lambda slot=slot, sD=sD: (lambda v: v.tensor_add(
                T[slot].ap()[:, :], T[slot].ap()[:, :], T[sD].ap()[:, :])))(),
              reads=[slot, sD], writes=[slot])
        build_window(0)
        if nt > 1:
            build_window(1)

        def fw_window(tw):
            c0 = 0 if tw > 0 else LPAD
            c1 = WWIN if tw < nt - 1 else LPAD + 128
            base = 3072 + 512 * (tw % 2)
            pw = "psB6w0" if tw % 2 == 0 else "psB7"
            fv = f"fwv{tw % 2}"
            blk = f"fwd{tw % 2}"
            if tw > 0:
                E("tensor", (lambda tw=tw, base=base: (lambda te: te.matmul(
                    PS[0:128, base:base+LPAD],
                    T[f"F{(tw-1) % 3}"].ap()[64:128, 128*tw:128*(tw+1)],
                    T["identd"].ap()[64:128, 0:64], start=True, stop=True)))(),
                  reads=[f"F{(tw-1) % 3}s{(128*tw) // SW}", "identd"], writes=[pw])
            E("tensor", (lambda tw=tw, base=base: (lambda te: te.matmul(
                PS[0:128, base+LPAD:base+LPAD+128],
                T[f"F{tw % 3}"].ap()[0:128, 128*tw:128*(tw+1)],
                T["ident"].ap()[:, :], start=True, stop=True)))(),
              reads=[f"F{tw % 3}s{(128*tw) // SW}", "ident"], writes=[pw])
            if tw < nt - 1:
                E("tensor", (lambda tw=tw, base=base: (lambda te: te.matmul(
                    PS[0:128, base+LPAD+128:base+WWIN],
                    T[f"F{(tw+1) % 3}"].ap()[0:RPAD, 128*tw:128*(tw+1)],
                    T["ident"].ap()[0:RPAD, 0:RPAD], start=True, stop=True)))(),
                  reads=[f"F{(tw+1) % 3}s{(128*tw) // SW}", "ident"], writes=[pw])
            if tw == 0:
                E("vector", (lambda fv=fv: (lambda v: v.memset(
                    T[fv].ap()[:, 0:LPAD], 0.0)))(), reads=[], writes=[fv])
            if tw == nt - 1:
                E("vector", (lambda fv=fv: (lambda v: v.memset(
                    T[fv].ap()[:, LPAD+128:WWIN], 0.0)))(), reads=[], writes=[fv])
            E("scalar", (lambda base=base, c0=c0, c1=c1, fv=fv: (lambda s: s.activation(
                T[fv].ap()[:, c0:c1], PS[0:128, base+c0:base+c1], AF.Copy)))(),
              reads=[pw], writes=[fv])
            dma("scalar", ap(blk)[0:1, 0:128 * WWIN].rearrange("o (p w) -> (o p) w", w=WWIN),
                T[fv].ap()[:, 0:WWIN], "d_fww", reads=[fv], writes=[f"fwb{tw % 2}"])
            dma("gpsimd",
                T["FBqp"].ap()[:, NJ*tw+1:NJ*tw+8],
                ap(blk)[0:1, 0:128 * (WWIN + 1)].rearrange(
                    "o (p w) -> (o p) w", w=WWIN + 1)[:, LPAD-3:LPAD+4],
                "d_fwr", reads=[f"fwb{tw % 2}"], writes=["FBqp"])
            for j in (0, 8):
                s0_, s1_ = (61, 188) if j == 0 else (68, 195)
                cc0, cc1 = max(c0, s0_), min(c1, s1_)
                E("vector", (lambda tw=tw, j=j, cc0=cc0, cc1=cc1, fv=fv: (lambda v: v.scalar_tensor_tensor(
                    T["ttr_scr"].ap()[:, cc0:cc1], T[fv].ap()[:, cc0:cc1], 1.0,
                    T["coefw"].ap()[:, WWIN*j+cc0:WWIN*j+cc1],
                    ALU.mult, ALU.mult,
                    accum_out=T["FBqp"].ap()[:, NJ*tw+j:NJ*tw+j+1])))(),
                  reads=[fv, "coefw"], writes=["ttr_scr", "FBqp"])

        par = "a" if hr == 0 else "b"
        for t in range(nt):
            stag = f"KQ{t % 2}{par}"
            E("scalar", (lambda t=t, kf=kf, stag=stag, hr=hr: (lambda s: s.activation(
                T[stag].ap()[hr:hr+64, 0:128], kf(128*t, 128*(t+1)), AF.Copy)))(),
              reads=["KF", f"{stag}_z"], writes=[stag])
            if t + 2 < nt:
                build_window(t + 2)
            for st4 in range(nst):
                g0, g1 = SW * st4, SW * (st4 + 1)
                pso = 2048 + 512 * (st4 % 2)
                pname = "psB4" if st4 % 2 == 0 else "psB5"
                lo, hi = max(g0, 128 * (t + 1)), g1
                have_sfx = lo < hi
                have_a = g0 <= 128 * t < g1
                have_b = t >= 1 and g0 <= 128 * (t - 1) < g1
                have_c = t + 1 < nt and g0 <= 128 * (t + 1) < g1
                npieces = int(have_a) + int(have_b) + int(have_c) + int(have_sfx)
                E("tensor", (lambda g0=g0, g1=g1, t=t, pso=pso, hc=hc, stag=stag,
                             stp=(npieces == 0): (lambda te: te.matmul(
                    PS[0:128, pso:pso+SW], T[stag].ap()[:, :],
                    T["QF"].ap()[0:128, S*hc+g0:S*hc+g1],
                    start=True, stop=stp)))(),
                  reads=[stag, "QF"], writes=[pname])
                if have_b:
                    E("tensor", (lambda t=t, pso=pso, g0=g0: (lambda te: te.matmul(
                        PS[0:RPAD, pso+128*(t-1)-g0:pso+128*t-g0],
                        T[f"A2X{(t-1) % 4}"].ap()[:, LPAD+128:WWIN],
                        T["ident"].ap()[:, :], start=False, stop=False)))(),
                      reads=[f"A2X{(t-1) % 4}", "ident"], writes=[pname])
                if have_c:
                    E("tensor", (lambda t=t, pso=pso, g0=g0: (lambda te: te.matmul(
                        PS[64:128, pso+128*(t+1)-g0:pso+128*(t+2)-g0],
                        T[f"A2X{(t+1) % 4}"].ap()[:, 0:LPAD],
                        T["ident"].ap()[:, :], start=False, stop=False)))(),
                      reads=[f"A2X{(t+1) % 4}", "ident"], writes=[pname])
                if have_a:
                    E("tensor", (lambda t=t, pso=pso, g0=g0, stp=not have_sfx: (lambda te: te.matmul(
                        PS[0:128, pso+128*t-g0:pso+128*(t+1)-g0],
                        T[f"A2X{t % 4}"].ap()[:, LPAD:LPAD+128],
                        T["ident"].ap()[:, :], start=False, stop=stp)))(),
                      reads=[f"A2X{t % 4}", "ident"], writes=[pname])
                if have_sfx:
                    E("tensor", (lambda lo=lo, hi=hi, pso=pso, g0=g0: (lambda te: te.matmul(
                        PS[0:128, pso+lo-g0:pso+hi-g0], T["ones_row"].ap()[:, :],
                        T["s08row"].ap()[0:1, lo:hi], start=False, stop=True)))(),
                      reads=["ones_row", "s08row"], writes=[pname])
                elif not have_a and npieces > 0:
                    E("tensor", (lambda pso=pso: (lambda te: te.matmul(
                        PS[0:128, pso:pso+1], T["ones_row"].ap()[:, :],
                        T["ident"].ap()[0:1, 1:2], start=False, stop=True)))(),
                      reads=["ones_row", "ident"], writes=[pname])
                E("scalar", (lambda g0=g0, g1=g1, t=t, pso=pso, h=h: (lambda s: s.activation(
                    T[f"F{t % 3}"].ap()[:, g0:g1], PS[0:128, pso:pso+SW], AF.Exp,
                    bias=T["negM"].ap()[:, h:h+1], scale=0.125)))(),
                  reads=[pname, "negM"], writes=[f"F{t % 3}s{st4}"])

            if t >= 1:
                fw_window(t - 1)
            if t == nt - 1:
                fw_window(t)
                E("vector", (lambda nt=nt: (lambda v: v.tensor_copy(
                    T["sqscr"].ap()[:, 0:NJ*nt], T["FBqp"].ap()[:, 0:NJ*nt])))(),
                  reads=["FBqp"], writes=["sqscr"])
                for s4 in range((nt + 3) // 4):
                    ss = list(range(4 * s4, min(4 * s4 + 4, nt)))
                    for i_, s in enumerate(ss):
                        E("tensor", (lambda s=s, i_=i_: (lambda te: te.matmul(
                            PS[0:NJ, 2048+128*i_:2048+128*(i_+1)],
                            T["sqscr"].ap()[:, NJ*s:NJ*(s+1)], T["ident"].ap()[:, :],
                            start=True, stop=True)))(), reads=["sqscr", "ident"],
                          writes=psnames(2048+128*i_, 2048+128*(i_+1)))
                    E("vector", (lambda s4=s4, n=len(ss): (lambda v: v.tensor_copy(
                        T["FBT"].ap()[0:NJ, 512*s4:512*s4+128*n],
                        PS[0:NJ, 2048:2048+128*n])))(),
                      reads=psnames(2048, 2048+128*len(ss)), writes=["FBT"])
                for s in range(nt):
                    E("tensor", (lambda s=s: (lambda te: te.matmul(
                        PS[0:64, 128*s:128*(s+1)], T["tv_b"].ap()[:, :],
                        T["FBT"].ap()[0:NJ, 128*s:128*(s+1)], start=False, stop=False)))(),
                      reads=["tv_b", "FBT"], writes=["psAV"])
            for st4 in range(nst):
                g0, g1 = SW * st4, SW * (st4 + 1)
                E("tensor", (lambda t=t, g0=g0, g1=g1: (lambda te: te.matmul(
                    PS[0:VW, g0:g1], T["vpm"].ap()[:, VW*t:VW*(t+1)],
                    T[f"F{t % 3}"].ap()[:, g0:g1],
                    start=(t == 0), stop=(t == nt - 1))))(),
                  reads=["vpm", f"F{t % 3}s{st4}"], writes=["psAV"])

        E("vector", (lambda nt=nt: (lambda v: v.tensor_copy(
            T["Rrows"].ap()[0:nt, 0:S], PS[64:64+nt, 0:S])))(),
          reads=["psAV"], writes=["Rrows"])
        for st4 in range(nst):
            z0, z1 = SW * st4, SW * (st4 + 1)
            E("scalar", (lambda nt=nt, z0=z0, z1=z1: (lambda s: s.activation(
                T["rowf"].ap()[0:1, 0:z1-z0], PS[96:97, z0:z1], AF.Copy)))(),
              reads=["psAV"], writes=["rowf"])
            E("scalar", (lambda z0=z0, z1=z1: (lambda s: s.activation(
                T["rowf"].ap()[0:1, 0:z1-z0], T["rowf"].ap()[0:1, 0:z1-z0],
                AF.Ln)))(), reads=["rowf"], writes=["rowf"])
            E("scalar", (lambda z0=z0, z1=z1: (lambda s: s.activation(
                T["rowf"].ap()[0:1, 0:z1-z0], T["rowf"].ap()[0:1, 0:z1-z0],
                AF.Exp, scale=-1.0)))(), reads=["rowf"], writes=["rowf"])
            dma("sync", ap("row_d")[2:3, z0:z1], ap("rowf")[0:1, 0:z1-z0], "d_z",
                reads=["rowf"], writes=["rdlane2"])
        dma("gpsimd", ap("zbt")[:, 0:S],
            ap("row_d")[2:3, 0:S].partition_broadcast(128), "d_zb",
            reads=["rdlane2"], writes=["zbt"])

        for st4 in range(nst):
            g0, g1 = SW*st4, SW*(st4+1)
            for si in range((g1 - g0) // 128):
                s = (g0 // 128) + si
                pso = 2048 + 128 * (s % 8)
                E("tensor", (lambda s=s, pso=pso: (lambda te: te.matmul(
                    PS[0:64, pso:pso+128], T["tvsel"].ap()[0:nt, 64*s:64*(s+1)],
                    T["Rrows"].ap()[0:nt, 128*s:128*(s+1)], start=True, stop=True)))(),
                  reads=["tvsel", "Rrows"], writes=psnames(pso, pso+128))
                E("vector", (lambda s=s, si=si, pso=pso: (lambda v: v.tensor_tensor(
                    T["toktmp"].ap()[0:64, SW+128*si:SW+128*(si+1)], PS[0:64, pso:pso+128],
                    T["zbt"].ap()[0:64, 128*s:128*(s+1)], ALU.mult)))(),
                  reads=psnames(pso, pso+128) + ["zbt"], writes=["toktmp2"])
            E("vector", (lambda g0=g0, g1=g1: (lambda v: v.tensor_tensor(
                T["toktmp"].ap()[0:64, 0:g1-g0], PS[0:64, g0:g1],
                T["zbt"].ap()[0:64, g0:g1], ALU.mult)))(),
              reads=["psAV", "zbt"], writes=["toktmp"])
            E("vector", (lambda g0=g0, g1=g1, hc=hc, hr=hr: (lambda v: v.tensor_add(
                T["XN1"].ap()[hr:hr+64, S*hc+g0:S*hc+g1],
                T["toktmp"].ap()[0:64, 0:g1-g0],
                T["toktmp"].ap()[0:64, SW:SW+g1-g0])))(),
              reads=["toktmp", "toktmp2"], writes=["XN1"])
        if cfg.get("dbg") and h == cfg["NH"] - 1 and b == 0:
            if "dbg_d" not in T:
                k.dram_tmp("dbg_d", [192, S], BF16)
            dma("sync", ap("dbg_d")[0:128, :], T["XN1"].ap()[0:128, 0:S], "d_dbg",
                reads=["XN1"], writes=["dbgd"])
    if cfg.get("dbg") and b == 0:
        if "dbg3_d" not in T:
            k.dram_tmp("dbg3_d", [128, S], BF16)


def build_tail(k, cfg, b):
    S = cfg["S"]; NH = cfg["NH"]; HID = 64 * NH; FF = cfg["FF"]
    nt = S // 128; nhc = HID // 128; nfc = FF // 128
    nst = max(1, S // 512); SW = min(512, S)
    T, plan = k.T, k.plan
    ap = lambda n: T[n].ap()
    PS = T["PS"].ap()
    E = plan.emit
    xoff = b * S
    LN127 = float(np.log(127.0))

    def dma(q, dst, src, sem, reads=(), writes=()):
        E(q, (lambda d=dst, s=src: (lambda eng: eng.dma_start(out=d, in_=s)))(),
          reads=reads, writes=writes, dma=sem)

    _BN = ["psB4", "psB5", "psB6w0", "psB7"]
    wslot = [0]
    for oc in range(nhc):
        for i in range(nhc):
            ws = f"Wc{wslot[0] % 2}"; wslot[0] += 1
            dma("sync", T[ws].ap()[:, 0:128],
                ap("wob_d")[128*i:128*(i+1), 128*oc:128*(oc+1)], "d_w",
                reads=["wob"], writes=[ws])
            for st4 in range(nst):
                pso = 2048 + 512 * st4
                E("tensor", (lambda st4=st4, i=i, pso=pso, ws=ws: (lambda te: te.matmul(
                    PS[0:128, pso:pso+SW], T[ws].ap()[:, 0:128],
                    T["XN1"].ap()[:, S*i+SW*st4:S*i+SW*(st4+1)],
                    start=(i == 0), stop=(i == nhc-1))))(),
                  reads=[ws, "XN1"], writes=[_BN[st4]])
        for st4 in range(nst):
            pso = 2048 + 512 * st4
            E("scalar", (lambda oc=oc, st4=st4, pso=pso: (lambda s: s.activation(
                T["QF"].ap()[:, S*oc+SW*st4:S*oc+SW*(st4+1)], PS[0:128, pso:pso+SW],
                AF.Identity, bias=T["bo_s"].ap()[:, oc:oc+1])))(),
              reads=[_BN[st4], "bo_s"], writes=["QF"])

    if cfg.get("dbg") and b == 0:
        if "dbg3_d" not in T:
            k.dram_tmp("dbg3_d", [128, S], BF16)
        dma("sync", ap("dbg3_d")[0:128, :], T["QF"].ap()[0:128, 0:S], "d_dbg",
            reads=["QF"], writes=["dbg3"])

    def ln_tok_chain(src_name, src_slice, eps, out_name, out_slice, width):
        E("scalar", (lambda src_slice=src_slice, width=width: (lambda s: s.activation(
            T["sqscr"].ap()[:, 0:width], src_slice(), AF.Square,
            accum_out=T["statc"].ap()[:, 1:2])))(),
          reads=[src_name], writes=["sqscr", "st1"])
        E("vector", (lambda src_slice=src_slice: (lambda v: v.reduce_sum(
            T["statc"].ap()[:, 0:1], src_slice(), AX.X)))(),
          reads=[src_name], writes=["st0"])
        E("vector", (lambda width=width: (lambda v: v.tensor_scalar_mul(
            T["statc"].ap()[:, 0:1], T["statc"].ap()[:, 0:1], 1.0 / width)))(),
          reads=["st0"], writes=["st0"])
        E("vector", lambda v: v.tensor_mul(T["statc"].ap()[:, 2:3],
            T["statc"].ap()[:, 0:1], T["statc"].ap()[:, 0:1]), reads=["st0"], writes=["st2"])
        E("vector", (lambda width=width: (lambda v: v.scalar_tensor_tensor(
            T["statc"].ap()[:, 3:4], T["statc"].ap()[:, 1:2], 1.0 / width,
            T["statc"].ap()[:, 2:3], ALU.mult, ALU.subtract)))(),
          reads=["st1", "st2"], writes=["st3"])
        E("vector", (lambda eps=eps: (lambda v: v.tensor_scalar_add(
            T["statc"].ap()[:, 3:4], T["statc"].ap()[:, 3:4],
            (eps - float(np.float32(eps))))))(), reads=["st3"], writes=["st3"])
        E("scalar", (lambda eps=eps: (lambda s: s.activation(T["statc"].ap()[:, 3:4],
            T["statc"].ap()[:, 3:4], AF.Ln, bias=eps)))(), reads=["st3"], writes=["st3"])
        E("scalar", lambda s: s.activation(T["statc"].ap()[:, 3:4],
            T["statc"].ap()[:, 3:4], AF.Exp, scale=-0.5), reads=["st3"], writes=["st3"])
        E("vector", (lambda src_slice=src_slice, out_slice=out_slice: (lambda v: v.tensor_scalar(
            out_slice(), src_slice(), T["statc"].ap()[:, 0:1], T["statc"].ap()[:, 3:4],
            ALU.subtract, ALU.mult)))(),
          reads=[src_name, "st0", "st3"], writes=[out_name])

    for t in range(nt):
        for c in range(nhc):
            E("tensor", (lambda t=t, c=c: (lambda te: te.matmul(
                PS[0:128, 2048+128*c:2048+128*(c+1)],
                T["QF"].ap()[:, S*c+128*t:S*c+128*(t+1)], T["ident"].ap()[:, :],
                start=True, stop=True)))(), reads=["QF", "ident"],
              writes=psnames(2048+128*c, 2048+128*(c+1)))
        E("vector", lambda v: v.tensor_copy(T["fq1"].ap()[:, 0:HID],
            PS[0:128, 2048:2048+HID]), reads=psnames(2048, 2048+HID), writes=["fq1"])
        ln_tok_chain("fq1", (lambda: T["fq1"].ap()[:, 0:HID]), 1e-5,
                     "toktmp", (lambda: T["toktmp"].ap()[:, 0:HID]), HID)
        dma("sync", ap("xtile")[:, :], ap("x")[xoff+128*t:xoff+128*(t+1), :],
            "d_x2", writes=["xtile"])
        E("vector", lambda v: v.tensor_add(T["x2t"].ap()[:, :],
            T["toktmp"].ap()[:, 0:HID], T["xtile"].ap()[:, :]),
          reads=["toktmp", "xtile"], writes=["x2t"])
        dma("sync", ap("x2_d")[xoff+128*t:xoff+128*(t+1), :], ap("x2t")[:, :],
            "d_x2o", reads=["x2t"], writes=[f"x2d_{b}_{t}"])

    if b == 0:
        for lane, col, nm in ((0, 7, "lnw1col"), (1, 15, "lnw2col")):
            dma("sync", ap("statc")[:, 14:15],
                ap("row_d")[lane:lane+1, 0:1].partition_broadcast(128), "d_m5",
                reads=[f"rdlane{lane}"], writes=["st14"])
            E("scalar", lambda s: s.activation(T["statc"].ap()[:, 14:15],
                T["statc"].ap()[:, 14:15], AF.Ln), reads=["st14"], writes=["st14"])
            E("vector", (lambda col=col, nm=nm: (lambda v: v.tensor_copy(
                T["statc"].ap()[:, col:col+1], T["statc"].ap()[:, 14:15])))(),
              reads=["st14"], writes=[nm])
        E("vector", lambda v: v.tensor_scalar(T["statc"].ap()[:, 16:17],
            T["scal"].ap()[:, 21:22], -127.0, 2.0, ALU.mult, ALU.add),
          reads=["scal"], writes=["r127t"])
        E("vector", lambda v: v.tensor_mul(T["statc"].ap()[:, 16:17],
            T["statc"].ap()[:, 16:17], T["scal"].ap()[:, 21:22]),
          reads=["r127t", "scal"], writes=["r127"])

    for t in range(nt):
        dma("sync", ap("x2t")[:, :], ap("x2_d")[xoff+128*t:xoff+128*(t+1), :],
            "d_x2i", reads=[f"x2d_{b}_{t}"], writes=["x2t"])
        E("vector", lambda v: v.tensor_copy(T["fq1"].ap()[:, 0:HID],
            T["x2t"].ap()[:, :]), reads=["x2t"], writes=["fq1"])
        ln_tok_chain("fq1", (lambda: T["fq1"].ap()[:, 0:HID]), 1e-5,
                     "toktmp", (lambda: T["toktmp"].ap()[:, 0:HID]), HID)
        E("scalar", lambda s: s.activation(T["sqscr"].ap()[:, 0:HID],
            T["toktmp"].ap()[:, 0:HID], AF.Square, accum_out=T["statc"].ap()[:, 4:5]),
          reads=["toktmp"], writes=["sqscr", "st4"])
        E("scalar", (lambda H=HID: (lambda s: s.activation(T["statc"].ap()[:, 4:5],
            T["statc"].ap()[:, 4:5], AF.Ln, bias=1e-6, scale=1.0 / H)))(),
          reads=["st4"], writes=["st4"])
        E("scalar", lambda s: s.activation(T["statc"].ap()[:, 4:5],
            T["statc"].ap()[:, 4:5], AF.Exp, scale=-0.5), reads=["st4"], writes=["st4"])
        E("vector", lambda v: v.reduce_max(T["statc"].ap()[:, 5:6],
            T["toktmp"].ap()[:, 0:HID], AX.X, apply_absolute_value=True),
          reads=["toktmp"], writes=["st5"])
        E("vector", lambda v: v.tensor_mul(T["statc"].ap()[:, 5:6],
            T["statc"].ap()[:, 4:5], T["statc"].ap()[:, 5:6]),
          reads=["st4", "st5"], writes=["st5"])
        E("vector", lambda v: v.tensor_scalar_max(T["statc"].ap()[:, 5:6],
            T["statc"].ap()[:, 5:6], 1e-5), reads=["st5"], writes=["st5"])
        E("scalar", lambda s: s.activation(T["statc"].ap()[:, 6:7],
            T["statc"].ap()[:, 5:6], AF.Ln), reads=["st5"], writes=["st6"])
        E("scalar", lambda s: s.activation(T["statc"].ap()[:, 5:6],
            T["statc"].ap()[:, 6:7], AF.Exp, scale=-1.0),
          reads=["st6"], writes=["st5"])
        E("vector", lambda v: v.tensor_mul(T["statc"].ap()[:, 8:9],
            T["statc"].ap()[:, 4:5], T["statc"].ap()[:, 5:6]),
          reads=["st4", "st5"], writes=["st8"])
        E("vector", lambda v: v.tensor_scalar(T["statc"].ap()[:, 8:9],
            T["statc"].ap()[:, 8:9], 127.0, None, ALU.mult),
          reads=["st8"], writes=["st8"])
        E("vector", lambda v: v.tensor_scalar(T["fq1"].ap()[:, 0:HID],
            T["toktmp"].ap()[:, 0:HID], T["statc"].ap()[:, 8:9],
            T["Ccol"].ap()[:, 0:1], ALU.mult, ALU.add),
          reads=["toktmp", "st8", "Ccol"], writes=["fq1"])
        E("vector", lambda v: v.tensor_add(T["statc"].ap()[:, 9:10],
            T["statc"].ap()[:, 7:8], T["statc"].ap()[:, 6:7]),
          reads=["lnw1col", "st6"], writes=["st9"])
        E("scalar", lambda s: s.activation(T["statc"].ap()[:, 9:10],
            T["statc"].ap()[:, 9:10], AF.Exp), reads=["st9"], writes=["st9"])
        E("vector", lambda v: v.tensor_mul(T["statc"].ap()[:, 9:10],
            T["statc"].ap()[:, 9:10], T["statc"].ap()[:, 16:17]),
          reads=["st9", "r127"], writes=["st9"])
        E("vector", lambda v: v.tensor_scalar(T["toktmp"].ap()[:, 0:HID],
            T["fq1"].ap()[:, 0:HID], T["Ccol"].ap()[:, 0:1],
            T["statc"].ap()[:, 9:10], ALU.subtract, ALU.mult),
          reads=["fq1", "Ccol", "st9"], writes=["toktmp"])
        for c in range(nhc):
            E("tensor", (lambda t=t, c=c: (lambda te: te.matmul(
                PS[0:128, 2048+128*c:2048+128*(c+1)],
                T["toktmp"].ap()[:, 128*c:128*(c+1)], T["ident"].ap()[:, :],
                start=True, stop=True)))(), reads=["toktmp", "ident"],
              writes=psnames(2048+128*c, 2048+128*(c+1)))
        E("vector", (lambda t=t: (lambda v: v.tensor_copy(
            T["KF"].ap().rearrange("p (c s) -> p c s", s=S)[:, :, 128*t:128*(t+1)],
            PS[0:128, 2048:2048+128*nhc].rearrange("p (c s) -> p c s", s=128))))(),
          reads=psnames(2048, 2048+128*nhc), writes=["KF"])

    for sp in range(nst):
        tg0 = SW * sp
        for ocg in range((nfc + 3) // 4):
            ocs = list(range(4 * ocg, min(4 * ocg + 4, nfc)))
            for i in range(nhc):
                wsn = "W1s" if i % 2 == 0 else "W2s"
                dma("sync", ap(wsn)[:, 0:128*len(ocs)],
                    ap("w1i_d")[128*i:128*(i+1), 128*ocs[0]:128*ocs[0]+128*len(ocs)],
                    "d_w1", writes=[wsn])
                for j, oc in enumerate(ocs):
                    E("tensor", (lambda i=i, j=j, tg0=tg0, wsn=wsn: (lambda te: te.matmul(
                        PS[0:128, 2048+512*j:2048+512*j+SW],
                        T[wsn].ap()[:, 128*j:128*(j+1)],
                        T["KF"].ap()[:, S*i+tg0:S*i+tg0+SW],
                        start=(i == 0), stop=(i == nhc-1))))(),
                      reads=[wsn, "KF"], writes=[f"psB{4+j}"])
            for j, oc in enumerate(ocs):
                E("scalar", (lambda j=j, oc=oc: (lambda s: s.activation(
                    T["QF"].ap()[:, SW*oc:SW*oc+SW], PS[0:128, 2048+512*j:2048+512*j+SW],
                    AF.Gelu, bias=T["b1_s"].ap()[:, oc:oc+1])))(),
                  reads=[f"psB{4+j}", "b1_s"], writes=[f"QFo{oc}"])
        for oc in range(nfc):
            E("scalar", (lambda oc=oc: (lambda s: s.activation(
                T["sqscr"].ap()[:, 0:SW], T["QF"].ap()[:, SW*oc:SW*oc+SW],
                AF.Square)))(), reads=[f"QFo{oc}"], writes=["sqscr"])
            E("tensor", (lambda oc=oc: (lambda te: te.matmul(
                PS[0:1, 3584:3584+SW], T["ones_colb"].ap()[:, 0:1],
                T["sqscr"].ap()[:, 0:SW], start=(oc == 0), stop=(oc == nfc-1))))(),
              reads=["sqscr", "ones_colb"], writes=["psB7"])
            if oc == 0:
                E("vector", lambda v: v.tensor_copy(
                    T["sqscr"].ap()[:, 512:512+SW], T["sqscr"].ap()[:, 0:SW]),
                  reads=["sqscr"], writes=["sqmax"])
            else:
                E("vector", lambda v: v.tensor_max(
                    T["sqscr"].ap()[:, 512:512+SW], T["sqscr"].ap()[:, 512:512+SW],
                    T["sqscr"].ap()[:, 0:SW]), reads=["sqscr", "sqmax"],
                  writes=["sqmax"])
        nb = SW // 128
        E("scalar", lambda s: s.activation(T["rowf"].ap()[0:1, 0:SW],
            PS[0:1, 3584:3584+SW], AF.Copy), reads=["psB7"], writes=["rowf"])
        for bq in range(nb):
            E("tensor", (lambda bq=bq: (lambda te: te.matmul(
                PS[0:128, 3648+bq:3648+bq+1],
                T["rowf"].ap()[0:1, 128*bq:128*(bq+1)],
                T["scal"].ap()[0:1, 2:3], start=True, stop=True)))(),
              reads=["rowf"], writes=["psB7"])
            E("tensor", (lambda bq=bq: (lambda te: te.matmul(
                PS[0:128, 2048+128*bq:2048+128*(bq+1)],
                T["sqscr"].ap()[:, 512+128*bq:512+128*(bq+1)], T["ident"].ap()[:, :],
                start=True, stop=True)))(), reads=["sqmax", "ident"],
              writes=["psB4"])
            E("vector", (lambda bq=bq: (lambda v: v.tensor_reduce(
                T["hcol"].ap()[:, bq:bq+1], PS[0:128, 2048+128*bq:2048+128*(bq+1)],
                AX.X, ALU.max)))(), reads=["psB4"], writes=["hcol_am"])
        E("vector", (lambda nb=nb: (lambda v: v.tensor_copy(
            T["hcol"].ap()[:, 4:4+nb], PS[0:128, 3648:3648+nb])))(),
          reads=["psB7"], writes=["hcol_sq"])
        E("scalar", (lambda FF=FF, nb=nb: (lambda s: s.activation(T["hcol"].ap()[:, 4:4+nb],
            T["hcol"].ap()[:, 4:4+nb], AF.Ln, bias=1e-6, scale=1.0 / FF)))(),
          reads=["hcol_sq"], writes=["hcol_sq"])
        E("scalar", (lambda nb=nb: (lambda s: s.activation(T["hcol"].ap()[:, 0:nb],
            T["hcol"].ap()[:, 0:nb], AF.Ln)))(), reads=["hcol_am"], writes=["hcol_am"])
        E("vector", (lambda nb=nb: (lambda v: v.tensor_tensor(T["hcol"].ap()[:, 0:nb],
            T["hcol"].ap()[:, 0:nb], T["hcol"].ap()[:, 4:4+nb], ALU.subtract)))(),
          reads=["hcol_am", "hcol_sq"], writes=["hcol_am"])
        E("vector", (lambda nb=nb: (lambda v: v.tensor_scalar_max(
            T["hcol"].ap()[:, 0:nb], T["hcol"].ap()[:, 0:nb], -23.025850929940457)))(),
          reads=["hcol_am"], writes=["hcol_am"])
        E("scalar", (lambda nb=nb: (lambda s: s.activation(T["hcol"].ap()[:, 4:4+nb],
            T["hcol"].ap()[:, 4:4+nb], AF.Exp, scale=-0.5)))(),
          reads=["hcol_sq"], writes=["hcol_sq"])
        E("scalar", (lambda nb=nb: (lambda s: s.activation(
            T["hcol"].ap()[:, 8:8+nb], T["hcol"].ap()[:, 0:nb],
            AF.Exp, scale=0.5, bias=T["statc"].ap()[:, 15:16])))(),
          reads=["hcol_am", "lnw2col"], writes=["hcol_cs"])
        E("vector", (lambda nb=nb: (lambda v: v.tensor_scalar(
            T["hcol"].ap()[:, 8:8+nb], T["hcol"].ap()[:, 8:8+nb],
            T["statc"].ap()[:, 16:17], None, ALU.mult)))(),
          reads=["hcol_cs", "r127"], writes=["hcol_cs"])
        E("scalar", (lambda nb=nb: (lambda s: s.activation(
            T["hcol"].ap()[:, 12:12+nb], T["hcol"].ap()[:, 0:nb],
            AF.Exp, scale=-0.5)))(),
          reads=["hcol_am"], writes=["hcol_hs"])
        E("vector", (lambda nb=nb: (lambda v: v.tensor_mul(
            T["hcol"].ap()[:, 12:12+nb], T["hcol"].ap()[:, 12:12+nb],
            T["hcol"].ap()[:, 4:4+nb])))(),
          reads=["hcol_hs", "hcol_sq"], writes=["hcol_hs"])
        E("vector", (lambda nb=nb: (lambda v: v.tensor_scalar(
            T["hcol"].ap()[:, 12:12+nb], T["hcol"].ap()[:, 12:12+nb],
            127.0, None, ALU.mult)))(), reads=["hcol_hs"], writes=["hcol_hs"])
        for c in range(nb):
            dma("sync", ap("row_d")[2:3, 128*c:128*(c+1)], ap("hcol")[:, 12+c:13+c],
                "d_h1", reads=["hcol_hs"], writes=[f"rdlane2_{c}"])
            dma("sync", ap("row_d")[3:4, 128*c:128*(c+1)], ap("hcol")[:, 8+c:9+c],
                "d_h3", reads=["hcol_cs"], writes=[f"rdlane3_{c}"])
        dma("sync", ap("fq1")[:, 0:SW],
            ap("row_d")[2:3, 0:SW].partition_broadcast(128), "d_h2",
            reads=[f"rdlane2_{c}" for c in range(nb)], writes=["fq1"])
        dma("gpsimd", ap("zbt")[:, 0:SW],
            ap("row_d")[3:4, 0:SW].partition_broadcast(128), "d_h4",
            reads=[f"rdlane3_{c}" for c in range(nb)], writes=["zbt"])
        for oc in range(nfc):
            E("vector", (lambda oc=oc: (lambda v: v.scalar_tensor_tensor(
                T["fq2"].ap()[:, 0:SW], T["QF"].ap()[:, SW*oc:SW*oc+SW],
                T["Ccol"].ap()[:, 0:1], T["fq1"].ap()[:, 0:SW],
                ALU.bypass, ALU.mult)))(),
              reads=[f"QFo{oc}", "fq1"], writes=["fq2"])
            E("vector", (lambda oc=oc: (lambda v: v.tensor_scalar(
                T["fq2"].ap()[:, 0:SW], T["fq2"].ap()[:, 0:SW],
                T["Ccol"].ap()[:, 0:1], None, ALU.add)))(),
              reads=["fq2", "Ccol"], writes=["fq2"])
            E("vector", (lambda oc=oc: (lambda v: v.tensor_scalar(
                T["VF"].ap()[:, SW*oc:SW*oc+SW], T["fq2"].ap()[:, 0:SW],
                T["Ccol"].ap()[:, 0:1], None, ALU.subtract)))(),
              reads=["fq2", "Ccol"], writes=[f"VFo{oc}"])
        for i2g in range((nhc + 3) // 4):
            i2s = list(range(4 * i2g, min(4 * i2g + 4, nhc)))
            for o in range(nfc):
                wsn = "W2s" if o % 2 == 0 else "W1s"
                dma("sync", ap(wsn)[:, 0:128*len(i2s)],
                    ap("w2i_d")[128*o:128*(o+1), 128*i2s[0]:128*i2s[0]+128*len(i2s)],
                    "d_w2", writes=[wsn])
                for j, i2 in enumerate(i2s):
                    E("tensor", (lambda o=o, j=j, tg0=tg0, wsn=wsn: (lambda te: te.matmul(
                        PS[0:128, 2048+512*j:2048+512*j+SW],
                        T[wsn].ap()[:, 128*j:128*(j+1)],
                        T["VF"].ap()[:, SW*o:SW*o+SW],
                        start=(o == 0), stop=(o == nfc-1))))(),
                      reads=[wsn, f"VFo{o}"], writes=[f"psB{4+j}"])
            for j, i2 in enumerate(i2s):
                E("vector", (lambda j=j: (lambda v: v.tensor_tensor(
                    T["fq2"].ap()[:, 0:SW], PS[0:128, 2048+512*j:2048+512*j+SW],
                    T["zbt"].ap()[0:128, 0:SW], ALU.mult)))(),
                  reads=[f"psB{4+j}", "zbt"], writes=["fq2"])
                E("vector", (lambda i2=i2: (lambda v: v.tensor_scalar_add(
                    T["XN1"].ap()[:, SW*i2:SW*i2+SW], T["fq2"].ap()[:, 0:SW],
                    T["b2_s"].ap()[:, i2:i2+1])))(),
                  reads=["fq2", "b2_s"], writes=["XN1"])
        for tt in range(SW // 128):
            t = (tg0 // 128) + tt
            for c in range(nhc):
                E("tensor", (lambda tt=tt, c=c: (lambda te: te.matmul(
                    PS[0:128, 2048+128*c:2048+128*(c+1)],
                    T["XN1"].ap()[:, SW*c+128*tt:SW*c+128*(tt+1)], T["ident"].ap()[:, :],
                    start=True, stop=True)))(), reads=["XN1", "ident"],
                  writes=psnames(2048+128*c, 2048+128*(c+1)))
            E("vector", lambda v: v.tensor_copy(T["fq1"].ap()[:, 0:HID],
                PS[0:128, 2048:2048+HID]), reads=psnames(2048, 2048+HID), writes=["fq1"])
            ln_tok_chain("fq1", (lambda: T["fq1"].ap()[:, 0:HID]), 1e-5,
                         "toktmp", (lambda: T["toktmp"].ap()[:, 0:HID]), HID)
            dma("sync", ap("x2t")[:, :], ap("x2_d")[xoff+128*t:xoff+128*(t+1), :],
                "d_x3", reads=[f"x2d_{b}_{t}"], writes=["x2t"])
            E("vector", lambda v: v.tensor_add(T["xtile"].ap()[:, :],
                T["toktmp"].ap()[:, 0:HID], T["x2t"].ap()[:, :]),
              reads=["toktmp", "x2t"], writes=["xtile"])
            dma("sync", ap("y")[xoff+128*t:xoff+128*(t+1), :], ap("xtile")[:, :],
                "d_out", reads=["xtile"], writes=["yout"])

    E("vector", lambda v: v.tensor_copy(T["sqscr"].ap()[0:1, 0:1],
                                        T["sqscr"].ap()[0:1, 0:1]),
      writes=["QF", "VF"] + [f"QFo{oc}" for oc in range(nfc)]
             + [f"VFo{o}" for o in range(nfc)])


def emit_block(k):
    import contextlib
    plan = k.plan
    nc = k.nc
    import os
    steps = plan.finalize(keep_all=bool(os.environ.get("KEEP_ALL_SEMS")))
    with contextlib.ExitStack() as st:
        sems = {}
        for name in plan.cnt:
            sems[name] = st.enter_context(nc.semaphore(name))
        with nc.Block() as block:
            def mk(eng_name):
                def body(eng):
                    for s_ in steps[eng_name]:
                        if s_[0] == "wait":
                            eng.wait_ge(sems[s_[1]], s_[2])
                        else:
                            ins = s_[1](eng)
                            if s_[2] is not None:
                                ins.then_inc(sems[s_[2]], 16 if plan.is_dma[s_[2]] else 1)
                return body
            block.sync(mk("sync"))
            block.gpsimd(mk("gpsimd"))
            block.tensor(mk("tensor"))
            block.vector(mk("vector"))
            block.scalar(mk("scalar"))
    return k.nc


def host_inputs(cfg, ins, core):
    S = cfg["S"]; NH = cfg["NH"]; HID = 64 * NH; FF = cfg["FF"]; BPC = cfg["BPC"]
    nt = S // 128; nhc = HID // 128; nfc = FF // 128
    f32 = np.float32
    d = {}
    x = np.asarray(ins["L0_attn_x"], f32)
    d["x"] = np.ascontiguousarray(x[BPC*core:BPC*(core+1)].reshape(BPC*S, HID))
    for n, nm in (("Wq", "wq"), ("Wk", "wk"), ("Wv", "wv"), ("Wo", "wo")):
        d[nm] = np.ascontiguousarray(np.asarray(ins[f"L0_attn_{n}"], f32))
    for n, nm, w in (("bq", "bq2", nhc), ("bk", "bk2", nhc), ("bv", "bv2", nhc),
                     ("bo", "bo2", nhc)):
        d[nm] = np.ascontiguousarray(np.asarray(ins[f"L0_attn_{n}"], f32).reshape(w, 128).T)
    d["b12"] = np.ascontiguousarray(np.asarray(ins["L0_ffn_b1"], f32).reshape(nfc, 128).T)
    d["b22"] = np.ascontiguousarray(np.asarray(ins["L0_ffn_b2"], f32).reshape(nhc, 128).T)
    tk = np.asarray(ins["L0_attn_table_k"], f32)
    tv = np.asarray(ins["L0_attn_table_v"], f32)
    tkp = np.zeros((64, 65), f32)
    tkp[:, 0:NJ] = tk.T
    tkp[:, 32] = tk.T[:, 2 * MAXREL]
    d["tkT2"] = np.ascontiguousarray(np.concatenate([tkp, tkp], 0))
    d["tv"] = np.ascontiguousarray(tv)
    d["tv08"] = np.ascontiguousarray(tv[[0, 2 * MAXREL], :])
    d["w1T"] = np.ascontiguousarray(np.asarray(ins["L0_ffn_w1"], f32).T)
    d["w2T"] = np.ascontiguousarray(np.asarray(ins["L0_ffn_w2"], f32).T)
    d["c_ident"] = np.eye(128, dtype=f32).astype(bfnp)
    d["c_identd"] = np.tile(np.eye(64, dtype=f32), (2, 1)).astype(bfnp)
    coef = make_coefs(a2x_mode=True)
    d["c_coef"] = np.ascontiguousarray(coef.transpose(1, 0, 2).reshape(128, NJ * WWIN)).astype(bfnp)
    coefw = make_coefs(a2x_mode=False)
    d["c_coefw"] = np.ascontiguousarray(coefw.transpose(1, 0, 2).reshape(128, NJ * WWIN)).astype(bfnp)
    d["c_ones_row"] = np.ones((1, 128), f32).astype(bfnp)
    d["c_ones_colb"] = np.ones((128, 1), f32).astype(bfnp)
    d["c_ones_colf"] = np.ones((128, 1), f32)
    sel = np.zeros((2, nt * nt), f32)
    for s in range(nt):
        for c in range(nt):
            if c < s:
                sel[0, s * nt + c] = 1.0
            elif c > s:
                sel[1, s * nt + c] = 1.0
    d["c_sel"] = sel.astype(bfnp)
    vind = np.zeros((128, nt * 33), f32)
    for c in range(nt):
        vind[:, c * 33 + c] = 1.0
        vind[:, c * 33 + 32] = 1.0
    d["c_vind"] = vind.astype(bfnp)
    d["c_scal"] = np.ascontiguousarray(
        np.tile(np.array(SCAL_VALS(HID, FF), f32)[None, :], (128, 1)))
    d["c_p52"] = np.full((128, 1), 2.0 ** 52, f32)
    return d


_CFG_FULL = {"S": 2048, "NH": 16, "FF": 4096, "BPC": 2}


def run_full(ins, ncores=8, cfg=None):
    from concourse.bass_utils import run_bass_kernel_spmd
    cfg = cfg or _CFG_FULL
    k = build_program(cfg)
    nc = emit_block(k)
    in_maps = [host_inputs(cfg, ins, c) for c in range(ncores)]
    res = run_bass_kernel_spmd(nc, in_maps, core_ids=list(range(ncores)))
    BPC, S, HID = cfg["BPC"], cfg["S"], 64 * cfg["NH"]
    out = np.concatenate([r["y"].reshape(BPC, S, HID) for r in res.results], axis=0)
    return out


NCORES = 8


def kernel(**inputs):
    from concourse.bass_utils import run_bass_kernel_spmd
    cfg = _CFG_FULL
    k = build_program(cfg)
    nc = emit_block(k)
    in_maps = [host_inputs(cfg, inputs, c) for c in range(NCORES)]
    res = run_bass_kernel_spmd(nc, in_maps, core_ids=list(range(NCORES)))
    BPC, S, HID = cfg["BPC"], cfg["S"], 64 * cfg["NH"]
    out = np.concatenate([r["y"].reshape(BPC, S, HID) for r in res.results], axis=0)
    return out


# version 20 (final)
# speedup vs baseline: 1.2338x; 1.0341x over previous
import numpy as np
import ml_dtypes

import concourse.bass as bass
import concourse.mybir as mybir

F32 = mybir.dt.float32
BF16 = mybir.dt.bfloat16
ALU = mybir.AluOpType
AF = mybir.ActivationFunctionType
AX = mybir.AxisListType
bfnp = ml_dtypes.bfloat16

MAXREL = 4
NJ = 9
LPAD, RPAD = 64, 4
WWIN = LPAD + 128 + RPAD

ENGINES = ("sync", "gpsimd", "tensor", "vector", "scalar")


def SCAL_VALS(HID, FF=4096):
    import math
    ln127 = math.log(127.0)
    return [0.0, 0.5, 1.0, -1.0, -0.125, 1e-5, 1.0 / HID, 1e-6,
            ln127, -ln127, 0.125, -0.5, 1.0 / FF, 1.0 / (HID * FF), 8388608.0, 536870912.0,
            805306368.0, 1.5, 12582912.0, 0.25, 127.0, 1.0 / 127.0]


def make_coefs(a2x_mode=False):
    coef = np.zeros((NJ, 128, WWIN), np.float32)
    p = np.arange(128)[:, None]
    kw = np.arange(WWIN)[None, :] - LPAD
    jj = np.clip(kw - p, -MAXREL, MAXREL) + MAXREL
    main = (kw >= 0) & (kw < 128)
    left = kw < 0
    right = kw >= 128
    for j in range(NJ):
        coef[j][main & (jj == j)] += 1.0
        if 1 <= j <= 3:
            coef[j][left & (jj == j)] += 1.0
            coef[0][left & (jj == j)] -= 1.0
        if 5 <= j <= 7:
            coef[j][right & (jj == j)] += 1.0
            coef[8][right & (jj == j)] -= 1.0
    if a2x_mode:
        coef[8] -= np.broadcast_to(main, (128, WWIN)).astype(np.float32)
    return coef


class Plan:

    def __init__(self):
        self.steps = {e: [] for e in ENGINES}
        self.cnt = {}
        self.is_dma = {}
        self.last_write = {}
        self.readers = {}
        self.waited = {e: {} for e in ENGINES}
        self.eng_sem = {e: f"S_{e}" for e in ENGINES}
        self.dma_gen = {}
        self.sem_prev = {}
        for e in ENGINES:
            self.cnt[self.eng_sem[e]] = 0
            self.is_dma[self.eng_sem[e]] = False

    def dma_sem(self, name):
        if name not in self.cnt:
            self.cnt[name] = 0
            self.is_dma[name] = True
        return name

    def _wait(self, eng, sem, v):
        if v <= 0:
            return
        if self.waited[eng].get(sem, 0) >= v:
            return
        self.waited[eng][sem] = v
        self.steps[eng].append(("wait", sem, v))

    def emit(self, eng, fn, reads=(), writes=(), dma=None):
        for b in reads:
            if b in self.last_write:
                self._wait(eng, *self.last_write[b])
        for b in writes:
            if b in self.last_write:
                self._wait(eng, *self.last_write[b])
            for rs in self.readers.get(b, ()):
                self._wait(eng, *rs)
        if dma is not None:
            gen = self.dma_gen.setdefault(dma, 0)
            self.dma_gen[dma] = gen + 1
            sem = self.dma_sem(f"{dma}.{gen % 3}")
            if self.cnt[sem] > 0:
                self._wait(eng, sem, self.cnt[sem])
            self.cnt[sem] += 16
        else:
            sem = self.eng_sem[eng]
            self.cnt[sem] += 1
        v = self.cnt[sem]
        self.steps[eng].append(("op", fn, sem, v))
        for b in reads:
            self.readers.setdefault(b, []).append((sem, v))
        for b in writes:
            self.last_write[b] = (sem, v)
            self.readers[b] = []
        return (sem, v)

    def barrier_read(self, eng, bufs):
        for b in bufs:
            if b in self.last_write:
                self._wait(eng, *self.last_write[b])

    def drain_all_dma(self, eng):
        for sem, n in self.cnt.items():
            if self.is_dma[sem] and n > 0:
                self._wait(eng, sem, n)

    def finalize(self, keep_all=False):
        if keep_all:
            out = {}
            for e in ENGINES:
                lst = []
                for st in self.steps[e]:
                    if st[0] == "wait":
                        lst.append(st)
                    else:
                        lst.append(("op", st[1], st[2], 0))
                out[e] = lst
            return out
        needed = {}
        for e in ENGINES:
            for st in self.steps[e]:
                if st[0] == "wait":
                    needed.setdefault(st[1], set()).add(st[2])
        remap = {}
        for sem, vals in needed.items():
            sv = sorted(vals)
            remap[sem] = {v: i + 1 for i, v in enumerate(sv)}
        out = {}
        for e in ENGINES:
            lst = []
            for st in self.steps[e]:
                if st[0] == "wait":
                    sem, v = st[1], st[2]
                    if self.is_dma[sem]:
                        lst.append(("wait", sem, v))
                    else:
                        lst.append(("wait", sem, remap[sem][v]))
                else:
                    _, fn, sem, v = st
                    keep = self.is_dma[sem] or (sem in remap and v in remap[sem])
                    lst.append(("op", fn, sem if keep else None, 0))
            out[e] = lst
        return out


def build_consts(cfg):
    c = {}
    c["ident"] = np.eye(128, dtype=np.float32).astype(bfnp)
    coef = make_coefs()
    c["coef"] = np.ascontiguousarray(
        coef.transpose(1, 0, 2).reshape(128, NJ * WWIN)).astype(bfnp)
    c["ones_row"] = np.ones((1, 128), np.float32).astype(bfnp)
    c["ones_col"] = np.ones((128, 1), np.float32).astype(bfnp)
    c["ones_col_f32"] = np.ones((128, 1), np.float32)
    nt = cfg["S"] // 128
    sel = np.zeros((2, nt * nt), np.float32)
    for s in range(nt):
        for cc in range(nt):
            if cc < s:
                sel[0, s * nt + cc] = 1.0
            elif cc > s:
                sel[1, s * nt + cc] = 1.0
    c["sel"] = sel.astype(bfnp)
    return c


class K:

    def __init__(self, cfg):
        self.cfg = cfg
        self.nc = bass.Bass("TRN2", target_bir_lowering=False)
        self.plan = Plan()
        self.T = {}
        self.uid = 0

    def u(self, p="t"):
        self.uid += 1
        return f"{p}{self.uid}"

    def dram_in(self, name, shape, dt=F32):
        h = self.nc.dram_tensor(name, list(shape), dt, kind="ExternalInput")
        self.T[name] = h
        return h

    def dram_out(self, name, shape, dt=F32):
        h = self.nc.dram_tensor(name, list(shape), dt, kind="ExternalOutput")
        self.T[name] = h
        return h

    def dram_tmp(self, name, shape, dt=F32):
        h = self.nc.dram_tensor(name, list(shape), dt, kind="Internal")
        self.T[name] = h
        return h

    def sbuf(self, name, shape, dt=BF16):
        h = self.nc.alloc_sbuf_tensor(name, list(shape), dt)
        self.T[name] = h
        return h

    def psum(self, name, shape, dt=F32):
        h = self.nc.alloc_psum_tensor(name, list(shape), dt)
        self.T[name] = h
        return h




def build_program(cfg):
    S = cfg["S"]; NH = cfg["NH"]; HID = 64 * NH; FF = cfg["FF"]; BPC = cfg["BPC"]
    nt = S // 128; nhc = HID // 128; nfc = FF // 128
    NTOK = BPC * S
    nst = S // 512 if S >= 512 else 1
    SW = min(512, S)
    k = K(cfg)
    nc, plan, T = k.nc, k.plan, k.T
    ap = lambda n: T[n].ap()
    E = plan.emit

    k.dram_in("x", [NTOK, HID])
    for n in ("wq", "wk", "wv", "wo"):
        k.dram_in(n, [HID, HID])
    for n, w in (("bq2", nhc), ("bk2", nhc), ("bv2", nhc), ("bo2", nhc),
                 ("b12", nfc), ("b22", nhc)):
        k.dram_in(n, [128, w])
    k.dram_in("tkT2", [128, 65]); k.dram_in("tv", [NJ, 64]); k.dram_in("tv08", [2, 64])
    k.dram_in("w1T", [HID, FF]); k.dram_in("w2T", [FF, HID])
    k.dram_in("c_ident", [128, 128], BF16)
    k.dram_in("c_identd", [128, 64], BF16)
    k.dram_in("c_coef", [128, NJ * WWIN], BF16)
    k.dram_in("c_coefw", [128, NJ * WWIN], BF16)
    k.dram_in("c_ones_row", [1, 128], BF16)
    k.dram_in("c_ones_colb", [128, 1], BF16)
    k.dram_in("c_ones_colf", [128, 1], F32)
    k.dram_in("c_sel", [2, nt * nt], BF16)
    k.dram_in("c_vind", [128, nt * 33], BF16)
    k.dram_in("c_scal", [128, 22], F32)
    k.dram_in("c_p52", [128, 1], F32)
    k.dram_out("y", [NTOK, HID])
    k.dram_tmp("w1i_d", [HID, FF], BF16)
    k.dram_tmp("w2i_d", [FF, HID], BF16)
    k.dram_tmp("row_d", [8, max(S, 2048)], F32)
    k.dram_tmp("x2_d", [NTOK, HID], BF16)

    sb = k.sbuf
    sb("ident", [128, 128]); sb("identd", [128, 64])
    sb("coef", [128, NJ * WWIN]); sb("coefw", [128, NJ * WWIN]); sb("ones_row", [1, 128])
    sb("ones_colb", [128, 1]); sb("sel", [2, nt * nt]); sb("vind", [128, nt * 33])
    sb("tvsel", [16, nt * 64])
    sb("ones_colf", [128, 1], F32)
    sb("scal", [128, 22], F32)
    sb("p52", [128, 1], F32)
    sb("Ccol", [128, 1], F32)
    sb("tkT2_b", [128, 65])
    sb("KQ0a", [128, 128]); sb("KQ1a", [128, 128])
    sb("KQ0b", [128, 128]); sb("KQ1b", [128, 128])
    sb("s08row", [1, S]); sb("tv_b", [NJ, 64]); sb("tv08_b", [2, 64])
    for n, w in (("bq_s", nhc), ("bk_s", nhc), ("bv_s", nhc), ("bo_s", nhc),
                 ("b1_s", nfc), ("b2_s", nhc)):
        sb(n, [128, w], F32)
    sb("Wc0", [128, 128]); sb("Wc1", [128, 128])
    sb("XN1", [128, nhc * S])
    qvsz = max(nhc * S, nfc * min(512, S))
    sb("QF", [128, qvsz]); sb("KF", [128, nhc * S]); sb("VF", [128, qvsz])
    for r in range(3):
        sb(f"F{r}", [128, S])
    for r in range(4):
        sb(f"A2X{r}", [128, WWIN])
    VW = 64 + 32 + 1
    sb("vpm", [128, nt * VW])
    sb("SattnT", [NJ, S])
    sb("Sqp", [128, nt * NJ], F32)
    sb("FBqp", [128, nt * NJ], F32)
    sb("FBT", [NJ, S])
    sb("Rrows", [16, S])
    sb("zbt", [128, S])
    sb("negM", [128, 16], F32)
    sb("ttr_scr", [128, WWIN], F32)
    sb("vj", [128, 8], F32)
    sb("slotD0", [128, WWIN]); sb("slotD1", [128, WWIN])
    sb("fwv0", [128, WWIN]); sb("fwv1", [128, WWIN])
    sb("rowf", [1, 512], F32)
    sb("x2t", [128, HID])
    sb("xtile", [128, HID], F32)
    sb("toktmp", [128, max(HID, 1024)])
    sb("sqscr", [128, max(HID, 1024)])
    sb("statc", [128, 18], F32)
    sb("maxs", [128, 512], F32)
    sb("hcol", [128, 16], F32)
    sb("W1s", [128, 512])
    sb("W2s", [128, 512])
    sb("fq1", [128, max(HID, 1024)], F32)
    sb("fq2", [128, 512], F32); sb("x2t2", [128, 512], F32)
    k.psum("PS", [128, 4096], F32)
    PS = T["PS"].ap()

    def dma(q, dst, src, sem, reads=(), writes=()):
        E(q, (lambda d=dst, s=src: (lambda eng: eng.dma_start(out=d, in_=s)))(),
          reads=reads, writes=writes, dma=sem)

    for name, dst in (("c_ident", "ident"), ("c_identd", "identd"), ("c_coef", "coef"), ("c_coefw", "coefw"),
                      ("c_ones_row", "ones_row"), ("c_ones_colb", "ones_colb"),
                      ("c_sel", "sel"), ("c_vind", "vind")):
        dma("sync", ap(dst)[:, :], ap(name)[:, :], "d_c", writes=[dst])
    dma("sync", ap("ones_colf")[:, :], ap("c_ones_colf")[:, :], "d_c", writes=["ones_colf"])
    dma("sync", ap("scal")[:, :], ap("c_scal")[:, :], "d_c", writes=["scal"])
    for _i in range(2):
        k.dram_tmp(f"a2xd{_i}", [1, 128 * (WWIN + 2)], BF16)
        k.dram_tmp(f"fwd{_i}", [1, 128 * (WWIN + 1)], BF16)
    plan.emit("vector", lambda v: v.memset(T["slotD0"].ap()[:, :], 0.0),
              writes=["slotD0"])
    for _i in range(2):
        _v198 = ap(f"a2xd{_i}")[0:1, :].rearrange("o (p w) -> (o p) w", w=WWIN + 2)
        dma("sync", _v198[:, 0:WWIN], T["slotD0"].ap()[:, 0:WWIN],
            "d_az", reads=["slotD0"], writes=[f"a2xz{_i}"])
        dma("sync", _v198[:, WWIN:WWIN + 2], T["slotD0"].ap()[:, 0:2],
            "d_az", reads=["slotD0"], writes=[f"a2xz{_i}b"])
    for _nm in ("KQ0a", "KQ1a", "KQ0b", "KQ1b"):
        plan.emit("vector", (lambda _nm=_nm: (lambda v: v.memset(T[_nm].ap()[:, :], 0.0)))(),
                  writes=[f"{_nm}_z"])
    dma("sync", ap("p52")[:, :], ap("c_p52")[:, :], "d_c", writes=["p52"])
    for ci, cv in enumerate(SCAL_VALS(HID, FF)):
        nc.const_aps.register(F32, cv, T["scal"].ap()[:, ci:ci+1])
    E("vector", lambda v: v.tensor_scalar_add(T["statc"].ap()[:, 0:1],
        T["p52"].ap()[:, 0:1], 8388608.0), reads=["p52"], writes=["st0"])
    E("vector", lambda v: v.tensor_sub(T["statc"].ap()[:, 1:2],
        T["statc"].ap()[:, 0:1], T["p52"].ap()[:, 0:1]),
      reads=["st0", "p52"], writes=["st1"])
    E("vector", lambda v: v.tensor_scalar_mul(T["statc"].ap()[:, 2:3],
        T["statc"].ap()[:, 1:2], 805306368.0), reads=["st1"], writes=["st2"])
    E("vector", lambda v: v.tensor_scalar_mul(T["statc"].ap()[:, 1:2],
        T["statc"].ap()[:, 1:2], 1.5), reads=["st1"], writes=["st1"])
    E("vector", lambda v: v.tensor_sub(T["statc"].ap()[:, 2:3],
        T["statc"].ap()[:, 2:3], T["statc"].ap()[:, 1:2]),
      reads=["st2", "st1"], writes=["st2"])
    E("vector", lambda v: v.tensor_scalar_add(T["Ccol"].ap()[:, 0:1],
        T["statc"].ap()[:, 2:3], 12582912.0), reads=["st2"], writes=["Ccol"])
    for name, dst in (("bq2", "bq_s"), ("bk2", "bk_s"), ("bv2", "bv_s"),
                      ("bo2", "bo_s"), ("b12", "b1_s"), ("b22", "b2_s")):
        dma("sync", ap(dst)[:, :], ap(name)[:, :], "d_c", writes=[dst])
    for name, dst in (("tkT2", "tkT2_b"), ("tv", "tv_b"), ("tv08", "tv08_b")):
        dma("gpsimd", ap(dst)[:, :], ap(name)[:, :], "d_cg", writes=[dst])
    E("vector", lambda v: v.tensor_sub(T["tkT2_b"].ap()[:, 64:65],
        T["tkT2_b"].ap()[:, 0:1], T["tkT2_b"].ap()[:, 32:33]),
      reads=["tkT2_b"], writes=["tkT2_b"])
    for c in range(nt):
        dma("sync", ap("vpm")[:, VW*c+64:VW*(c+1)], ap("c_vind")[:, 33*c:33*(c+1)],
            "d_c", writes=["vpm"])
    for s in range(nt):
        E("tensor", (lambda s=s: (lambda te: te.matmul(
            T["PS"].ap()[0:nt, 3584+64*(s % 8):3584+64*(s % 8)+64],
            T["sel"].ap()[:, nt*s:nt*(s+1)], T["tv08_b"].ap()[:, :],
            start=True, stop=True)))(), reads=["sel", "tv08_b"], writes=["psB7"])
        E("vector", (lambda s=s: (lambda v: v.tensor_copy(
            T["tvsel"].ap()[0:nt, 64*s:64*(s+1)],
            T["PS"].ap()[0:nt, 3584+64*(s % 8):3584+64*(s % 8)+64])))(),
          reads=["psB7"], writes=["tvsel"])

    def wquant(srcname, dstname, rows, cols, lane):
        nr = rows // 128
        npc = (cols + 1023) // 1024
        first = True
        for r in range(nr):
          for pc in range(npc):
            c0, c1 = 1024 * pc, min(1024 * (pc + 1), cols)
            ld = "fq1" if (r * npc + pc) % 2 == 0 else "xtile"
            dma("sync", ap(ld)[:, 0:c1-c0], ap(srcname)[128*r:128*(r+1), c0:c1],
                "d_wq", writes=[ld])
            E("vector", (lambda c0=c0, c1=c1, ld=ld: (lambda v: v.tensor_reduce(
                T["statc"].ap()[:, 1:2], T[ld].ap()[:, 0:c1-c0], AX.X, ALU.add,
                apply_absolute_value=True)))(), reads=[ld], writes=["st1"])
            if first:
                first = False
                E("vector", lambda v: v.tensor_copy(T["statc"].ap()[:, 0:1],
                                                    T["statc"].ap()[:, 1:2]),
                  reads=["st1"], writes=["st0"])
            else:
                E("vector", lambda v: v.tensor_add(T["statc"].ap()[:, 0:1],
                                                   T["statc"].ap()[:, 0:1],
                                                   T["statc"].ap()[:, 1:2]),
                  reads=["st1", "st0"], writes=["st0"])
        E("tensor", lambda t: t.matmul(PS[0:1, 3584:3585], T["ones_colf"].ap()[:, 0:1],
                                       T["statc"].ap()[:, 0:1], start=True, stop=True),
          reads=["st0", "ones_colf"], writes=["psB7"])
        E("scalar", (lambda rows=rows, cols=cols: (lambda s: s.activation(
            T["statc"].ap()[0:1, 2:3], PS[0:1, 3584:3585], AF.Copy,
            scale=1.0 / (rows * cols))))(), reads=["psB7"], writes=["st2"])
        E("vector", lambda v: v.tensor_scalar_max(T["statc"].ap()[0:1, 2:3],
                                                  T["statc"].ap()[0:1, 2:3], 1e-5),
          reads=["st2"], writes=["st2"])
        E("scalar", lambda s: s.activation(T["statc"].ap()[0:1, 3:4],
                                           T["statc"].ap()[0:1, 2:3], AF.Ln),
          reads=["st2"], writes=["st3"])
        E("scalar", lambda s: s.activation(T["statc"].ap()[0:1, 3:4],
                                           T["statc"].ap()[0:1, 3:4], AF.Exp, scale=-1.0),
          reads=["st3"], writes=["st3"])
        dma("sync", ap("row_d")[lane:lane+1, 0:1], ap("statc")[0:1, 2:3],
            "d_wm", reads=["st2"], writes=[f"rdlane{lane}"])
        dma("sync", ap("row_d")[lane+2:lane+3, 0:1], ap("statc")[0:1, 3:4],
            "d_wm", reads=["st3"], writes=[f"rdlane{lane+2}"])
        dma("sync", ap("statc")[:, 3:4],
            ap("row_d")[lane+2:lane+3, 0:1].partition_broadcast(128),
            "d_wm2", reads=[f"rdlane{lane+2}"], writes=["st3c"])
        npc2 = (cols + 511) // 512
        for r in range(nr):
          for pc in range(npc2):
            c0, c1 = 512 * pc, min(512 * (pc + 1), cols)
            w_ = c1 - c0
            i_ = r * npc2 + pc
            ld = "fq1" if i_ % 2 == 0 else "xtile"
            lq = "fq2" if i_ % 2 == 0 else "x2t2"
            so_ = 512 * (i_ % 2)
            dma("sync", ap(ld)[:, 0:w_], ap(srcname)[128*r:128*(r+1), c0:c1],
                "d_wq", writes=[ld])
            E("vector", (lambda w_=w_, ld=ld, lq=lq: (lambda v: v.tensor_scalar(
                T[lq].ap()[:, 0:w_], T[ld].ap()[:, 0:w_],
                T["statc"].ap()[:, 3:4], T["Ccol"].ap()[:, 0:1],
                ALU.mult, ALU.add)))(),
              reads=[ld, "st3c", "Ccol"], writes=[lq])
            E("vector", (lambda w_=w_, lq=lq: (lambda v: v.tensor_scalar(
                T[lq].ap()[:, 0:w_], T[lq].ap()[:, 0:w_],
                T["Ccol"].ap()[:, 0:1], None, ALU.subtract)))(),
              reads=[lq, "Ccol"], writes=[lq])
            E("vector", (lambda w_=w_, lq=lq, so_=so_: (lambda v: v.tensor_scalar(
                T["sqscr"].ap()[:, so_:so_+w_], T[lq].ap()[:, 0:w_],
                1.0, -1.0, ALU.min, ALU.max)))(), reads=[lq],
              writes=[f"sqscr{i_ % 2}"])
            dma("gpsimd" if i_ % 2 else "sync",
                ap(dstname)[128*r:128*(r+1), c0:c1], ap("sqscr")[:, so_:so_+w_],
                "d_wqo" + str(i_ % 2), reads=[f"sqscr{i_ % 2}"], writes=[f"{dstname}{r}"])
    wquant("w1T", "w1i_d", HID, FF, 0)
    wquant("w2T", "w2i_d", FF, HID, 1)
    plan.emit("vector", lambda v: v.tensor_copy(T["sqscr"].ap()[0:1, 0:1],
                                                T["sqscr"].ap()[0:1, 0:1]),
              writes=["sqscr", "sqscr0", "sqscr1", "fq1", "fq2", "xtile", "x2t2"])

    for n in ("wq", "wk", "wv", "wo"):
        k.dram_tmp(f"{n}b_d", [HID, HID], BF16)
        dma("gpsimd", ap(f"{n}b_d")[:, :], ap(n)[:, :], "d_wc", writes=[f"{n}b"])

    for b in range(BPC):
        xoff = b * S

        for t in range(nt):
            dma("sync", ap("xtile")[:, :], ap("x")[xoff+128*t:xoff+128*(t+1), :],
                "d_x", writes=["xtile"])
            E("scalar", lambda s: s.activation(T["sqscr"].ap()[:, 0:HID],
                T["xtile"].ap()[:, :], AF.Square, accum_out=T["statc"].ap()[:, 1:2]),
              reads=["xtile"], writes=["sqscr", "st1"])
            E("vector", lambda v: v.reduce_sum(T["statc"].ap()[:, 0:1],
                T["xtile"].ap()[:, :], AX.X), reads=["xtile"], writes=["st0"])
            E("vector", (lambda H=HID: (lambda v: v.tensor_scalar_mul(
                T["statc"].ap()[:, 0:1], T["statc"].ap()[:, 0:1], 1.0 / H)))(),
              reads=["st0"], writes=["st0"])
            E("vector", lambda v: v.tensor_mul(T["statc"].ap()[:, 2:3],
                T["statc"].ap()[:, 0:1], T["statc"].ap()[:, 0:1]),
              reads=["st0"], writes=["st2"])
            E("vector", (lambda H=HID: (lambda v: v.scalar_tensor_tensor(
                T["statc"].ap()[:, 3:4], T["statc"].ap()[:, 1:2], 1.0 / H,
                T["statc"].ap()[:, 2:3], ALU.mult, ALU.subtract)))(),
              reads=["st1", "st2"], writes=["st3"])
            E("vector", lambda v: v.tensor_scalar_add(
                T["statc"].ap()[:, 3:4], T["statc"].ap()[:, 3:4],
                (1e-5 - float(np.float32(1e-5)))), reads=["st3"], writes=["st3"])
            E("scalar", lambda s: s.activation(T["statc"].ap()[:, 3:4],
                T["statc"].ap()[:, 3:4], AF.Ln, bias=1e-5), reads=["st3"], writes=["st3"])
            E("scalar", lambda s: s.activation(T["statc"].ap()[:, 3:4],
                T["statc"].ap()[:, 3:4], AF.Exp, scale=-0.5), reads=["st3"], writes=["st3"])
            E("vector", lambda v: v.tensor_scalar(T["toktmp"].ap()[:, 0:HID],
                T["xtile"].ap()[:, :], T["statc"].ap()[:, 0:1],
                T["statc"].ap()[:, 3:4], ALU.subtract, ALU.mult),
              reads=["xtile", "st0", "st3"], writes=["toktmp"])
            for c in range(nhc):
                E("tensor", (lambda c=c: (lambda te: te.matmul(
                    PS[0:128, 2048+128*c:2048+128*(c+1)],
                    T["toktmp"].ap()[:, 128*c:128*(c+1)], T["ident"].ap()[:, :],
                    start=True, stop=True)))(), reads=["toktmp", "ident"],
                  writes=psnames(2048+128*c, 2048+128*(c+1)))
            E("vector", (lambda t=t: (lambda v: v.tensor_copy(
                T["XN1"].ap().rearrange("p (c s) -> p c s", s=S)[:, :, 128*t:128*(t+1)],
                PS[0:128, 2048:2048+128*nhc].rearrange("p (c s) -> p c s", s=128))))(),
              reads=psnames(2048, 2048+128*nhc), writes=["XN1"])

        def proj(wname, bname, dstname):
            wslot = [0]
            for oc in range(nhc):
                    for i in range(nhc):
                        ws = f"Wc{wslot[0] % 2}"; wslot[0] += 1
                        dma("sync", T[ws].ap()[:, 0:128],
                            ap(f"{wname}b_d")[128*i:128*(i+1), 128*oc:128*(oc+1)], "d_w",
                            reads=[f"{wname}b"], writes=[ws])
                        for st4 in range(nst):
                            pso = 2048 + 512 * st4
                            pname = ["psB4", "psB5", "psB6w0", "psB7"][st4]
                            E("tensor", (lambda oc=oc, st4=st4, i=i, pso=pso, ws=ws: (lambda te: te.matmul(
                                PS[0:128, pso:pso+SW], T[ws].ap()[:, 0:128],
                                T["XN1"].ap()[:, S*i+SW*st4:S*i+SW*(st4+1)],
                                start=(i == 0), stop=(i == nhc-1))))(),
                              reads=[ws, "XN1"], writes=[pname])
                    for st4 in range(nst):
                        pso = 2048 + 512 * st4
                        pname = ["psB4", "psB5", "psB6w0", "psB7"][st4]
                        E("scalar", (lambda oc=oc, st4=st4, pso=pso, bname=bname, dstname=dstname:
                            (lambda s: s.activation(
                                T[dstname].ap()[:, S*oc+SW*st4:S*oc+SW*(st4+1)],
                                PS[0:128, pso:pso+SW], AF.Identity,
                                bias=T[bname].ap()[:, oc:oc+1])))(),
                          reads=[pname, bname], writes=[dstname])
        proj("wq", "bq_s", "QF")
        proj("wk", "bk_s", "KF")
        proj("wv", "bv_s", "VF")

        for side, FNAME, lane in (("q", "QF", 4), ("k", "KF", 5)):
            for hc in range(nhc):
                for half in range(2):
                    h = 2 * hc + half
                    hr = 64 * half
                    for st4 in range(nst):
                        E("scalar", (lambda hc=hc, hr=hr, st4=st4, FNAME=FNAME: (lambda s: s.activation(
                            T["sqscr"].ap()[hr:hr+64, 0:SW],
                            T[FNAME].ap()[hr:hr+64, S*hc+SW*st4:S*hc+SW*(st4+1)],
                            AF.Square)))(), reads=[FNAME], writes=["sqscr"])
                        E("tensor", (lambda st4=st4, hr=hr: (lambda te: te.matmul(
                            PS[0:1, 3584:3584+SW],
                            T["ones_colb"].ap()[hr:hr+64, 0:1],
                            T["sqscr"].ap()[hr:hr+64, 0:SW],
                            start=True, stop=True)))(),
                          reads=["sqscr", "ones_colb"], writes=["psB7"])
                        E("vector", (lambda: (lambda v: v.tensor_reduce(
                            T["statc"].ap()[0:1, 8:9], PS[0:1, 3584:3584+SW],
                            AX.X, ALU.max)))(), reads=["psB7"], writes=["st8"])
                        if st4 == 0:
                            E("vector", (lambda: (lambda v: v.tensor_copy(
                                T["statc"].ap()[0:1, 9:10], T["statc"].ap()[0:1, 8:9])))(),
                              reads=["st8"], writes=["st9"])
                        else:
                            E("vector", lambda v: v.tensor_max(
                                T["statc"].ap()[0:1, 9:10], T["statc"].ap()[0:1, 9:10],
                                T["statc"].ap()[0:1, 8:9]), reads=["st8", "st9"], writes=["st9"])
                    E("scalar", lambda s: s.activation(T["statc"].ap()[0:1, 9:10],
                        T["statc"].ap()[0:1, 9:10], AF.Ln), reads=["st9"], writes=["st9"])
                    E("scalar", lambda s: s.activation(T["statc"].ap()[0:1, 9:10],
                        T["statc"].ap()[0:1, 9:10], AF.Exp, scale=0.5), reads=["st9"], writes=["st9"])
                    dma("sync", ap("row_d")[lane:lane+1, h:h+1], ap("statc")[0:1, 9:10],
                        "d_m", reads=["st9"], writes=[f"rdl_{side}{h}"])
        if b == 0:
            E("scalar", lambda s: s.activation(T["sqscr"].ap()[0:64, 0:NJ],
                T["tkT2_b"].ap()[0:64, 0:NJ], AF.Square), reads=["tkT2_b"], writes=["sqscr"])
            E("tensor", lambda te: te.matmul(PS[0:1, 3584:3584+NJ],
                T["ones_colb"].ap()[0:64, 0:1], T["sqscr"].ap()[0:64, 0:NJ],
                start=True, stop=True), reads=["sqscr", "ones_colb"], writes=["psB7"])
            E("vector", lambda v: v.tensor_reduce(T["statc"].ap()[0:1, 10:11],
                PS[0:1, 3584:3584+NJ], AX.X, ALU.max), reads=["psB7"], writes=["st10"])
            E("scalar", lambda s: s.activation(T["statc"].ap()[0:1, 10:11],
                T["statc"].ap()[0:1, 10:11], AF.Ln), reads=["st10"], writes=["st10"])
            E("scalar", lambda s: s.activation(T["statc"].ap()[0:1, 10:11],
                T["statc"].ap()[0:1, 10:11], AF.Exp, scale=0.5), reads=["st10"], writes=["st10"])
            dma("sync", ap("row_d")[6:7, 0:1], ap("statc")[0:1, 10:11], "d_m",
                reads=["st10"], writes=["rdl_t"])
        build_attn(k, cfg, b)
        if not cfg.get("no_tail"):
            build_tail(k, cfg, b)
    return k

def psnames(c0, c1):
    out = []
    if c0 < 2048:
        out.append("psAV")
    for bkt, nm in ((2048, "psB4"), (2560, "psB5"), (3072, "psB6"), (3584, "psB7")):
        if c0 < bkt + 512 and c1 > bkt:
            out.append(nm)
    return out


def build_attn(k, cfg, b):
    S = cfg["S"]; NH = cfg["NH"]; HID = 64 * NH
    nt = S // 128; nhc = HID // 128
    nst = max(1, S // 512); SW = min(512, S)
    VW = 64 + 32 + 1
    T, plan = k.T, k.plan
    ap = lambda n: T[n].ap()
    PS = T["PS"].ap()
    E = plan.emit

    def dma(q, dst, src, sem, reads=(), writes=()):
        E(q, (lambda d=dst, s=src: (lambda eng: eng.dma_start(out=d, in_=s)))(),
          reads=reads, writes=writes, dma=sem)

    dma("sync", ap("maxs")[0:1, 0:NH], ap("row_d")[4:5, 0:NH], "d_m2",
        reads=[f"rdl_q{h}" for h in range(NH)], writes=["nmq"])
    dma("sync", ap("maxs")[0:1, 32:32+NH], ap("row_d")[5:6, 0:NH], "d_m2",
        reads=[f"rdl_k{h}" for h in range(NH)], writes=["nmk"])
    dma("sync", ap("maxs")[0:1, 64:65], ap("row_d")[6:7, 0:1], "d_m2",
        reads=["rdl_t"], writes=["nmt"])
    E("vector", lambda v: v.tensor_scalar(T["maxs"].ap()[0:1, 32:32+NH],
        T["maxs"].ap()[0:1, 32:32+NH], T["maxs"].ap()[0:1, 64:65], None, ALU.add),
      reads=["nmk", "nmt"], writes=["nmk"])
    E("vector", lambda v: v.tensor_mul(T["maxs"].ap()[0:1, 0:NH],
        T["maxs"].ap()[0:1, 0:NH], T["maxs"].ap()[0:1, 32:32+NH]),
      reads=["nmq", "nmk"], writes=["nmq"])
    E("vector", lambda v: v.tensor_scalar_mul(T["maxs"].ap()[0:1, 0:NH],
        T["maxs"].ap()[0:1, 0:NH], -0.125), reads=["nmq"], writes=["nmq"])
    dma("sync", ap("row_d")[7:8, 0:NH], ap("maxs")[0:1, 0:NH], "d_m3",
        reads=["nmq"], writes=["negm_r"])
    dma("sync", ap("negM")[:, 0:NH],
        ap("row_d")[7:8, 0:NH].partition_broadcast(128), "d_m4",
        reads=["negm_r"], writes=["negM"])

    for h in range(cfg.get("heads", NH)):
        hc, hr = h // 2, 64 * (h % 2)
        qf = lambda c0, c1, hr=hr, hc=hc: T["QF"].ap()[hr:hr+64, S*hc+c0:S*hc+c1]
        kf = lambda c0, c1, hr=hr, hc=hc: T["KF"].ap()[hr:hr+64, S*hc+c0:S*hc+c1]
        vf = lambda c0, c1, hr=hr, hc=hc: T["VF"].ap()[hr:hr+64, S*hc+c0:S*hc+c1]

        for t in range(nt):
            E("tensor", (lambda t=t, vf=vf, hr=hr: (lambda te: te.matmul(
                PS[0:128, 2048+64*t:2048+64*(t+1)], vf(128*t, 128*(t+1)),
                T["identd"].ap()[hr:hr+64, 0:64], start=True, stop=True)))(),
              reads=["VF", "identd"], writes=psnames(2048+64*t, 2048+64*(t+1)))
        E("vector", (lambda nt=nt: (lambda v: v.tensor_copy(
            T["vpm"].ap().rearrange("p (c w) -> p c w", w=VW)[:, :, 0:64],
            PS[0:128, 2048:2048+64*nt].rearrange("p (c w) -> p c w", w=64))))(),
          reads=psnames(2048, 2048+64*nt), writes=["vpm"])

        for st4 in range(nst):
            E("tensor", (lambda st4=st4, qf=qf, hr=hr: (lambda te: te.matmul(
                PS[0:65, 3584:3584+SW], T["tkT2_b"].ap()[hr:hr+64, :],
                qf(SW*st4, SW*(st4+1)), start=True, stop=True)))(),
              reads=["tkT2_b", "QF"], writes=["psB7"])
            E("scalar", (lambda st4=st4: (lambda s: s.activation(
                T["SattnT"].ap()[:, SW*st4:SW*(st4+1)], PS[0:NJ, 3584:3584+SW],
                AF.Copy)))(), reads=["psB7"], writes=["SattnT"])
            E("scalar", (lambda st4=st4: (lambda s: s.activation(
                T["s08row"].ap()[0:1, SW*st4:SW*(st4+1)], PS[64:65, 3584:3584+SW],
                AF.Copy)))(), reads=["psB7"], writes=["s08row"])

        for t in range(nt):
            E("tensor", (lambda t=t: (lambda te: te.matmul(
                PS[0:128, 3072+NJ*t:3072+NJ*(t+1)],
                T["SattnT"].ap()[0:NJ, 128*t:128*(t+1)], T["ident"].ap()[0:NJ, 0:NJ],
                start=True, stop=True)))(), reads=["SattnT", "ident"],
              writes=["psB6w0"])
        E("vector", (lambda nt=nt: (lambda v: v.tensor_copy(
            T["Sqp"].ap()[:, 0:NJ*nt], PS[0:128, 3072:3072+NJ*nt])))(),
          reads=["psB6w0"], writes=["Sqp"])


        def build_window(tw):
            slot = f"A2X{tw % 4}"
            blk = f"a2xd{tw % 2}"
            sD = f"slotD{tw % 2}"
            v198 = ap(blk)[0:1, :].rearrange("o (p w) -> (o p) w", w=WWIN + 2)
            v197 = ap(blk)[0:1, 0:128 * (WWIN + 1)].rearrange(
                "o (p w) -> (o p) w", w=WWIN + 1)
            E("vector", (lambda tw=tw: (lambda v: v.tensor_scalar(
                T["vj"].ap()[:, 0:8], T["Sqp"].ap()[:, NJ*tw:NJ*tw+8],
                T["Sqp"].ap()[:, NJ*tw+8:NJ*tw+9], None, ALU.subtract)))(),
              reads=["Sqp"], writes=["vj"])
            dma("gpsimd", v198[:, LPAD-3:LPAD+4], T["vj"].ap()[:, 1:8], "d_a2w",
                reads=["vj", f"a2xz{tw % 2}", f"a2xz{tw % 2}b"], writes=[f"a2dg{tw % 2}"])
            dma("sync", T[sD].ap()[:, 0:WWIN], v197[:, 0:WWIN], "d_a2r",
                reads=[f"a2dg{tw % 2}"], writes=[sD])
            E("vector", (lambda tw=tw, slot=slot: (lambda v: v.tensor_scalar_mul(
                T[slot].ap()[:, :], T["coef"].ap()[:, 0:WWIN],
                T["vj"].ap()[:, 0:1])))(),
              reads=["coef", "vj"], writes=[slot])
            E("vector", (lambda slot=slot, sD=sD: (lambda v: v.tensor_add(
                T[slot].ap()[:, :], T[slot].ap()[:, :], T[sD].ap()[:, :])))(),
              reads=[slot, sD], writes=[slot])
        build_window(0)
        if nt > 1:
            build_window(1)

        def fw_window(tw):
            c0 = 0 if tw > 0 else LPAD
            c1 = WWIN if tw < nt - 1 else LPAD + 128
            base = 3072 + 512 * (tw % 2)
            pw = "psB6w0" if tw % 2 == 0 else "psB7"
            fv = f"fwv{tw % 2}"
            blk = f"fwd{tw % 2}"
            if tw > 0:
                E("tensor", (lambda tw=tw, base=base: (lambda te: te.matmul(
                    PS[0:128, base:base+LPAD],
                    T[f"F{(tw-1) % 3}"].ap()[64:128, 128*tw:128*(tw+1)],
                    T["identd"].ap()[64:128, 0:64], start=True, stop=True)))(),
                  reads=[f"F{(tw-1) % 3}s{(128*tw) // SW}", "identd"], writes=[pw])
            E("tensor", (lambda tw=tw, base=base: (lambda te: te.matmul(
                PS[0:128, base+LPAD:base+LPAD+128],
                T[f"F{tw % 3}"].ap()[0:128, 128*tw:128*(tw+1)],
                T["ident"].ap()[:, :], start=True, stop=True)))(),
              reads=[f"F{tw % 3}s{(128*tw) // SW}", "ident"], writes=[pw])
            if tw < nt - 1:
                E("tensor", (lambda tw=tw, base=base: (lambda te: te.matmul(
                    PS[0:128, base+LPAD+128:base+WWIN],
                    T[f"F{(tw+1) % 3}"].ap()[0:RPAD, 128*tw:128*(tw+1)],
                    T["ident"].ap()[0:RPAD, 0:RPAD], start=True, stop=True)))(),
                  reads=[f"F{(tw+1) % 3}s{(128*tw) // SW}", "ident"], writes=[pw])
            if tw == 0:
                E("vector", (lambda fv=fv: (lambda v: v.memset(
                    T[fv].ap()[:, 0:LPAD], 0.0)))(), reads=[], writes=[fv])
            if tw == nt - 1:
                E("vector", (lambda fv=fv: (lambda v: v.memset(
                    T[fv].ap()[:, LPAD+128:WWIN], 0.0)))(), reads=[], writes=[fv])
            E("scalar", (lambda base=base, c0=c0, c1=c1, fv=fv: (lambda s: s.activation(
                T[fv].ap()[:, c0:c1], PS[0:128, base+c0:base+c1], AF.Copy)))(),
              reads=[pw], writes=[fv])
            dma("scalar", ap(blk)[0:1, 0:128 * WWIN].rearrange("o (p w) -> (o p) w", w=WWIN),
                T[fv].ap()[:, 0:WWIN], "d_fww", reads=[fv], writes=[f"fwb{tw % 2}"])
            dma("gpsimd",
                T["FBqp"].ap()[:, NJ*tw+1:NJ*tw+8],
                ap(blk)[0:1, 0:128 * (WWIN + 1)].rearrange(
                    "o (p w) -> (o p) w", w=WWIN + 1)[:, LPAD-3:LPAD+4],
                "d_fwr", reads=[f"fwb{tw % 2}"], writes=["FBqp"])
            for j in (0, 8):
                s0_, s1_ = (61, 188) if j == 0 else (68, 195)
                cc0, cc1 = max(c0, s0_), min(c1, s1_)
                E("vector", (lambda tw=tw, j=j, cc0=cc0, cc1=cc1, fv=fv: (lambda v: v.scalar_tensor_tensor(
                    T["ttr_scr"].ap()[:, cc0:cc1], T[fv].ap()[:, cc0:cc1], 1.0,
                    T["coefw"].ap()[:, WWIN*j+cc0:WWIN*j+cc1],
                    ALU.mult, ALU.mult,
                    accum_out=T["FBqp"].ap()[:, NJ*tw+j:NJ*tw+j+1])))(),
                  reads=[fv, "coefw"], writes=["ttr_scr", "FBqp"])

        par = "a" if hr == 0 else "b"
        for t in range(nt):
            stag = f"KQ{t % 2}{par}"
            E("scalar", (lambda t=t, kf=kf, stag=stag, hr=hr: (lambda s: s.activation(
                T[stag].ap()[hr:hr+64, 0:128], kf(128*t, 128*(t+1)), AF.Copy)))(),
              reads=["KF", f"{stag}_z"], writes=[stag])
            if t + 2 < nt:
                build_window(t + 2)
            for st4 in range(nst):
                g0, g1 = SW * st4, SW * (st4 + 1)
                pso = 2048 + 512 * (st4 % 2)
                pname = "psB4" if st4 % 2 == 0 else "psB5"
                lo, hi = max(g0, 128 * (t + 1)), g1
                have_sfx = lo < hi
                have_a = g0 <= 128 * t < g1
                have_b = t >= 1 and g0 <= 128 * (t - 1) < g1
                have_c = t + 1 < nt and g0 <= 128 * (t + 1) < g1
                npieces = int(have_a) + int(have_b) + int(have_c) + int(have_sfx)
                E("tensor", (lambda g0=g0, g1=g1, t=t, pso=pso, hc=hc, stag=stag,
                             stp=(npieces == 0): (lambda te: te.matmul(
                    PS[0:128, pso:pso+SW], T[stag].ap()[:, :],
                    T["QF"].ap()[0:128, S*hc+g0:S*hc+g1],
                    start=True, stop=stp)))(),
                  reads=[stag, "QF"], writes=[pname])
                if have_b:
                    E("tensor", (lambda t=t, pso=pso, g0=g0: (lambda te: te.matmul(
                        PS[0:RPAD, pso+128*(t-1)-g0:pso+128*t-g0],
                        T[f"A2X{(t-1) % 4}"].ap()[:, LPAD+128:WWIN],
                        T["ident"].ap()[:, :], start=False, stop=False)))(),
                      reads=[f"A2X{(t-1) % 4}", "ident"], writes=[pname])
                if have_c:
                    E("tensor", (lambda t=t, pso=pso, g0=g0: (lambda te: te.matmul(
                        PS[64:128, pso+128*(t+1)-g0:pso+128*(t+2)-g0],
                        T[f"A2X{(t+1) % 4}"].ap()[:, 0:LPAD],
                        T["ident"].ap()[:, :], start=False, stop=False)))(),
                      reads=[f"A2X{(t+1) % 4}", "ident"], writes=[pname])
                if have_a:
                    E("tensor", (lambda t=t, pso=pso, g0=g0, stp=not have_sfx: (lambda te: te.matmul(
                        PS[0:128, pso+128*t-g0:pso+128*(t+1)-g0],
                        T[f"A2X{t % 4}"].ap()[:, LPAD:LPAD+128],
                        T["ident"].ap()[:, :], start=False, stop=stp)))(),
                      reads=[f"A2X{t % 4}", "ident"], writes=[pname])
                if have_sfx:
                    E("tensor", (lambda lo=lo, hi=hi, pso=pso, g0=g0: (lambda te: te.matmul(
                        PS[0:128, pso+lo-g0:pso+hi-g0], T["ones_row"].ap()[:, :],
                        T["s08row"].ap()[0:1, lo:hi], start=False, stop=True)))(),
                      reads=["ones_row", "s08row"], writes=[pname])
                elif not have_a and npieces > 0:
                    E("tensor", (lambda pso=pso: (lambda te: te.matmul(
                        PS[0:128, pso:pso+1], T["ones_row"].ap()[:, :],
                        T["ident"].ap()[0:1, 1:2], start=False, stop=True)))(),
                      reads=["ones_row", "ident"], writes=[pname])
                E("scalar", (lambda g0=g0, g1=g1, t=t, pso=pso, h=h: (lambda s: s.activation(
                    T[f"F{t % 3}"].ap()[:, g0:g1], PS[0:128, pso:pso+SW], AF.Exp,
                    bias=T["negM"].ap()[:, h:h+1], scale=0.125)))(),
                  reads=[pname, "negM"], writes=[f"F{t % 3}s{st4}"])

            if t >= 1:
                fw_window(t - 1)
            if t == nt - 1:
                fw_window(t)
                E("vector", (lambda nt=nt: (lambda v: v.tensor_copy(
                    T["sqscr"].ap()[:, 0:NJ*nt], T["FBqp"].ap()[:, 0:NJ*nt])))(),
                  reads=["FBqp"], writes=["sqscr"])
                for s4 in range((nt + 3) // 4):
                    ss = list(range(4 * s4, min(4 * s4 + 4, nt)))
                    for i_, s in enumerate(ss):
                        E("tensor", (lambda s=s, i_=i_: (lambda te: te.matmul(
                            PS[0:NJ, 2048+128*i_:2048+128*(i_+1)],
                            T["sqscr"].ap()[:, NJ*s:NJ*(s+1)], T["ident"].ap()[:, :],
                            start=True, stop=True)))(), reads=["sqscr", "ident"],
                          writes=psnames(2048+128*i_, 2048+128*(i_+1)))
                    E("vector", (lambda s4=s4, n=len(ss): (lambda v: v.tensor_copy(
                        T["FBT"].ap()[0:NJ, 512*s4:512*s4+128*n],
                        PS[0:NJ, 2048:2048+128*n])))(),
                      reads=psnames(2048, 2048+128*len(ss)), writes=["FBT"])
                for s in range(nt):
                    E("tensor", (lambda s=s: (lambda te: te.matmul(
                        PS[0:64, 128*s:128*(s+1)], T["tv_b"].ap()[:, :],
                        T["FBT"].ap()[0:NJ, 128*s:128*(s+1)], start=False, stop=False)))(),
                      reads=["tv_b", "FBT"], writes=["psAV"])
            for st4 in range(nst):
                g0, g1 = SW * st4, SW * (st4 + 1)
                E("tensor", (lambda t=t, g0=g0, g1=g1: (lambda te: te.matmul(
                    PS[0:VW, g0:g1], T["vpm"].ap()[:, VW*t:VW*(t+1)],
                    T[f"F{t % 3}"].ap()[:, g0:g1],
                    start=(t == 0), stop=(t == nt - 1))))(),
                  reads=["vpm", f"F{t % 3}s{st4}"], writes=["psAV"])

        E("vector", lambda v: v.memset(T["zbt"].ap()[0:1, 0:1], 0.0),
          writes=["zbt"] + [f"zbts{i}" for i in range(nst)])
        E("vector", (lambda nt=nt: (lambda v: v.tensor_copy(
            T["Rrows"].ap()[0:nt, 0:S], PS[64:64+nt, 0:S])))(),
          reads=["psAV"], writes=["Rrows"])
        for st4 in range(nst):
            z0, z1 = SW * st4, SW * (st4 + 1)
            E("scalar", (lambda nt=nt, z0=z0, z1=z1: (lambda s: s.activation(
                T["rowf"].ap()[0:1, 0:z1-z0], PS[96:97, z0:z1], AF.Copy)))(),
              reads=["psAV"], writes=["rowf"])
            E("scalar", (lambda z0=z0, z1=z1: (lambda s: s.activation(
                T["rowf"].ap()[0:1, 0:z1-z0], T["rowf"].ap()[0:1, 0:z1-z0],
                AF.Ln)))(), reads=["rowf"], writes=["rowf"])
            E("scalar", (lambda z0=z0, z1=z1: (lambda s: s.activation(
                T["rowf"].ap()[0:1, 0:z1-z0], T["rowf"].ap()[0:1, 0:z1-z0],
                AF.Exp, scale=-1.0)))(), reads=["rowf"], writes=["rowf"])
            dma("sync", ap("row_d")[2:3, z0:z1], ap("rowf")[0:1, 0:z1-z0], "d_z",
                reads=["rowf"], writes=[f"rdlane2_{st4}"])
            dma("gpsimd", ap("zbt")[:, z0:z1],
                ap("row_d")[2:3, z0:z1].partition_broadcast(128), "d_zb",
                reads=[f"rdlane2_{st4}"], writes=[f"zbts{st4}"])

        for st4 in range(nst):
            g0, g1 = SW*st4, SW*(st4+1)
            base45 = 2048 + 512 * (st4 % 2)
            bn = "psB4" if st4 % 2 == 0 else "psB5"
            for si in range((g1 - g0) // 128):
                s = (g0 // 128) + si
                E("tensor", (lambda s=s, si=si, base45=base45: (lambda te: te.matmul(
                    PS[0:64, base45+128*si:base45+128*(si+1)],
                    T["tvsel"].ap()[0:nt, 64*s:64*(s+1)],
                    T["Rrows"].ap()[0:nt, 128*s:128*(s+1)], start=True, stop=True)))(),
                  reads=["tvsel", "Rrows"], writes=[bn])
            E("vector", (lambda g0=g0, g1=g1, base45=base45: (lambda v: v.tensor_tensor(
                T["toktmp"].ap()[0:64, 0:g1-g0], PS[0:64, base45:base45+(g1-g0)],
                T["zbt"].ap()[0:64, g0:g1], ALU.mult)))(),
              reads=[bn, f"zbts{st4}"], writes=["toktmp"])
            E("vector", (lambda g0=g0, g1=g1: (lambda v: v.tensor_tensor(
                T["toktmp"].ap()[0:64, SW:SW+g1-g0], PS[0:64, g0:g1],
                T["zbt"].ap()[0:64, g0:g1], ALU.mult)))(),
              reads=["psAV", f"zbts{st4}"], writes=["toktmp2"])
            E("vector", (lambda g0=g0, g1=g1, hc=hc, hr=hr: (lambda v: v.tensor_add(
                T["XN1"].ap()[hr:hr+64, S*hc+g0:S*hc+g1],
                T["toktmp"].ap()[0:64, 0:g1-g0],
                T["toktmp"].ap()[0:64, SW:SW+g1-g0])))(),
              reads=["toktmp", "toktmp2"], writes=["XN1"])
        if cfg.get("dbg") and h == cfg["NH"] - 1 and b == 0:
            if "dbg_d" not in T:
                k.dram_tmp("dbg_d", [192, S], BF16)
            dma("sync", ap("dbg_d")[0:128, :], T["XN1"].ap()[0:128, 0:S], "d_dbg",
                reads=["XN1"], writes=["dbgd"])
    if cfg.get("dbg") and b == 0:
        if "dbg3_d" not in T:
            k.dram_tmp("dbg3_d", [128, S], BF16)


def build_tail(k, cfg, b):
    S = cfg["S"]; NH = cfg["NH"]; HID = 64 * NH; FF = cfg["FF"]
    nt = S // 128; nhc = HID // 128; nfc = FF // 128
    nst = max(1, S // 512); SW = min(512, S)
    T, plan = k.T, k.plan
    ap = lambda n: T[n].ap()
    PS = T["PS"].ap()
    E = plan.emit
    xoff = b * S
    LN127 = float(np.log(127.0))

    def dma(q, dst, src, sem, reads=(), writes=()):
        E(q, (lambda d=dst, s=src: (lambda eng: eng.dma_start(out=d, in_=s)))(),
          reads=reads, writes=writes, dma=sem)

    _BN = ["psB4", "psB5", "psB6w0", "psB7"]
    wslot = [0]
    for oc in range(nhc):
        for i in range(nhc):
            ws = f"Wc{wslot[0] % 2}"; wslot[0] += 1
            dma("sync", T[ws].ap()[:, 0:128],
                ap("wob_d")[128*i:128*(i+1), 128*oc:128*(oc+1)], "d_w",
                reads=["wob"], writes=[ws])
            for st4 in range(nst):
                pso = 2048 + 512 * st4
                E("tensor", (lambda st4=st4, i=i, pso=pso, ws=ws: (lambda te: te.matmul(
                    PS[0:128, pso:pso+SW], T[ws].ap()[:, 0:128],
                    T["XN1"].ap()[:, S*i+SW*st4:S*i+SW*(st4+1)],
                    start=(i == 0), stop=(i == nhc-1))))(),
                  reads=[ws, "XN1"], writes=[_BN[st4]])
        for st4 in range(nst):
            pso = 2048 + 512 * st4
            E("scalar", (lambda oc=oc, st4=st4, pso=pso: (lambda s: s.activation(
                T["QF"].ap()[:, S*oc+SW*st4:S*oc+SW*(st4+1)], PS[0:128, pso:pso+SW],
                AF.Identity, bias=T["bo_s"].ap()[:, oc:oc+1])))(),
              reads=[_BN[st4], "bo_s"], writes=["QF"])

    if cfg.get("dbg") and b == 0:
        if "dbg3_d" not in T:
            k.dram_tmp("dbg3_d", [128, S], BF16)
        dma("sync", ap("dbg3_d")[0:128, :], T["QF"].ap()[0:128, 0:S], "d_dbg",
            reads=["QF"], writes=["dbg3"])

    def ln_tok_chain(src_name, src_slice, eps, out_name, out_slice, width):
        E("scalar", (lambda src_slice=src_slice, width=width: (lambda s: s.activation(
            T["sqscr"].ap()[:, 0:width], src_slice(), AF.Square,
            accum_out=T["statc"].ap()[:, 1:2])))(),
          reads=[src_name], writes=["sqscr", "st1"])
        E("vector", (lambda src_slice=src_slice: (lambda v: v.reduce_sum(
            T["statc"].ap()[:, 0:1], src_slice(), AX.X)))(),
          reads=[src_name], writes=["st0"])
        E("vector", (lambda width=width: (lambda v: v.tensor_scalar_mul(
            T["statc"].ap()[:, 0:1], T["statc"].ap()[:, 0:1], 1.0 / width)))(),
          reads=["st0"], writes=["st0"])
        E("vector", lambda v: v.tensor_mul(T["statc"].ap()[:, 2:3],
            T["statc"].ap()[:, 0:1], T["statc"].ap()[:, 0:1]), reads=["st0"], writes=["st2"])
        E("vector", (lambda width=width: (lambda v: v.scalar_tensor_tensor(
            T["statc"].ap()[:, 3:4], T["statc"].ap()[:, 1:2], 1.0 / width,
            T["statc"].ap()[:, 2:3], ALU.mult, ALU.subtract)))(),
          reads=["st1", "st2"], writes=["st3"])
        E("vector", (lambda eps=eps: (lambda v: v.tensor_scalar_add(
            T["statc"].ap()[:, 3:4], T["statc"].ap()[:, 3:4],
            (eps - float(np.float32(eps))))))(), reads=["st3"], writes=["st3"])
        E("scalar", (lambda eps=eps: (lambda s: s.activation(T["statc"].ap()[:, 3:4],
            T["statc"].ap()[:, 3:4], AF.Ln, bias=eps)))(), reads=["st3"], writes=["st3"])
        E("scalar", lambda s: s.activation(T["statc"].ap()[:, 3:4],
            T["statc"].ap()[:, 3:4], AF.Exp, scale=-0.5), reads=["st3"], writes=["st3"])
        E("vector", (lambda src_slice=src_slice, out_slice=out_slice: (lambda v: v.tensor_scalar(
            out_slice(), src_slice(), T["statc"].ap()[:, 0:1], T["statc"].ap()[:, 3:4],
            ALU.subtract, ALU.mult)))(),
          reads=[src_name, "st0", "st3"], writes=[out_name])

    for t in range(nt):
        for c in range(nhc):
            E("tensor", (lambda t=t, c=c: (lambda te: te.matmul(
                PS[0:128, 2048+128*c:2048+128*(c+1)],
                T["QF"].ap()[:, S*c+128*t:S*c+128*(t+1)], T["ident"].ap()[:, :],
                start=True, stop=True)))(), reads=["QF", "ident"],
              writes=psnames(2048+128*c, 2048+128*(c+1)))
        E("vector", lambda v: v.tensor_copy(T["fq1"].ap()[:, 0:HID],
            PS[0:128, 2048:2048+HID]), reads=psnames(2048, 2048+HID), writes=["fq1"])
        ln_tok_chain("fq1", (lambda: T["fq1"].ap()[:, 0:HID]), 1e-5,
                     "toktmp", (lambda: T["toktmp"].ap()[:, 0:HID]), HID)
        dma("sync", ap("xtile")[:, :], ap("x")[xoff+128*t:xoff+128*(t+1), :],
            "d_x2", writes=["xtile"])
        E("vector", lambda v: v.tensor_add(T["x2t"].ap()[:, :],
            T["toktmp"].ap()[:, 0:HID], T["xtile"].ap()[:, :]),
          reads=["toktmp", "xtile"], writes=["x2t"])
        dma("sync", ap("x2_d")[xoff+128*t:xoff+128*(t+1), :], ap("x2t")[:, :],
            "d_x2o", reads=["x2t"], writes=[f"x2d_{b}_{t}"])

    if b == 0:
        for lane, col, nm in ((0, 7, "lnw1col"), (1, 15, "lnw2col")):
            dma("sync", ap("statc")[:, 14:15],
                ap("row_d")[lane:lane+1, 0:1].partition_broadcast(128), "d_m5",
                reads=[f"rdlane{lane}"], writes=["st14"])
            E("scalar", lambda s: s.activation(T["statc"].ap()[:, 14:15],
                T["statc"].ap()[:, 14:15], AF.Ln), reads=["st14"], writes=["st14"])
            E("vector", (lambda col=col, nm=nm: (lambda v: v.tensor_copy(
                T["statc"].ap()[:, col:col+1], T["statc"].ap()[:, 14:15])))(),
              reads=["st14"], writes=[nm])
        E("vector", lambda v: v.tensor_scalar(T["statc"].ap()[:, 16:17],
            T["scal"].ap()[:, 21:22], -127.0, 2.0, ALU.mult, ALU.add),
          reads=["scal"], writes=["r127t"])
        E("vector", lambda v: v.tensor_mul(T["statc"].ap()[:, 16:17],
            T["statc"].ap()[:, 16:17], T["scal"].ap()[:, 21:22]),
          reads=["r127t", "scal"], writes=["r127"])

    for t in range(nt):
        dma("sync", ap("x2t")[:, :], ap("x2_d")[xoff+128*t:xoff+128*(t+1), :],
            "d_x2i", reads=[f"x2d_{b}_{t}"], writes=["x2t"])
        E("vector", lambda v: v.tensor_copy(T["fq1"].ap()[:, 0:HID],
            T["x2t"].ap()[:, :]), reads=["x2t"], writes=["fq1"])
        ln_tok_chain("fq1", (lambda: T["fq1"].ap()[:, 0:HID]), 1e-5,
                     "toktmp", (lambda: T["toktmp"].ap()[:, 0:HID]), HID)
        E("scalar", lambda s: s.activation(T["sqscr"].ap()[:, 0:HID],
            T["toktmp"].ap()[:, 0:HID], AF.Square, accum_out=T["statc"].ap()[:, 4:5]),
          reads=["toktmp"], writes=["sqscr", "st4"])
        E("scalar", (lambda H=HID: (lambda s: s.activation(T["statc"].ap()[:, 4:5],
            T["statc"].ap()[:, 4:5], AF.Ln, bias=1e-6, scale=1.0 / H)))(),
          reads=["st4"], writes=["st4"])
        E("scalar", lambda s: s.activation(T["statc"].ap()[:, 4:5],
            T["statc"].ap()[:, 4:5], AF.Exp, scale=-0.5), reads=["st4"], writes=["st4"])
        E("vector", lambda v: v.reduce_max(T["statc"].ap()[:, 5:6],
            T["toktmp"].ap()[:, 0:HID], AX.X, apply_absolute_value=True),
          reads=["toktmp"], writes=["st5"])
        E("vector", lambda v: v.tensor_mul(T["statc"].ap()[:, 5:6],
            T["statc"].ap()[:, 4:5], T["statc"].ap()[:, 5:6]),
          reads=["st4", "st5"], writes=["st5"])
        E("vector", lambda v: v.tensor_scalar_max(T["statc"].ap()[:, 5:6],
            T["statc"].ap()[:, 5:6], 1e-5), reads=["st5"], writes=["st5"])
        E("scalar", lambda s: s.activation(T["statc"].ap()[:, 6:7],
            T["statc"].ap()[:, 5:6], AF.Ln), reads=["st5"], writes=["st6"])
        E("scalar", lambda s: s.activation(T["statc"].ap()[:, 5:6],
            T["statc"].ap()[:, 6:7], AF.Exp, scale=-1.0),
          reads=["st6"], writes=["st5"])
        E("vector", lambda v: v.tensor_mul(T["statc"].ap()[:, 8:9],
            T["statc"].ap()[:, 4:5], T["statc"].ap()[:, 5:6]),
          reads=["st4", "st5"], writes=["st8"])
        E("vector", lambda v: v.tensor_scalar(T["statc"].ap()[:, 8:9],
            T["statc"].ap()[:, 8:9], 127.0, None, ALU.mult),
          reads=["st8"], writes=["st8"])
        E("vector", lambda v: v.tensor_scalar(T["fq1"].ap()[:, 0:HID],
            T["toktmp"].ap()[:, 0:HID], T["statc"].ap()[:, 8:9],
            T["Ccol"].ap()[:, 0:1], ALU.mult, ALU.add),
          reads=["toktmp", "st8", "Ccol"], writes=["fq1"])
        E("vector", lambda v: v.tensor_add(T["statc"].ap()[:, 9:10],
            T["statc"].ap()[:, 7:8], T["statc"].ap()[:, 6:7]),
          reads=["lnw1col", "st6"], writes=["st9"])
        E("scalar", lambda s: s.activation(T["statc"].ap()[:, 9:10],
            T["statc"].ap()[:, 9:10], AF.Exp), reads=["st9"], writes=["st9"])
        E("vector", lambda v: v.tensor_mul(T["statc"].ap()[:, 9:10],
            T["statc"].ap()[:, 9:10], T["statc"].ap()[:, 16:17]),
          reads=["st9", "r127"], writes=["st9"])
        E("vector", lambda v: v.tensor_scalar(T["toktmp"].ap()[:, 0:HID],
            T["fq1"].ap()[:, 0:HID], T["Ccol"].ap()[:, 0:1],
            T["statc"].ap()[:, 9:10], ALU.subtract, ALU.mult),
          reads=["fq1", "Ccol", "st9"], writes=["toktmp"])
        for c in range(nhc):
            E("tensor", (lambda t=t, c=c: (lambda te: te.matmul(
                PS[0:128, 2048+128*c:2048+128*(c+1)],
                T["toktmp"].ap()[:, 128*c:128*(c+1)], T["ident"].ap()[:, :],
                start=True, stop=True)))(), reads=["toktmp", "ident"],
              writes=psnames(2048+128*c, 2048+128*(c+1)))
        E("vector", (lambda t=t: (lambda v: v.tensor_copy(
            T["KF"].ap().rearrange("p (c s) -> p c s", s=S)[:, :, 128*t:128*(t+1)],
            PS[0:128, 2048:2048+128*nhc].rearrange("p (c s) -> p c s", s=128))))(),
          reads=psnames(2048, 2048+128*nhc), writes=["KF"])

    for sp in range(nst):
        tg0 = SW * sp
        for ocg in range((nfc + 3) // 4):
            ocs = list(range(4 * ocg, min(4 * ocg + 4, nfc)))
            for i in range(nhc):
                wsn = "W1s" if i % 2 == 0 else "W2s"
                dma("sync", ap(wsn)[:, 0:128*len(ocs)],
                    ap("w1i_d")[128*i:128*(i+1), 128*ocs[0]:128*ocs[0]+128*len(ocs)],
                    "d_w1", writes=[wsn])
                for j, oc in enumerate(ocs):
                    E("tensor", (lambda i=i, j=j, tg0=tg0, wsn=wsn: (lambda te: te.matmul(
                        PS[0:128, 2048+512*j:2048+512*j+SW],
                        T[wsn].ap()[:, 128*j:128*(j+1)],
                        T["KF"].ap()[:, S*i+tg0:S*i+tg0+SW],
                        start=(i == 0), stop=(i == nhc-1))))(),
                      reads=[wsn, "KF"], writes=[f"psB{4+j}"])
            for j, oc in enumerate(ocs):
                E("scalar", (lambda j=j, oc=oc: (lambda s: s.activation(
                    T["QF"].ap()[:, SW*oc:SW*oc+SW], PS[0:128, 2048+512*j:2048+512*j+SW],
                    AF.Gelu, bias=T["b1_s"].ap()[:, oc:oc+1])))(),
                  reads=[f"psB{4+j}", "b1_s"], writes=[f"QFo{oc}"])
        for oc in range(nfc):
            E("scalar", (lambda oc=oc: (lambda s: s.activation(
                T["sqscr"].ap()[:, 0:SW], T["QF"].ap()[:, SW*oc:SW*oc+SW],
                AF.Square)))(), reads=[f"QFo{oc}"], writes=["sqscr"])
            E("tensor", (lambda oc=oc: (lambda te: te.matmul(
                PS[0:1, 3584:3584+SW], T["ones_colb"].ap()[:, 0:1],
                T["sqscr"].ap()[:, 0:SW], start=(oc == 0), stop=(oc == nfc-1))))(),
              reads=["sqscr", "ones_colb"], writes=["psB7"])
            if oc == 0:
                E("vector", lambda v: v.tensor_copy(
                    T["sqscr"].ap()[:, 512:512+SW], T["sqscr"].ap()[:, 0:SW]),
                  reads=["sqscr"], writes=["sqmax"])
            else:
                E("vector", lambda v: v.tensor_max(
                    T["sqscr"].ap()[:, 512:512+SW], T["sqscr"].ap()[:, 512:512+SW],
                    T["sqscr"].ap()[:, 0:SW]), reads=["sqscr", "sqmax"],
                  writes=["sqmax"])
        nb = SW // 128
        E("scalar", lambda s: s.activation(T["rowf"].ap()[0:1, 0:SW],
            PS[0:1, 3584:3584+SW], AF.Copy), reads=["psB7"], writes=["rowf"])
        for bq in range(nb):
            E("tensor", (lambda bq=bq: (lambda te: te.matmul(
                PS[0:128, 3648+bq:3648+bq+1],
                T["rowf"].ap()[0:1, 128*bq:128*(bq+1)],
                T["scal"].ap()[0:1, 2:3], start=True, stop=True)))(),
              reads=["rowf"], writes=["psB7"])
            E("tensor", (lambda bq=bq: (lambda te: te.matmul(
                PS[0:128, 2048+128*bq:2048+128*(bq+1)],
                T["sqscr"].ap()[:, 512+128*bq:512+128*(bq+1)], T["ident"].ap()[:, :],
                start=True, stop=True)))(), reads=["sqmax", "ident"],
              writes=["psB4"])
            E("vector", (lambda bq=bq: (lambda v: v.tensor_reduce(
                T["hcol"].ap()[:, bq:bq+1], PS[0:128, 2048+128*bq:2048+128*(bq+1)],
                AX.X, ALU.max)))(), reads=["psB4"], writes=["hcol_am"])
        E("vector", (lambda nb=nb: (lambda v: v.tensor_copy(
            T["hcol"].ap()[:, 4:4+nb], PS[0:128, 3648:3648+nb])))(),
          reads=["psB7"], writes=["hcol_sq"])
        E("scalar", (lambda FF=FF, nb=nb: (lambda s: s.activation(T["hcol"].ap()[:, 4:4+nb],
            T["hcol"].ap()[:, 4:4+nb], AF.Ln, bias=1e-6, scale=1.0 / FF)))(),
          reads=["hcol_sq"], writes=["hcol_sq"])
        E("scalar", (lambda nb=nb: (lambda s: s.activation(T["hcol"].ap()[:, 0:nb],
            T["hcol"].ap()[:, 0:nb], AF.Ln)))(), reads=["hcol_am"], writes=["hcol_am"])
        E("vector", (lambda nb=nb: (lambda v: v.tensor_tensor(T["hcol"].ap()[:, 0:nb],
            T["hcol"].ap()[:, 0:nb], T["hcol"].ap()[:, 4:4+nb], ALU.subtract)))(),
          reads=["hcol_am", "hcol_sq"], writes=["hcol_am"])
        E("vector", (lambda nb=nb: (lambda v: v.tensor_scalar_max(
            T["hcol"].ap()[:, 0:nb], T["hcol"].ap()[:, 0:nb], -23.025850929940457)))(),
          reads=["hcol_am"], writes=["hcol_am"])
        E("scalar", (lambda nb=nb: (lambda s: s.activation(T["hcol"].ap()[:, 4:4+nb],
            T["hcol"].ap()[:, 4:4+nb], AF.Exp, scale=-0.5)))(),
          reads=["hcol_sq"], writes=["hcol_sq"])
        E("scalar", (lambda nb=nb: (lambda s: s.activation(
            T["hcol"].ap()[:, 8:8+nb], T["hcol"].ap()[:, 0:nb],
            AF.Exp, scale=0.5, bias=T["statc"].ap()[:, 15:16])))(),
          reads=["hcol_am", "lnw2col"], writes=["hcol_cs"])
        E("vector", (lambda nb=nb: (lambda v: v.tensor_scalar(
            T["hcol"].ap()[:, 8:8+nb], T["hcol"].ap()[:, 8:8+nb],
            T["statc"].ap()[:, 16:17], None, ALU.mult)))(),
          reads=["hcol_cs", "r127"], writes=["hcol_cs"])
        E("scalar", (lambda nb=nb: (lambda s: s.activation(
            T["hcol"].ap()[:, 12:12+nb], T["hcol"].ap()[:, 0:nb],
            AF.Exp, scale=-0.5)))(),
          reads=["hcol_am"], writes=["hcol_hs"])
        E("vector", (lambda nb=nb: (lambda v: v.tensor_mul(
            T["hcol"].ap()[:, 12:12+nb], T["hcol"].ap()[:, 12:12+nb],
            T["hcol"].ap()[:, 4:4+nb])))(),
          reads=["hcol_hs", "hcol_sq"], writes=["hcol_hs"])
        E("vector", (lambda nb=nb: (lambda v: v.tensor_scalar(
            T["hcol"].ap()[:, 12:12+nb], T["hcol"].ap()[:, 12:12+nb],
            127.0, None, ALU.mult)))(), reads=["hcol_hs"], writes=["hcol_hs"])
        for c in range(nb):
            dma("sync", ap("row_d")[2:3, 128*c:128*(c+1)], ap("hcol")[:, 12+c:13+c],
                "d_h1", reads=["hcol_hs"], writes=[f"rdlane2_{c}"])
            dma("sync", ap("row_d")[3:4, 128*c:128*(c+1)], ap("hcol")[:, 8+c:9+c],
                "d_h3", reads=["hcol_cs"], writes=[f"rdlane3_{c}"])
        dma("sync", ap("fq1")[:, 0:SW],
            ap("row_d")[2:3, 0:SW].partition_broadcast(128), "d_h2",
            reads=[f"rdlane2_{c}" for c in range(nb)], writes=["fq1"])
        dma("gpsimd", ap("zbt")[:, 0:SW],
            ap("row_d")[3:4, 0:SW].partition_broadcast(128), "d_h4",
            reads=[f"rdlane3_{c}" for c in range(nb)],
            writes=["zbt"] + [f"zbts{i}" for i in range(nst)])
        for oc in range(nfc):
            E("vector", (lambda oc=oc: (lambda v: v.scalar_tensor_tensor(
                T["fq2"].ap()[:, 0:SW], T["QF"].ap()[:, SW*oc:SW*oc+SW],
                T["Ccol"].ap()[:, 0:1], T["fq1"].ap()[:, 0:SW],
                ALU.bypass, ALU.mult)))(),
              reads=[f"QFo{oc}", "fq1"], writes=["fq2"])
            E("vector", (lambda oc=oc: (lambda v: v.tensor_scalar(
                T["fq2"].ap()[:, 0:SW], T["fq2"].ap()[:, 0:SW],
                T["Ccol"].ap()[:, 0:1], None, ALU.add)))(),
              reads=["fq2", "Ccol"], writes=["fq2"])
            E("vector", (lambda oc=oc: (lambda v: v.tensor_scalar(
                T["VF"].ap()[:, SW*oc:SW*oc+SW], T["fq2"].ap()[:, 0:SW],
                T["Ccol"].ap()[:, 0:1], None, ALU.subtract)))(),
              reads=["fq2", "Ccol"], writes=[f"VFo{oc}"])
        for i2g in range((nhc + 3) // 4):
            i2s = list(range(4 * i2g, min(4 * i2g + 4, nhc)))
            for o in range(nfc):
                wsn = "W2s" if o % 2 == 0 else "W1s"
                dma("sync", ap(wsn)[:, 0:128*len(i2s)],
                    ap("w2i_d")[128*o:128*(o+1), 128*i2s[0]:128*i2s[0]+128*len(i2s)],
                    "d_w2", writes=[wsn])
                for j, i2 in enumerate(i2s):
                    E("tensor", (lambda o=o, j=j, tg0=tg0, wsn=wsn: (lambda te: te.matmul(
                        PS[0:128, 2048+512*j:2048+512*j+SW],
                        T[wsn].ap()[:, 128*j:128*(j+1)],
                        T["VF"].ap()[:, SW*o:SW*o+SW],
                        start=(o == 0), stop=(o == nfc-1))))(),
                      reads=[wsn, f"VFo{o}"], writes=[f"psB{4+j}"])
            for j, i2 in enumerate(i2s):
                E("vector", (lambda j=j: (lambda v: v.tensor_tensor(
                    T["fq2"].ap()[:, 0:SW], PS[0:128, 2048+512*j:2048+512*j+SW],
                    T["zbt"].ap()[0:128, 0:SW], ALU.mult)))(),
                  reads=[f"psB{4+j}", "zbt"], writes=["fq2"])
                E("vector", (lambda i2=i2: (lambda v: v.tensor_scalar_add(
                    T["XN1"].ap()[:, SW*i2:SW*i2+SW], T["fq2"].ap()[:, 0:SW],
                    T["b2_s"].ap()[:, i2:i2+1])))(),
                  reads=["fq2", "b2_s"], writes=["XN1"])
        for tt in range(SW // 128):
            t = (tg0 // 128) + tt
            for c in range(nhc):
                E("tensor", (lambda tt=tt, c=c: (lambda te: te.matmul(
                    PS[0:128, 2048+128*c:2048+128*(c+1)],
                    T["XN1"].ap()[:, SW*c+128*tt:SW*c+128*(tt+1)], T["ident"].ap()[:, :],
                    start=True, stop=True)))(), reads=["XN1", "ident"],
                  writes=psnames(2048+128*c, 2048+128*(c+1)))
            E("vector", lambda v: v.tensor_copy(T["fq1"].ap()[:, 0:HID],
                PS[0:128, 2048:2048+HID]), reads=psnames(2048, 2048+HID), writes=["fq1"])
            ln_tok_chain("fq1", (lambda: T["fq1"].ap()[:, 0:HID]), 1e-5,
                         "toktmp", (lambda: T["toktmp"].ap()[:, 0:HID]), HID)
            dma("sync", ap("x2t")[:, :], ap("x2_d")[xoff+128*t:xoff+128*(t+1), :],
                "d_x3", reads=[f"x2d_{b}_{t}"], writes=["x2t"])
            E("vector", lambda v: v.tensor_add(T["xtile"].ap()[:, :],
                T["toktmp"].ap()[:, 0:HID], T["x2t"].ap()[:, :]),
              reads=["toktmp", "x2t"], writes=["xtile"])
            dma("sync", ap("y")[xoff+128*t:xoff+128*(t+1), :], ap("xtile")[:, :],
                "d_out", reads=["xtile"], writes=["yout"])

    E("vector", lambda v: v.tensor_copy(T["sqscr"].ap()[0:1, 0:1],
                                        T["sqscr"].ap()[0:1, 0:1]),
      writes=["QF", "VF"] + [f"QFo{oc}" for oc in range(nfc)]
             + [f"VFo{o}" for o in range(nfc)])


def emit_block(k):
    import contextlib
    plan = k.plan
    nc = k.nc
    import os
    steps = plan.finalize(keep_all=bool(os.environ.get("KEEP_ALL_SEMS")))
    with contextlib.ExitStack() as st:
        sems = {}
        for name in plan.cnt:
            sems[name] = st.enter_context(nc.semaphore(name))
        with nc.Block() as block:
            def mk(eng_name):
                def body(eng):
                    for s_ in steps[eng_name]:
                        if s_[0] == "wait":
                            eng.wait_ge(sems[s_[1]], s_[2])
                        else:
                            ins = s_[1](eng)
                            if s_[2] is not None:
                                ins.then_inc(sems[s_[2]], 16 if plan.is_dma[s_[2]] else 1)
                return body
            block.sync(mk("sync"))
            block.gpsimd(mk("gpsimd"))
            block.tensor(mk("tensor"))
            block.vector(mk("vector"))
            block.scalar(mk("scalar"))
    return k.nc


def host_inputs(cfg, ins, core):
    S = cfg["S"]; NH = cfg["NH"]; HID = 64 * NH; FF = cfg["FF"]; BPC = cfg["BPC"]
    nt = S // 128; nhc = HID // 128; nfc = FF // 128
    f32 = np.float32
    d = {}
    x = np.asarray(ins["L0_attn_x"], f32)
    d["x"] = np.ascontiguousarray(x[BPC*core:BPC*(core+1)].reshape(BPC*S, HID))
    for n, nm in (("Wq", "wq"), ("Wk", "wk"), ("Wv", "wv"), ("Wo", "wo")):
        d[nm] = np.ascontiguousarray(np.asarray(ins[f"L0_attn_{n}"], f32))
    for n, nm, w in (("bq", "bq2", nhc), ("bk", "bk2", nhc), ("bv", "bv2", nhc),
                     ("bo", "bo2", nhc)):
        d[nm] = np.ascontiguousarray(np.asarray(ins[f"L0_attn_{n}"], f32).reshape(w, 128).T)
    d["b12"] = np.ascontiguousarray(np.asarray(ins["L0_ffn_b1"], f32).reshape(nfc, 128).T)
    d["b22"] = np.ascontiguousarray(np.asarray(ins["L0_ffn_b2"], f32).reshape(nhc, 128).T)
    tk = np.asarray(ins["L0_attn_table_k"], f32)
    tv = np.asarray(ins["L0_attn_table_v"], f32)
    tkp = np.zeros((64, 65), f32)
    tkp[:, 0:NJ] = tk.T
    tkp[:, 32] = tk.T[:, 2 * MAXREL]
    d["tkT2"] = np.ascontiguousarray(np.concatenate([tkp, tkp], 0))
    d["tv"] = np.ascontiguousarray(tv)
    d["tv08"] = np.ascontiguousarray(tv[[0, 2 * MAXREL], :])
    d["w1T"] = np.ascontiguousarray(np.asarray(ins["L0_ffn_w1"], f32).T)
    d["w2T"] = np.ascontiguousarray(np.asarray(ins["L0_ffn_w2"], f32).T)
    d["c_ident"] = np.eye(128, dtype=f32).astype(bfnp)
    d["c_identd"] = np.tile(np.eye(64, dtype=f32), (2, 1)).astype(bfnp)
    coef = make_coefs(a2x_mode=True)
    d["c_coef"] = np.ascontiguousarray(coef.transpose(1, 0, 2).reshape(128, NJ * WWIN)).astype(bfnp)
    coefw = make_coefs(a2x_mode=False)
    d["c_coefw"] = np.ascontiguousarray(coefw.transpose(1, 0, 2).reshape(128, NJ * WWIN)).astype(bfnp)
    d["c_ones_row"] = np.ones((1, 128), f32).astype(bfnp)
    d["c_ones_colb"] = np.ones((128, 1), f32).astype(bfnp)
    d["c_ones_colf"] = np.ones((128, 1), f32)
    sel = np.zeros((2, nt * nt), f32)
    for s in range(nt):
        for c in range(nt):
            if c < s:
                sel[0, s * nt + c] = 1.0
            elif c > s:
                sel[1, s * nt + c] = 1.0
    d["c_sel"] = sel.astype(bfnp)
    vind = np.zeros((128, nt * 33), f32)
    for c in range(nt):
        vind[:, c * 33 + c] = 1.0
        vind[:, c * 33 + 32] = 1.0
    d["c_vind"] = vind.astype(bfnp)
    d["c_scal"] = np.ascontiguousarray(
        np.tile(np.array(SCAL_VALS(HID, FF), f32)[None, :], (128, 1)))
    d["c_p52"] = np.full((128, 1), 2.0 ** 52, f32)
    return d


_CFG_FULL = {"S": 2048, "NH": 16, "FF": 4096, "BPC": 2}


def run_full(ins, ncores=8, cfg=None):
    from concourse.bass_utils import run_bass_kernel_spmd
    cfg = cfg or _CFG_FULL
    k = build_program(cfg)
    nc = emit_block(k)
    in_maps = [host_inputs(cfg, ins, c) for c in range(ncores)]
    res = run_bass_kernel_spmd(nc, in_maps, core_ids=list(range(ncores)))
    BPC, S, HID = cfg["BPC"], cfg["S"], 64 * cfg["NH"]
    out = np.concatenate([r["y"].reshape(BPC, S, HID) for r in res.results], axis=0)
    return out


NCORES = 8


def kernel(**inputs):
    from concourse.bass_utils import run_bass_kernel_spmd
    cfg = _CFG_FULL
    k = build_program(cfg)
    nc = emit_block(k)
    in_maps = [host_inputs(cfg, inputs, c) for c in range(NCORES)]
    res = run_bass_kernel_spmd(nc, in_maps, core_ids=list(range(NCORES)))
    BPC, S, HID = cfg["BPC"], cfg["S"], 64 * cfg["NH"]
    out = np.concatenate([r["y"].reshape(BPC, S, HID) for r in res.results], axis=0)
    return out
